# version 06
# speedup vs baseline: 1.8207x; 1.0094x over previous
import numpy as np
from contextlib import ExitStack

import concourse.bass as bass
import concourse.mybir as mybir
from concourse.bass_utils import run_bass_kernel_spmd

F32 = mybir.dt.float32
BF16 = mybir.dt.bfloat16
A = mybir.AluOpType
AF = mybir.ActivationFunctionType
AX = mybir.AxisListType

T, E, H, D, HID = 2048, 1024, 16, 64, 4096
EC, HC, NIT = 8, 32, 16
TT, NFT = 256, 8
W, WP = 192, 96
C52 = 1.5 * 2.0 ** 52
C23 = 1.5 * 2.0 ** 23
PI = float(np.pi)


class Sem:
    def __init__(self, h):
        self.h = h
        self.n = 0

    def inc(self, k=1):
        self.n += k
        return self.n


def cut512(lo, hi):
    out = []
    while lo < hi:
        nxt = min(hi, (lo // 512 + 1) * 512)
        out.append((lo, nxt))
        lo = nxt
    return out


UPTO = 99
A2_NG = 4
A2_DEPTH = 9
A2_COL0 = True
A2_EXP = True
A2_ROW0 = True
A2_TAIL = True


def build():
    nc = bass.Bass("TRN2", target_bir_lowering=False)
    es = ExitStack()

    x_d = nc.dram_tensor("x", [T, E], F32, kind="ExternalInput").ap()
    wT_d = {w: nc.dram_tensor(w, [E, E], F32, kind="ExternalInput").ap()
            for w in ("wqT", "wkT", "wvT", "woT")}
    bqk_d = {b: nc.dram_tensor(b, [128, EC], F32, kind="ExternalInput").ap()
             for b in ("bqP", "bkP")}
    b_d = {b: nc.dram_tensor(b, [E], F32, kind="ExternalInput").ap()
           for b in ("bv", "bo")}
    w1T_d = nc.dram_tensor("w1T", [E, HID], F32, kind="ExternalInput").ap()
    cw_d = nc.dram_tensor("cw", [128, HC, 3], F32, kind="ExternalInput").ap()
    cb_d = nc.dram_tensor("cb", [128, HC], F32, kind="ExternalInput").ap()
    w2T_d = nc.dram_tensor("w2T", [HID, E], F32, kind="ExternalInput").ap()
    pbB_d = nc.dram_tensor("pbB", [128, W], F32, kind="ExternalInput").ap()
    pbC_d = nc.dram_tensor("pbC", [128, NIT], F32, kind="ExternalInput").ap()
    pbR_d = nc.dram_tensor("pbR", [16, T], F32, kind="ExternalInput").ap()
    y_d = nc.dram_tensor("y", [T, E], F32, kind="ExternalOutput").ap()

    qS = nc.dram_tensor("qS", [EC, 128, T], F32, kind="Internal").ap()
    kS = nc.dram_tensor("kS", [EC, 128, T], F32, kind="Internal").ap()
    vS = nc.dram_tensor("vS", [NIT, 128, E], F32, kind="Internal").ap()
    w1qS = nc.dram_tensor("w1qS", [EC, 128, HID], BF16, kind="Internal").ap()
    w2qS = nc.dram_tensor("w2qS", [HC, 128, E], BF16, kind="Internal").ap()
    x1S = nc.dram_tensor("x1S", [NIT, 128, E], F32, kind="Internal").ap()
    xq2S = nc.dram_tensor("xq2S", [HC, 128, T], BF16, kind="Internal").ap()

    ident = es.enter_context(nc.sbuf_tensor("ident", [128, 128], F32))
    identb = es.enter_context(nc.sbuf_tensor("identb", [128, 128], BF16))
    ones_col = es.enter_context(nc.sbuf_tensor("ones_col", [128, 1], F32))
    ones_row = es.enter_context(nc.sbuf_tensor("ones_row", [1, 128], F32))
    zrow = es.enter_context(nc.sbuf_tensor("zrow", [1, 512], F32))
    magic = es.enter_context(nc.sbuf_tensor("magic", [128, 1], F32))
    negmagic = es.enter_context(nc.sbuf_tensor("negmagic", [128, 1], F32))
    eB = es.enter_context(nc.sbuf_tensor("eB", [128, 3, W], F32))
    efc = es.enter_context(nc.sbuf_tensor("efc", [128, NIT], F32))
    bqs = es.enter_context(nc.sbuf_tensor("bqs", [128, EC], F32))
    bks = es.enter_context(nc.sbuf_tensor("bks", [128, EC], F32))
    bob = es.enter_context(nc.sbuf_tensor("bob", [128, E], F32))
    cws = es.enter_context(nc.sbuf_tensor("cws", [128, HC, 3], F32))
    cbs = es.enter_context(nc.sbuf_tensor("cbs", [128, HC], F32))
    sw1 = es.enter_context(nc.sbuf_tensor("sw1", [128, 1], F32))
    sw2 = es.enter_context(nc.sbuf_tensor("sw2", [128, 1], F32))
    S_all = es.enter_context(nc.sbuf_tensor("S_all", [128, NIT, H], F32))
    pc_st = es.enter_context(nc.sbuf_tensor("pc_st", [128, NIT, H], F32))
    sg1tm = es.enter_context(nc.sbuf_tensor("sg1tm", [128, NIT], F32))
    sg2tm = es.enter_context(nc.sbuf_tensor("sg2tm", [128, NIT], F32))
    p0 = ExitStack()
    brow = p0.enter_context(nc.sbuf_tensor("brow", [1, 2, E], F32))

    sems = {}
    for name in ("dL", "dS", "dG", "pe", "dve", "act", "gp", "pe2", "dve2", "gpq",
                 "fv", "fa", "dP0", "dX", "dW", "dA2", "dCt0", "dCt1", "dWo",
                 "dF", "dFf0", "dFf1"):
        sems[name] = Sem(es.enter_context(nc.semaphore(name)))
    dL, dS, dG = sems["dL"], sems["dS"], sems["dG"]
    dP0, dX, dW, dA2 = sems["dP0"], sems["dX"], sems["dW"], sems["dA2"]
    dCt = [sems["dCt0"], sems["dCt1"]]
    dWo, dF = sems["dWo"], sems["dF"]
    dFf = [sems["dFf0"], sems["dFf1"]]
    pe, dve, act, gp = sems["pe"], sems["dve"], sems["act"], sems["gp"]
    pe2, dve2, gpq = sems["pe2"], sems["dve2"], sems["gpq"]
    fv, fa = sems["fv"], sems["fa"]

    block = es.enter_context(nc.Block())
    V, Sc, Te, G, Sy = block.vector, block.scalar, block.tensor, block.gpsimd, block.sync

    @Sy
    def _(sy):
        sy.dma_start(out=bqs[:], in_=bqk_d["bqP"][:, :]).then_inc(dP0.h, 16)
        sy.dma_start(out=bks[:], in_=bqk_d["bkP"][:, :]).then_inc(dP0.h, 16)
        sy.dma_start(out=brow[0:1, 0, :], in_=b_d["bv"][:]).then_inc(dP0.h, 16)
        sy.dma_start(out=brow[0:1, 1, :], in_=b_d["bo"][:]).then_inc(dP0.h, 16)
        sy.dma_start(out=cws[:], in_=cw_d[:, :, :]).then_inc(dP0.h, 16)
        sy.dma_start(out=cbs[:], in_=cb_d[:, :]).then_inc(dP0.h, 16)
        sy.dma_start(out=eB[:, 0, :], in_=pbB_d[:, :]).then_inc(dP0.h, 16)
        sy.dma_start(out=efc[:], in_=pbC_d[:, :]).then_inc(dP0.h, 16)
        dP0.inc(16 * 8)
    P0_LOADS = dP0.n

    @G
    def _(g):
        g.iota(ident[:], [[-1, 128]], base=0, channel_multiplier=1,
               allow_small_or_imprecise_dtypes=True)
        g.drain().then_inc(gp.h, 1)
        gp.inc()
    IOTA_GP = gp.n

    @V
    def _(v):
        v.memset(magic[:], 0.75)
        v.drain()
        v.tensor_scalar(out=magic[:], in0=magic[:], scalar1=C52, scalar2=C52,
                        op0=A.add, op1=A.subtract)
        v.drain()
        v.tensor_scalar(out=magic[:], in0=magic[:], scalar1=(C52 - C23), scalar2=C23,
                        op0=A.mult, op1=A.add)
        v.drain()
        v.tensor_scalar(out=negmagic[:], in0=magic[:], scalar1=-1.0, scalar2=None,
                        op0=A.mult)
        v.memset(ones_col[:], 1.0)
        v.memset(ones_row[:], 1.0)
        v.memset(zrow[:], 0.0)
        v.drain()
        v.wait_ge(gp.h, IOTA_GP)
        v.tensor_single_scalar(out=ident[:], in_=ident[:], scalar=0.0, op=A.is_equal)
        v.drain()
        v.tensor_copy(out=identb[:], in_=ident[:])
        v.drain()
        v.engine_nop().then_inc(dve.h, 1)
        dve.inc()
    MAGIC_DVE = dve.n

    c_scr = p0.enter_context(nc.sbuf_tensor("c_scr", [128, T], F32))

    @Sc
    def _(sc):
        sc.wait_ge(dP0.h, P0_LOADS)
        sc.activation(out=eB[:, 0, :], in_=eB[:, 0, :], func=AF.Exp)
        sc.drain()
        sc.activation(out=efc[:], in_=efc[:], func=AF.Exp)
        sc.drain()
        sc.drain().then_inc(fa.h, 1)
    fa.inc()
    FPIPE_FA = fa.n

    @G
    def _(g):
        g.wait_ge(fa.h, FPIPE_FA)
        g.iota(c_scr[:, 0:W], [[1, W]], base=-32, channel_multiplier=-1,
               allow_small_or_imprecise_dtypes=True)
        g.drain().then_inc(gp.h, 1)
        gp.inc()
    D2_GP = gp.n

    @V
    def _(v):
        v.wait_ge(gp.h, D2_GP)
        v.tensor_tensor(out=c_scr[:, 0:W], in0=c_scr[:, 0:W], in1=c_scr[:, 0:W], op=A.mult)
        v.drain()
        v.tensor_single_scalar(out=c_scr[:, 0:W], in_=c_scr[:, 0:W], scalar=4.5, op=A.is_le)
        v.drain()
        v.tensor_tensor(out=eB[:, 0, :], in0=eB[:, 0, :], in1=c_scr[:, 0:W], op=A.mult)
        v.drain()
        v.tensor_copy(out=eB[:, 1, :], in_=eB[:, 0, :])
        v.drain()
        v.memset(eB[:, 1, 0:32], 0.0)
        v.drain()
        v.tensor_copy(out=eB[:, 1, 32:33], in_=efc[:, 0:1])
        v.drain()
        v.memset(eB[0:1, 1, :], 0.0)
        v.drain()
        v.tensor_copy(out=eB[:, 2, :], in_=eB[:, 0, :])
        v.drain()
        v.memset(eB[:, 2, 160:192], 0.0)
        v.drain()
        v.memset(efc[:, 0:1], 0.0)
        v.drain()
        v.engine_nop().then_inc(dve.h, 1)
        dve.inc()
    CONSTS_DVE = dve.n


    p0.close()
    if UPTO <= 1:
        es.close()
        return nc

    a1 = ExitStack()
    xhi = a1.enter_context(nc.sbuf_tensor("xhi", [128, EC, T], BF16))
    xlo = a1.enter_context(nc.sbuf_tensor("xlo", [128, EC, T], BF16))
    xtile = [a1.enter_context(nc.sbuf_tensor(f"xtile{i}", [128, E], F32)) for i in range(2)]
    wslF = [a1.enter_context(nc.sbuf_tensor(f"wslF{i}", [128, EC, 128], F32)) for i in range(2)]
    wslH = [a1.enter_context(nc.sbuf_tensor(f"wslH{i}", [128, EC, 128], BF16)) for i in range(2)]
    wslL = [a1.enter_context(nc.sbuf_tensor(f"wslL{i}", [128, EC, 128], BF16)) for i in range(2)]
    wv_h = a1.enter_context(nc.sbuf_tensor("wv_h", [128, EC, E], BF16))
    wv_l = a1.enter_context(nc.sbuf_tensor("wv_l", [128, EC, E], BF16))
    qsl = a1.enter_context(nc.sbuf_tensor("qsl", [128, T], F32))
    vtsl_s = a1.enter_context(nc.sbuf_tensor("vtsl", [128, E], F32))
    vtsl = [vtsl_s, vtsl_s]
    wqbuf = [a1.enter_context(nc.sbuf_tensor(f"wqbuf{i}", [128, 1024], F32)) for i in range(2)]
    wqrnd = [a1.enter_context(nc.sbuf_tensor(f"wqrnd{i}", [128, 1024], F32)) for i in range(2)]
    wqout = [a1.enter_context(nc.sbuf_tensor(f"wqout{i}", [128, 1024], BF16)) for i in range(2)]
    wqpart = a1.enter_context(nc.sbuf_tensor("wqpart", [128, 64], F32))
    wqsum = a1.enter_context(nc.sbuf_tensor("wqsum", [128, 4], F32))
    bvb = a1.enter_context(nc.sbuf_tensor("bvb", [128, E], F32))
    ps_w = a1.enter_context(nc.psum_tensor("ps_w", [128, 64], F32))
    ps_q = a1.enter_context(nc.psum_tensor("ps_q", [128, 2, 512], F32))
    axs = ExitStack()
    ps_x = [axs.enter_context(nc.psum_tensor(f"ps_x{i}", [128, 2, 512], F32)) for i in range(2)]

    with nc.psum_tensor("ps_b", [128, 512], F32) as ps_b:
        for (bi, dst) in ((0, bvb), (1, bob)):
            for half in range(2):
                @Te
                def _(t, bi=bi, half=half):
                    t.wait_ge(dP0.h, P0_LOADS)
                    t.wait_ge(dve.h, MAGIC_DVE)
                    t.wait_ge(act.h, act.n)
                    t.matmul(ps_b[:, 0:512], ones_row[0:1, :],
                             brow[0:1, bi, half * 512:(half + 1) * 512],
                             start=True, stop=True)
                    t.drain().then_inc(pe.h, 1)
                    pe.inc()

                @Sc
                def _(sc, dst=dst, half=half):
                    sc.wait_ge(pe.h, pe.n)
                    sc.copy(out=dst[:, half * 512:(half + 1) * 512], in_=ps_b[:, 0:512])
                    sc.drain().then_inc(act.h, 1)
                    act.inc()
    BCAST_ACT = act.n

    w1re_d = w1T_d.rearrange("(kc p) c -> kc p c", p=128)
    w2re_d = w2T_d.rearrange("(kc p) c -> kc p c", p=128)
    p1f = {"s": 0, "buf": [None, None], "rnd": [None, None], "out": [None, None]}
    p1_jobs = []

    def p1_pass1(w_re, kc, cc, si):
        def job():
            s = p1f["s"]; p1f["s"] += 1
            bi = s % 2
            free = p1f["buf"][bi]

            @G
            def _(g):
                if free is not None:
                    g.wait_ge(*free)
                g.wait_ge(dG.h, dG.n)
                g.dma_start(out=wqbuf[bi][:],
                            in_=w_re[kc, :, cc * 1024:(cc + 1) * 1024]).then_inc(dG.h, 16)
                dG.inc(16)
            LD = dG.n

            @V
            def _(v):
                v.wait_ge(dG.h, LD)
                v.tensor_reduce(out=wqpart[:, si:si + 1], in_=wqbuf[bi][:], axis=AX.X,
                                op=A.add, apply_absolute_value=True).then_inc(dve2.h, 1)
                dve2.inc()
            p1f["buf"][bi] = (dve2.h, dve2.n)
        return job

    def p1_fin(si0, ns, n_elems, mcol, sumpos, invcol, sw_dst, key):
        def job():
            @Te
            def _(t):
                t.wait_ge(dve2.h, dve2.n)
                t.wait_ge(dve.h, MAGIC_DVE)
                t.matmul(ps_w[0:1, si0:si0 + ns], ones_col[:, 0:1], wqpart[:, si0:si0 + ns],
                         start=True, stop=True).then_inc(pe2.h, 1)
                pe2.inc()

            @V
            def _(v):
                v.wait_ge(pe2.h, pe2.n)
                v.tensor_reduce(out=wqsum[0:1, sumpos:sumpos + 1],
                                in_=ps_w[0:1, si0:si0 + ns], axis=AX.X, op=A.add)
                v.drain()
                v.tensor_scalar(out=wqsum[0:1, sumpos:sumpos + 1],
                                in0=wqsum[0:1, sumpos:sumpos + 1],
                                scalar1=1.0 / n_elems, scalar2=1e-5,
                                op0=A.mult, op1=A.max)
                v.drain().then_inc(dve2.h, 1)
                dve2.inc()

            @Te
            def _(t):
                t.wait_ge(dve2.h, dve2.n)
                t.matmul(ps_w[:, mcol:mcol + 1], ones_row[0:1, :],
                         wqsum[0:1, sumpos:sumpos + 1],
                         start=True, stop=True).then_inc(pe2.h, 1)
                pe2.inc()

            @V
            def _(v):
                v.wait_ge(pe2.h, pe2.n)
                v.tensor_copy(out=sw_dst[:], in_=ps_w[:, mcol:mcol + 1])
                v.drain()
                v.reciprocal(out=wqsum[:, invcol:invcol + 1], in_=sw_dst[:])
                v.drain().then_inc(dve2.h, 1)
                dve2.inc()
            p1f[key] = dve2.n
        return job

    def p1_pass2(w_re, scr_dd, kc, cc, invcol, last_key=None):
        def job():
            s = p1f["s"]; p1f["s"] += 1
            bi = s % 2
            bfree, rfree, ofree = p1f["buf"][bi], p1f["rnd"][bi], p1f["out"][bi]

            @G
            def _(g):
                if bfree is not None:
                    g.wait_ge(*bfree)
                g.wait_ge(dG.h, dG.n)
                g.dma_start(out=wqbuf[bi][:],
                            in_=w_re[kc, :, cc * 1024:(cc + 1) * 1024]).then_inc(dG.h, 16)
                dG.inc(16)
            LD = dG.n

            @V
            def _(v):
                v.wait_ge(dG.h, LD)
                if rfree is not None:
                    v.wait_ge(*rfree)
                v.tensor_scalar(out=wqrnd[bi][:], in0=wqbuf[bi][:],
                                scalar1=wqsum[:, invcol:invcol + 1], scalar2=magic[:, 0:1],
                                op0=A.mult, op1=A.add).then_inc(dve2.h, 1)
                dve2.inc()
            T1 = dve2.n

            @Sc
            def _(sc):
                sc.wait_ge(dve2.h, T1)
                sc.add(out=wqbuf[bi][:], in_=wqrnd[bi][:],
                       add=negmagic[:, 0:1]).then_inc(gpq.h, 1)
                gpq.inc()
            RND = gpq.n
            p1f["rnd"][bi] = (gpq.h, RND)

            @V
            def _(v):
                v.wait_ge(gpq.h, RND)
                if ofree is not None:
                    v.wait_ge(*ofree)
                v.tensor_scalar(out=wqout[bi][:], in0=wqbuf[bi][:], scalar1=-1.0,
                                scalar2=1.0, op0=A.max, op1=A.min).then_inc(dve2.h, 1)
                dve2.inc()
            CL = dve2.n
            p1f["buf"][bi] = (dve2.h, CL)

            @G
            def _(g):
                g.wait_ge(dve2.h, CL)
                g.wait_ge(dG.h, dG.n)
                g.dma_start(out=scr_dd[kc, :, cc * 1024:(cc + 1) * 1024],
                            in_=wqout[bi][:]).then_inc(dG.h, 16)
                dG.inc(16)
            p1f["out"][bi] = (dG.h, dG.n)
            if last_key:
                p1f[last_key] = dG.n
        return job

    for kc in range(EC):
        for cc in range(4):
            p1_jobs.append(p1_pass1(w1re_d, kc, cc, kc * 4 + cc))
    p1_jobs.append(p1_fin(0, 32, E * HID, 0, 0, 1, sw1, "w1inv"))
    for kc in range(EC):
        for cc in range(4):
            p1_jobs.append(p1_pass2(w1re_d, w1qS, kc, cc, 1,
                                    "w1q" if (kc == EC - 1 and cc == 3) else None))
    for kc in range(HC):
        p1_jobs.append(p1_pass1(w2re_d, kc, 0, 32 + kc))
    p1_jobs.append(p1_fin(32, 32, HID * E, 1, 2, 3, sw2, "w2inv"))
    for kc in range(HC):
        p1_jobs.append(p1_pass2(w2re_d, w2qS, kc, 0, 3,
                                "w2q" if kc == HC - 1 else None))

    def p1_step(k=1):
        for _ in range(k):
            if p1_jobs:
                p1_jobs.pop(0)()

    tr_done, sp_done = {}, {}
    for it in range(NIT):
        bi = it % 2

        @Sy
        def _(sy, it=it, bi=bi):
            sy.wait_ge(dX.h, dX.n)
            if it >= 2:
                sy.wait_ge(pe.h, tr_done[it - 2])
            sy.dma_start(out=xtile[bi][:],
                         in_=x_d.rearrange("(it p) e -> it p e", p=128)[it]).then_inc(dX.h, 16)
            dX.inc(16)
        XT_L = dX.n

        @Te
        def _(t, it=it, bi=bi, XT_L=XT_L):
            t.wait_ge(dX.h, XT_L)
            if it == 0:
                t.wait_ge(dve.h, CONSTS_DVE)
                t.wait_ge(act.h, BCAST_ACT)
            if it >= 2:
                t.wait_ge(dve.h, sp_done[it - 2])
            mm = None
            for ec in range(EC):
                mm = t.transpose(ps_x[bi][:, ec // 4, (ec % 4) * 128:(ec % 4) * 128 + 128],
                                 xtile[bi][:, ec * 128:(ec + 1) * 128], ident[:])
            mm.then_inc(pe.h, 1)
            pe.inc()
            tr_done[it] = pe.n

        @V
        def _(v, it=it, bi=bi):
            v.wait_ge(pe.h, tr_done[it])
            for h in range(2):
                v.tensor_copy(out=xhi[:, 4 * h:4 * h + 4, it * 128:(it + 1) * 128],
                              in_=ps_x[bi][:, h].rearrange("p (a b) -> p a b", a=4))
            v.drain()
            for h in range(2):
                mm = v.tensor_tensor(out=xlo[:, 4 * h:4 * h + 4, it * 128:(it + 1) * 128],
                                     in0=ps_x[bi][:, h].rearrange("p (a b) -> p a b", a=4),
                                     in1=xhi[:, 4 * h:4 * h + 4, it * 128:(it + 1) * 128],
                                     op=A.subtract)
            mm.then_inc(dve.h, 1)
            dve.inc()
            sp_done[it] = dve.n
        p1_step(4)
    SPLIT_DVE = dve.n

    wslF_free = {0: None, 1: None}
    wslHL_free = {0: None, 1: None}
    psq_evac = {0: None, 1: None}
    qsl_free = [None]

    def proj_qk(wkey, bias_sb, spill):
        w_re = wT_d[wkey].rearrange("(kc p) o -> p kc o", p=128)
        wsplit, mm_done, evac_done, spill_dS = {}, {}, {}, {}
        for oc in range(EC):
            bi = oc % 2
            ffre, hfre = wslF_free[bi], wslHL_free[bi]

            @Sy
            def _(sy, oc=oc, bi=bi, ffre=ffre):
                sy.wait_ge(dW.h, dW.n)
                if ffre is not None:
                    sy.wait_ge(dve.h, ffre)
                sy.dma_start(out=wslF[bi][:],
                             in_=w_re[:, :, oc * 128:(oc + 1) * 128]).then_inc(dW.h, 16)
                dW.inc(16)
            WL = dW.n

            @V
            def _(v, oc=oc, bi=bi, WL=WL, hfre=hfre):
                v.wait_ge(dW.h, WL)
                if hfre is not None:
                    v.wait_ge(pe.h, hfre)
                v.tensor_copy(out=wslH[bi][:], in_=wslF[bi][:])
                v.drain()
                v.tensor_tensor(out=wslL[bi][:], in0=wslF[bi][:], in1=wslH[bi][:],
                                op=A.subtract).then_inc(dve.h, 1)
                dve.inc()
            wsplit[oc] = dve.n
            wslF_free[bi] = dve.n

            act_base = act.n
            for tt in range(4):
                g = 4 * oc + tt
                bfree = psq_evac[tt % 2]

                @Te
                def _(t, oc=oc, tt=tt, g=g, bi=bi, act_base=act_base, bfree=bfree):
                    if tt == 0:
                        t.wait_ge(dve.h, wsplit[oc])
                    if tt >= 2:
                        t.wait_ge(act.h, act_base + tt - 1)
                    elif bfree is not None:
                        t.wait_ge(act.h, bfree)
                    first = True
                    mm = None
                    for (aw, bx) in ((wslH[bi], xhi), (wslH[bi], xlo), (wslL[bi], xhi)):
                        for kc in range(EC):
                            mm = t.matmul(ps_q[:, tt % 2], aw[:, kc, :],
                                          bx[:, kc, tt * 512:(tt + 1) * 512],
                                          start=first,
                                          stop=(aw is wslL[bi] and kc == EC - 1))
                            first = False
                    mm.then_inc(pe.h, 1)
                    pe.inc()
                mm_done[g] = pe.n
            wslHL_free[bi] = mm_done[4 * oc + 3]

            qfree = qsl_free[0]

            @Sc
            def _(sc, oc=oc, qfree=qfree):
                if oc == 0:
                    sc.wait_ge(dP0.h, P0_LOADS)
                if qfree is not None:
                    sc.wait_ge(dS.h, qfree)
                for tt in range(4):
                    g = 4 * oc + tt
                    sc.wait_ge(pe.h, mm_done[g])
                    sc.add(out=qsl[:, tt * 512:(tt + 1) * 512], in_=ps_q[:, tt % 2],
                           add=bias_sb[:, oc:oc + 1]).then_inc(act.h, 1)
                    act.inc()
                    evac_done[g] = act.n
                sc.drain()
                sc.wait_ge(dS.h, dS.n)
                sc.dma_start(out=spill[oc], in_=qsl[:]).then_inc(dS.h, 16)
                dS.inc(16)
            spill_dS[oc] = dS.n
            qsl_free[0] = dS.n
            psq_evac[0] = evac_done[4 * oc + 2]
            psq_evac[1] = evac_done[4 * oc + 3]
            p1_step(4)

    proj_qk("wqT", bqs, [qS[oc] for oc in range(EC)])
    proj_qk("wkT", bks, [kS[oc] for oc in range(EC)])

    wv_re = wT_d["wvT"].rearrange("(kc p) o -> p kc o", p=128)
    wv_done = {}
    for oc in range(EC):
        bi = oc % 2

        ffre = wslF_free[bi]

        @Sy
        def _(sy, oc=oc, bi=bi, ffre=ffre):
            sy.wait_ge(dW.h, dW.n)
            if ffre is not None:
                sy.wait_ge(dve.h, ffre)
            sy.dma_start(out=wslF[bi][:],
                         in_=wv_re[:, :, oc * 128:(oc + 1) * 128]).then_inc(dW.h, 16)
            dW.inc(16)
        WL = dW.n

        @V
        def _(v, oc=oc, bi=bi, WL=WL):
            v.wait_ge(dW.h, WL)
            v.tensor_copy(out=wv_h[:, :, oc * 128:(oc + 1) * 128], in_=wslF[bi][:])
            v.drain()
            v.tensor_tensor(out=wv_l[:, :, oc * 128:(oc + 1) * 128], in0=wslF[bi][:],
                            in1=wv_h[:, :, oc * 128:(oc + 1) * 128],
                            op=A.subtract).then_inc(dve.h, 1)
            dve.inc()
        wv_done[oc] = dve.n
        wslF_free[bi] = dve.n
        p1_step(2)
    WV_DVE = dve.n

    axs.close()
    ps_v = [a1.enter_context(nc.psum_tensor(f"ps_v{i}", [128, 2, 512], F32)) for i in range(2)]

    vt_mm, vt_evac, vspill = {}, {}, {}
    for it in range(NIT):
        bi = it % 2

        @Te
        def _(t, it=it, bi=bi):
            if it == 0:
                t.wait_ge(dve.h, WV_DVE)
            if it >= 2:
                t.wait_ge(dve.h, vt_evac[it - 2])
            mm = None
            for nh in range(2):
                first = True
                for (ax, bw) in ((xhi, wv_h), (xhi, wv_l), (xlo, wv_h)):
                    for kc in range(EC):
                        mm = t.matmul(ps_v[bi][:, nh], ax[:, kc, it * 128:(it + 1) * 128],
                                      bw[:, kc, nh * 512:(nh + 1) * 512],
                                      start=first, stop=(ax is xlo and kc == EC - 1))
                        first = False
            mm.then_inc(pe.h, 1)
            pe.inc()
        vt_mm[it] = pe.n

        @V
        def _(v, it=it, bi=bi):
            v.wait_ge(pe.h, vt_mm[it])
            if it == 0:
                v.wait_ge(act.h, BCAST_ACT)
            if it >= 1:
                v.wait_ge(dS.h, vspill[it - 1])
            v.tensor_add(out=vtsl[bi][:], in0=ps_v[bi][:].rearrange("p a b -> p (a b)"),
                         in1=bvb[:]).then_inc(dve.h, 1)
            dve.inc()
        vt_evac[it] = dve.n

        @Sc
        def _(sc, it=it, bi=bi):
            sc.wait_ge(dve.h, vt_evac[it])
            sc.wait_ge(dS.h, dS.n)
            sc.dma_start(out=vS[it], in_=vtsl[bi][:]).then_inc(dS.h, 16)
            dS.inc(16)
        vspill[it] = dS.n
        p1_step(2)

    p1_step(len(p1_jobs))
    assert not p1_jobs
    P1_W1INV, P1_W2INV = p1f["w1inv"], p1f["w2inv"]
    P1_W1Q, P1_W2Q = p1f["w1q"], p1f["w2q"]

    @Sc
    def _(sc):
        sc.wait_ge(dS.h, dS.n)
        sc.wait_ge(dG.h, P1_W2Q)
        sc.drain().then_inc(act.h, 1)
        act.inc()
    A1_DONE_ACT = act.n
    a1.close()
    fs = ExitStack()
    ctxflat = fs.enter_context(nc.sbuf_tensor("ctxflat", [128, 2 * EC * T], BF16))
    cxAll = ctxflat.rearrange("p (d e t) -> p d e t", d=2, e=EC)
    wbufF = ctxflat.rearrange("p (kc c) -> p kc c", kc=EC)
    s2row = fs.enter_context(nc.sbuf_tensor("s2row", [1, T], F32))
    if UPTO <= 2:
        fs.close()
        es.close()
        return nc

    a2 = ExitStack()
    efrow = a2.enter_context(nc.sbuf_tensor("efrow", [16, T], F32))
    qzA = a2.enter_context(nc.sbuf_tensor("qzA", [128, 2, T], F32))
    qzB = a2.enter_context(nc.sbuf_tensor("qzB", [128, 2, T], F32))
    kg = a2.enter_context(nc.sbuf_tensor("kg", [128, 2, T + 64], F32))
    gzA = a2.enter_context(nc.sbuf_tensor("gzA", [128, NIT, 128], F32))
    gzB = a2.enter_context(nc.sbuf_tensor("gzB", [128, NIT, 128], F32))
    pbuf = a2.enter_context(nc.sbuf_tensor("pbuf", [128, 2, 2, WP + 196], F32))
    pT = a2.enter_context(nc.sbuf_tensor("pT", [128, 2, 3, W], F32))
    Rt = a2.enter_context(nc.sbuf_tensor("Rt", [128, 2], F32))
    e0sb = a2.enter_context(nc.sbuf_tensor("e0sb", [2, 4, 512], F32))
    p0T = a2.enter_context(nc.sbuf_tensor("p0T", [128, NIT, 2], F32))
    S0sb = a2.enter_context(nc.sbuf_tensor("S0sb", [2, 2], F32))
    iszsc = a2.enter_context(nc.sbuf_tensor("iszsc", [128, NIT, 2], F32))
    q0sel = a2.enter_context(nc.sbuf_tensor("q0sel", [128, 2], F32))
    pcI = a2.enter_context(nc.sbuf_tensor("pcI", [1, 2 * T], F32))
    izI = a2.enter_context(nc.sbuf_tensor("izI", [1, 2 * T], F32))
    mvzI = a2.enter_context(nc.sbuf_tensor("mvzI", [1, 2, 128], F32))

    @Sy
    def _(sy):
        sy.wait_ge(act.h, A1_DONE_ACT)
        sy.wait_ge(dA2.h, dA2.n)
        sy.dma_start(out=efrow[:], in_=pbR_d[:, :]).then_inc(dA2.h, 16)
        dA2.inc(16)
    EFROW_L = dA2.n

    @Sc
    def _(sc):
        sc.wait_ge(dA2.h, EFROW_L)
        sc.activation(out=efrow[:], in_=efrow[:], func=AF.Exp).then_inc(act.h, 1)
        act.inc()
    EFROW_ACT = act.n

    @V
    def _(v):
        v.wait_ge(act.h, A1_DONE_ACT)
        v.memset(kg[:, :, 0:32], 0.0)
        v.memset(kg[:, :, T + 32:T + 64], 0.0)
        v.memset(qzA[64:128, :, :], 0.0)
        v.memset(qzB[0:64, :, :], 0.0)
        v.memset(gzA[:], 0.0)
        v.memset(gzB[:], 0.0)
        v.memset(pbuf[:], 0.0)
        v.memset(pT[:], 0.0)
        v.memset(S_all[:], 0.0)
        v.drain().then_inc(dve.h, 1)
        dve.inc()
    A2_INIT_DVE = dve.n

    vre = vS.rearrange("it p d -> p it d")
    pbuf_last = {0: None, 1: None}

    for g_i in range(A2_NG):
        @Sy
        def _(sy, g_i=g_i):
            sy.wait_ge(dA2.h, dA2.n)
            sy.wait_ge(pe.h, pe.n)
            sy.wait_ge(dve.h, dve.n)
            if g_i == 0:
                sy.wait_ge(dS.h, dS.n)
            for cc in range(2):
                sy.dma_start(out=qzA[0:64, cc, :],
                             in_=qS[2 * g_i + cc][0:64, :]).then_inc(dA2.h, 16)
                sy.dma_start(out=qzB[64:128, cc, :],
                             in_=qS[2 * g_i + cc][64:128, :]).then_inc(dA2.h, 16)
                sy.dma_start(out=kg[:, cc, 32:32 + T], in_=kS[2 * g_i + cc]).then_inc(dA2.h, 16)
                dA2.inc(48)
            sy.dma_start(out=gzA[:, :, 0:64],
                         in_=vre[:, :, 256 * g_i:256 * g_i + 64]).then_inc(dA2.h, 16)
            sy.dma_start(out=gzB[:, :, 64:128],
                         in_=vre[:, :, 256 * g_i + 64:256 * g_i + 128]).then_inc(dA2.h, 16)
            dA2.inc(32)
        A2_G_LOADS = dA2.n

        for p_i in range(2):
            gh = 4 * g_i + 2 * p_i
            cc = p_i

            if p_i == 1:
                @Sy
                def _(sy, g_i=g_i):
                    sy.wait_ge(dA2.h, dA2.n)
                    sy.wait_ge(pe.h, pe.n)
                    sy.dma_start(out=gzA[:, :, 0:64],
                                 in_=vre[:, :, 256 * g_i + 128:256 * g_i + 192]).then_inc(dA2.h, 16)
                    sy.dma_start(out=gzB[:, :, 64:128],
                                 in_=vre[:, :, 256 * g_i + 192:256 * g_i + 256]).then_inc(dA2.h, 16)
                    dA2.inc(32)
            PAIR_LOADS = dA2.n

            pair = ExitStack()
            sfx = f"_{g_i}_{p_i}"
            ps_sc = [pair.enter_context(nc.psum_tensor(f"ps_sc{i}" + sfx, [128, 2, 256], F32))
                     for i in range(2)]
            ps_trA = pair.enter_context(nc.psum_tensor("ps_trA" + sfx, [128, 512], F32))
            ps_trB = pair.enter_context(nc.psum_tensor("ps_trB" + sfx, [128, 512], F32))
            ps_ctx = pair.enter_context(nc.psum_tensor("ps_ctx" + sfx, [128, 4, 512], F32))

            def ctx_mm(t, lhsT, rhs, lo, hi, stop_banks=()):
                mm = None
                for (aa, bb) in cut512(lo, hi):
                    bank = aa // 512
                    piece = rhs if (aa == lo and bb == hi) else rhs[:, aa - lo:bb - lo]
                    mm = t.matmul(ps_ctx[:, bank, aa - bank * 512:bb - bank * 512],
                                  lhsT, piece, start=False,
                                  stop=(bank in stop_banks))
                return mm

            @Te
            def _(t):
                t.wait_ge(dve.h, dve.n)
                t.wait_ge(act.h, act.n)
                mm = None
                for bank in range(4):
                    mm = t.matmul(ps_ctx[:, bank, :], ones_row[0:1, :], zrow[0:1, :],
                                  start=True, stop=False)
                mm.then_inc(pe.h, 1)
                pe.inc()

            ACT_BASE = act.n
            sc_pe, tr_pe, epi_dve = {}, {}, {}
            for it in range(NIT):
                i0 = it * 128
                buf = it % 2

                @Te
                def _(t, it=it, i0=i0, cc=cc, buf=buf):
                    t.wait_ge(dA2.h, PAIR_LOADS)
                    if it < 2:
                        t.wait_ge(act.h, ACT_BASE)
                    else:
                        t.wait_ge(act.h, ACT_BASE + it - 1)
                    if it == 0:
                        t.wait_ge(dve.h, max(A2_INIT_DVE, dve.n))
                    mm = None
                    for hh, qz in ((0, qzA), (1, qzB)):
                        q_l = qz[:, cc, i0:i0 + 128]
                        t.matmul(ps_sc[buf][:, hh, 0:192], q_l,
                                 kg[:, cc, i0:i0 + 192], start=True, stop=True)
                        mm = t.matmul(ps_sc[buf][:, hh, 192:193], q_l,
                                      kg[:, cc, 32:33], start=True, stop=True)
                    mm.then_inc(pe.h, 1)
                    pe.inc()
                sc_pe[it] = pe.n

                @V
                def _(v, it=it, gh=gh, buf=buf):
                    v.wait_ge(act.h, ACT_BASE + it + 1)
                    var = 1 if it == 0 else (2 if it == NIT - 1 else 0)
                    v.tensor_tensor(out=pbuf[:, buf, :, WP:WP + 192],
                                    in0=pbuf[:, buf, :, WP:WP + 192],
                                    in1=eB[:, var, :].unsqueeze(1).broadcast_to([128, 2, W]),
                                    op=A.mult)
                    v.tensor_tensor(out=pbuf[:, buf, :, WP + 192:WP + 193],
                                    in0=pbuf[:, buf, :, WP + 192:WP + 193],
                                    in1=efc[:, it:it + 1].unsqueeze(1).broadcast_to([128, 2, 1]),
                                    op=A.mult)
                    v.drain()
                    for hh in range(2):
                        v.scalar_tensor_tensor(out=pbuf[:, buf, hh, WP:WP + 193],
                                               in0=pbuf[:, buf, hh, WP:WP + 193], scalar=1.0,
                                               in1=pbuf[:, buf, hh, WP:WP + 193],
                                               op0=A.is_ge, op1=A.mult,
                                               accum_out=S_all[:, it, gh + hh:gh + hh + 1])
                    v.drain()
                    v.tensor_single_scalar(out=Rt[:], in_=S_all[:, it, gh:gh + 2],
                                           scalar=0.0, op=A.is_equal)
                    v.drain()
                    v.tensor_add(out=Rt[:], in0=Rt[:], in1=S_all[:, it, gh:gh + 2])
                    v.drain()
                    v.reciprocal(out=Rt[:], in_=Rt[:])
                    v.drain()
                    v.tensor_tensor(out=pbuf[:, buf, :, WP:WP + 193],
                                    in0=pbuf[:, buf, :, WP:WP + 193],
                                    in1=Rt[:].unsqueeze(2).broadcast_to([128, 2, 193]),
                                    op=A.mult)
                    v.drain()
                    v.tensor_copy(out=pc_st[:, it, gh:gh + 2],
                                  in_=pbuf[:, buf, :, WP + 192:WP + 193]).then_inc(dve.h, 1)
                    dve.inc()
                epi_dve[it] = dve.n

                @Te
                def _(t, it=it, buf=buf):
                    t.wait_ge(dve.h, epi_dve[it])
                    mm = None
                    for hh, ps_tr in ((0, ps_trA), (1, ps_trB)):
                        t.transpose(ps_tr[:, 0:128], pbuf[:, buf, hh, 0:128], ident[:])
                        t.transpose(ps_tr[:, 128:256], pbuf[:, buf, hh, 128:256], ident[:])
                        mm = t.transpose(ps_tr[0:33, 256:384], pbuf[:, buf, hh, 256:289],
                                         ident[:])
                    mm.then_inc(pe.h, 1)
                    pe.inc()
                tr_pe[it] = pe.n

                @V
                def _(v, it=it):
                    v.wait_ge(pe.h, tr_pe[it])
                    for hh, ps_tr in ((0, ps_trA), (1, ps_trB)):
                        v.tensor_copy(out=pT[:, hh, it % 3, 32:160], in_=ps_tr[:, 128:256])
                        if it > 0:
                            v.tensor_copy(out=pT[96:128, hh, (it - 1) % 3, 160:192],
                                          in_=ps_tr[96:128, 0:32])
                        if it < NIT - 1:
                            v.tensor_copy(out=pT[0:32, hh, (it + 1) % 3, 0:32],
                                          in_=ps_tr[0:32, 256 + 96:256 + 128])
                    v.drain().then_inc(dve.h, 1)
                    dve.inc()
                ASM_DVE = dve.n

                if it >= 1:
                    tt_ = it - 1

                    @Te
                    def _(t, tt_=tt_, ASM_DVE=ASM_DVE):
                        t.wait_ge(dve.h, ASM_DVE)
                        mm = None
                        for hh, gz in ((0, gzA), (1, gzB)):
                            lo = max(0, 128 * tt_ - 32)
                            hi = min(T, 128 * tt_ + 160)
                            rhs = pT[:, hh, tt_ % 3,
                                     (lo - (128 * tt_ - 32)):(hi - (128 * tt_ - 32))]
                            mm = ctx_mm(t, gz[:, tt_, :], rhs, lo, hi)
                        mm.then_inc(pe.h, 1)
                        pe.inc()

            @Sc
            def _(sc, gh=gh, ACT_BASE=ACT_BASE, pb0=pbuf_last[0], pb1=pbuf_last[1]):
                pbl = {0: pb0, 1: pb1}
                for it in range(NIT):
                    buf = it % 2
                    sc.wait_ge(pe.h, sc_pe[it])
                    if it >= 2:
                        sc.wait_ge(pe.h, tr_pe[it - 2])
                    elif pbl[buf] is not None:
                        sc.wait_ge(pe.h, pbl[buf])
                    sc.activation(out=pbuf[:, buf, :, WP:WP + 193],
                                  in_=ps_sc[buf][:, :, 0:193], func=AF.Exp,
                                  scale=0.125).then_inc(act.h, 1)
                    act.inc()
            assert act.n == ACT_BASE + NIT
            pbuf_last[0] = tr_pe[NIT - 2]
            pbuf_last[1] = tr_pe[NIT - 1]

            @Te
            def _(t):
                t.wait_ge(dve.h, dve.n)
                tt_ = NIT - 1
                mm = None
                for hh, gz in ((0, gzA), (1, gzB)):
                    lo, hi = 128 * tt_ - 32, T
                    rhs = pT[:, hh, tt_ % 3, 0:hi - lo]
                    mm = ctx_mm(t, gz[:, tt_, :], rhs, lo, hi)
                mm.then_inc(pe.h, 1)
                pe.inc()
            CTXW_PE = pe.n

            @V
            def _(v, cc=cc):
                v.wait_ge(pe.h, CTXW_PE)
                v.memset(q0sel[:], 0.0)
                v.drain()
                v.tensor_copy(out=q0sel[0:64, 0:1], in_=qzA[0:64, cc, 0:1])
                v.tensor_copy(out=q0sel[64:128, 1:2], in_=qzB[64:128, cc, 0:1])
                v.drain().then_inc(dve.h, 1)
                dve.inc()
            Q0_DVE = dve.n

            e0_pe = {}
            for nt in range(4):
                @Te
                def _(t, nt=nt, cc=cc):
                    t.wait_ge(dve.h, Q0_DVE)
                    if nt > 0:
                        t.wait_ge(act.h, act.n + nt)
                    t.matmul(ps_trA[0:2, 0:512], q0sel[:],
                             kg[:, cc, 32 + nt * 512:32 + (nt + 1) * 512],
                             start=True, stop=True).then_inc(pe.h, 1)
                    pe.inc()
                e0_pe[nt] = pe.n

            @Sc
            def _(sc):
                for nt in range(4):
                    sc.wait_ge(pe.h, e0_pe[nt])
                    sc.activation(out=e0sb[0:2, nt, :], in_=ps_trA[0:2, 0:512],
                                  func=AF.Exp, scale=0.125).then_inc(act.h, 1)
                    act.inc()
            E0_ACT = act.n

            @V
            def _(v):
                v.wait_ge(act.h, E0_ACT)
                v.tensor_tensor(out=e0sb[0:2, :, :], in0=e0sb[0:2, :, :],
                                in1=efrow[0:2, :].rearrange("p (a b) -> p a b", a=4),
                                op=A.mult)
                v.drain()
                v.scalar_tensor_tensor(out=e0sb[0:2, :, :], in0=e0sb[0:2, :, :],
                                       scalar=1.0, in1=e0sb[0:2, :, :],
                                       op0=A.is_ge, op1=A.mult,
                                       accum_out=S0sb[0:2, 0:1])
                v.drain()
                v.tensor_single_scalar(out=S0sb[0:2, 1:2], in_=S0sb[0:2, 0:1],
                                       scalar=0.0, op=A.is_equal)
                v.drain()
                v.tensor_add(out=S0sb[0:2, 1:2], in0=S0sb[0:2, 1:2], in1=S0sb[0:2, 0:1])
                v.drain()
                v.reciprocal(out=S0sb[0:2, 1:2], in_=S0sb[0:2, 1:2])
                v.drain()
                v.tensor_scalar(out=e0sb[0:2, :, :], in0=e0sb[0:2, :, :],
                                scalar1=S0sb[0:2, 1:2], scalar2=None, op0=A.mult)
                v.drain().then_inc(dve.h, 1)
                dve.inc()
            E0_DVE = dve.n

            @Te
            def _(t):
                t.wait_ge(dve.h, E0_DVE)
                mm = None
                for jt in range(NIT):
                    mm = t.transpose(ps_trB[:, 2 * jt:2 * jt + 2],
                                     e0sb[0:2, jt // 4, (jt % 4) * 128:(jt % 4) * 128 + 128],
                                     ident[0:2, 0:2])
                mm.then_inc(pe.h, 1)
                pe.inc()
            P0T_PE = pe.n

            @V
            def _(v):
                v.wait_ge(pe.h, P0T_PE)
                v.tensor_copy(out=p0T[:].rearrange("p a b -> p (a b)"),
                              in_=ps_trB[:, 0:32]).then_inc(dve.h, 1)
                dve.inc()
            P0T_DVE = dve.n

            @Te
            def _(t):
                t.wait_ge(dve.h, P0T_DVE)
                t.transpose(ps_trA[0:1, 0:2], S0sb[0:2, 0:1],
                            ident[0:2, 0:2]).then_inc(pe.h, 1)
                pe.inc()
            S0T_PE = pe.n

            @V
            def _(v, gh=gh):
                v.wait_ge(pe.h, S0T_PE)
                v.tensor_add(out=S_all[0:1, 0, gh:gh + 2], in0=S_all[0:1, 0, gh:gh + 2],
                             in1=ps_trA[0:1, 0:2])
                v.drain()
                v.tensor_single_scalar(out=iszsc[:], in_=S_all[:, :, gh:gh + 2],
                                       scalar=0.0, op=A.is_equal)
                v.drain().then_inc(dve.h, 1)
                dve.inc()
            ISZ_DVE = dve.n

            mv_pe = {}
            for hh, gz in ((0, gzA), (1, gzB)):
                @Te
                def _(t, hh=hh, gz=gz):
                    t.wait_ge(dve.h, ISZ_DVE)
                    if hh == 1:
                        t.wait_ge(dve.h, dve.n)
                    mm = None
                    for jt in range(NIT):
                        mm = t.matmul(ps_trA[0:1, 128:256], ones_col[:, 0:1], gz[:, jt, :],
                                      start=(jt == 0), stop=(jt == NIT - 1))
                    mm.then_inc(pe.h, 1)
                    pe.inc()
                mv_pe[hh] = pe.n

                @V
                def _(v, hh=hh):
                    v.wait_ge(pe.h, mv_pe[hh])
                    v.tensor_scalar(out=mvzI[0:1, hh, :], in0=ps_trA[0:1, 128:256],
                                    scalar1=1.0 / T, scalar2=None,
                                    op0=A.mult).then_inc(dve.h, 1)
                    dve.inc()
            MV_DVE = dve.n

            @G
            def _(g, gh=gh):
                g.wait_ge(dve.h, ISZ_DVE)
                g.wait_ge(dG.h, dG.n)
                g.dma_start(out=pcI[0:1, :], in_=pc_st[:, :, gh:gh + 2]).then_inc(dG.h, 16)
                dG.inc(16)
                g.dma_start(out=izI[0:1, :], in_=iszsc[:]).then_inc(dG.h, 16)
                dG.inc(16)
                g.wait_ge(dG.h, dG.n)
                g.engine_nop().then_inc(gp.h, 1)
                gp.inc()
            ROWS_GP = gp.n

            @Te
            def _(t, gh=gh):
                t.wait_ge(gp.h, ROWS_GP)
                t.wait_ge(dve.h, MV_DVE)
                pcv = pcI[0:1, :].rearrange("o (p it h) -> o it h p", p=128, it=NIT)
                izv = izI[0:1, :].rearrange("o (p it h) -> o it h p", p=128, it=NIT)
                mm = None
                for hh, gz in ((0, gzA), (1, gzB)):
                    last_h = hh == 1
                    for bank in range(4):
                        ctx_mm(t, gz[0:1, 0, :], pcv[:, 4 * bank:4 * bank + 4, hh, :],
                               bank * 512, (bank + 1) * 512)
                    for bank in range(4):
                        ctx_mm(t, mvzI[0:1, hh, :],
                               izv[:, 4 * bank:4 * bank + 4, hh, :],
                               bank * 512, (bank + 1) * 512,
                               stop_banks=(1, 2, 3) if last_h else ())
                    t.drain()
                    for jt in range(NIT):
                        mm = ctx_mm(t, gz[:, jt, :], p0T[:, jt, hh:hh + 1], 0, 1,
                                    stop_banks=(0,) if (last_h and jt == NIT - 1) else ())
                    if not last_h:
                        t.drain()
                mm.then_inc(pe.h, 1)
                pe.inc()
            CTX_PE = pe.n

            @V
            def _(v, g_i=g_i, p_i=p_i):
                v.wait_ge(pe.h, CTX_PE)
                dc = 2 * g_i + p_i
                v.tensor_copy(out=cxAll[:, 0, dc, :],
                              in_=ps_ctx[:].rearrange("p a b -> p (a b)"))
                v.drain()
                v.tensor_tensor(out=cxAll[:, 1, dc, :],
                                in0=ps_ctx[:].rearrange("p a b -> p (a b)"),
                                in1=cxAll[:, 0, dc, :], op=A.subtract)
                v.drain().then_inc(dve.h, 1)
                dve.inc()
            CXL_DVE = dve.n

            pair.close()

    @Sc
    def _(sc):
        sc.wait_ge(dve.h, CXL_DVE)
        sc.drain().then_inc(act.h, 1)
        act.inc()
    A2_DONE_ACT = act.n
    A2_CX_DVE = CXL_DVE
    a2.close()
    if UPTO <= 3:
        fs.close()
        es.close()
        return nc

    wfs = ExitStack()
    xq1FM = wfs.enter_context(nc.sbuf_tensor("xq1FM", [128, EC, T], BF16))
    a3 = ExitStack()
    wo_h = a3.enter_context(nc.sbuf_tensor("wo_h", [128, EC, E], BF16))
    wo_l = a3.enter_context(nc.sbuf_tensor("wo_l", [128, EC, E], BF16))
    xq1TM = a3.enter_context(nc.sbuf_tensor("xq1TM", [128, NIT, E], BF16))
    xt3 = a3.enter_context(nc.sbuf_tensor("xt3", [128, E], F32))
    resid = a3.enter_context(nc.sbuf_tensor("resid", [128, E], F32))
    sqscr = a3.enter_context(nc.sbuf_tensor("sqscr", [128, E], F32))
    lnst = a3.enter_context(nc.sbuf_tensor("lnst", [128, 8], F32))
    wslo = sqscr[:, :].rearrange("p (a b) -> p a b", a=EC)
    ps_o = [a3.enter_context(nc.psum_tensor(f"ps_o{i}", [128, 2, 512], F32))
            for i in range(2)]
    ps_bf = a3.enter_context(nc.psum_tensor("ps_bf", [128, 2, 512], BF16))

    w1re = w1qS.rearrange("kc p c -> p kc c")

    wo_done = {}
    for oc in range(EC):
        @Sy
        def _(sy, oc=oc):
            sy.wait_ge(dWo.h, dWo.n)
            if oc == 0:
                sy.wait_ge(act.h, A2_DONE_ACT)
            else:
                sy.wait_ge(dve.h, wo_done[oc - 1])
            sy.dma_start(out=wslo,
                         in_=wT_d["woT"].rearrange("(kc p) o -> p kc o", p=128)
                         [:, :, oc * 128:(oc + 1) * 128]).then_inc(dWo.h, 16)
            dWo.inc(16)
        WL = dWo.n

        @V
        def _(v, oc=oc, WL=WL):
            v.wait_ge(dWo.h, WL)
            v.tensor_copy(out=wo_h[:, :, oc * 128:(oc + 1) * 128], in_=wslo)
            v.drain()
            v.tensor_tensor(out=wo_l[:, :, oc * 128:(oc + 1) * 128], in0=wslo,
                            in1=wo_h[:, :, oc * 128:(oc + 1) * 128],
                            op=A.subtract).then_inc(dve.h, 1)
            dve.inc()
        wo_done[oc] = dve.n
    WO_DVE = dve.n

    ctmm, res_dve, q1_dve, x1sp = {}, {}, {}, {}
    for it in range(NIT):
        ps = ps_o[it % 2]
        dci = dCt[it % 2]

        @Sy
        def _(sy, it=it, dci=dci):
            sy.wait_ge(dci.h, dci.n)
            if it >= 1:
                sy.wait_ge(dve.h, res_dve[it - 1])
            sy.dma_start(out=xt3[:],
                         in_=x_d.rearrange("(it p) e -> it p e", p=128)[it]).then_inc(dci.h, 16)
            dci.inc(16)
        CT_L = dci.n

        @Te
        def _(t, it=it, ps=ps, dci=dci, CT_L=CT_L):
            t.wait_ge(dci.h, CT_L)
            if it == 0:
                t.wait_ge(dve.h, A2_CX_DVE)
            t.wait_ge(dve.h, max(WO_DVE, res_dve.get(it - 2, 0)))
            i0 = it * 128
            mm = None
            for nh in range(2):
                first = True
                for (hl, bw) in ((0, wo_h), (0, wo_l), (1, wo_h)):
                    for kc in range(EC):
                        mm = t.matmul(ps[:, nh], cxAll[:, hl, kc, i0:i0 + 128],
                                      bw[:, kc, nh * 512:(nh + 1) * 512],
                                      start=first, stop=(hl == 1 and kc == EC - 1))
                        first = False
            mm.then_inc(pe.h, 1)
            pe.inc()
        ctmm[it] = pe.n

        @V
        def _(v, it=it, ps=ps):
            v.wait_ge(pe.h, ctmm[it])
            if it == 0:
                v.wait_ge(act.h, BCAST_ACT)
            v.drain()
            if it >= 1:
                v.wait_ge(dS.h, x1sp[it - 1])
            v.tensor_add(out=sqscr[:], in0=ps[:].rearrange("p a b -> p (a b)"), in1=bob[:])
            v.drain()
            v.scalar_tensor_tensor(out=resid[:], in0=sqscr[:], scalar=0.0, in1=xt3[:],
                                   op0=A.add, op1=A.add,
                                   accum_out=lnst[:, 0:1]).then_inc(dve.h, 1)
            dve.inc()
            res_dve[it] = dve.n
            v.drain()
            v.tensor_tensor(out=sqscr[:], in0=resid[:], in1=resid[:], op=A.mult)
            v.drain()
            v.tensor_reduce(out=lnst[:, 1:2], in_=sqscr[:], axis=AX.X, op=A.add)
            v.drain()
            v.tensor_scalar(out=lnst[:, 0:2], in0=lnst[:, 0:2], scalar1=1.0 / E,
                            scalar2=None, op0=A.mult)
            v.drain()
            v.tensor_tensor(out=lnst[:, 2:3], in0=lnst[:, 0:1], in1=lnst[:, 0:1], op=A.mult)
            v.drain()
            v.tensor_tensor(out=lnst[:, 1:2], in0=lnst[:, 1:2], in1=lnst[:, 2:3],
                            op=A.subtract)
            v.drain()
            v.tensor_scalar(out=lnst[:, 1:2], in0=lnst[:, 1:2], scalar1=1e-5,
                            scalar2=None, op0=A.add).then_inc(dve.h, 1)
            dve.inc()
        ST_DVE = dve.n

        @Sc
        def _(sc, it=it, ST_DVE=ST_DVE):
            sc.wait_ge(dve.h, ST_DVE)
            sc.activation(out=lnst[:, 2:3], in_=lnst[:, 1:2],
                          func=AF.Sqrt).then_inc(act.h, 1)
            act.inc()
        SQT_ACT = act.n

        @V
        def _(v, it=it, SQT_ACT=SQT_ACT):
            v.wait_ge(act.h, SQT_ACT)
            v.reciprocal(out=lnst[:, 3:4], in_=lnst[:, 2:3])
            v.drain()
            v.tensor_scalar(out=resid[:], in0=resid[:], scalar1=lnst[:, 0:1],
                            scalar2=lnst[:, 3:4], op0=A.subtract,
                            op1=A.mult).then_inc(dve.h, 1)
            dve.inc()
            X1_DVE = dve.n
            v.drain()
            if it == 0:
                v.wait_ge(dve2.h, P1_W1INV)
            v.reduce_max(out=lnst[:, 4:5], in_=resid[:], axis=AX.X,
                         apply_absolute_value=True)
            v.drain()
            v.tensor_scalar(out=lnst[:, 4:5], in0=lnst[:, 4:5], scalar1=1e-5,
                            scalar2=None, op0=A.max)
            v.drain()
            v.reciprocal(out=lnst[:, 5:6], in_=lnst[:, 4:5])
            v.drain()
            v.tensor_scalar(out=lnst[:, 5:6], in0=lnst[:, 5:6], scalar1=127.0,
                            scalar2=None, op0=A.mult)
            v.tensor_scalar(out=sg1tm[:, it:it + 1], in0=lnst[:, 4:5],
                            scalar1=sw1[:, 0:1], scalar2=1.0 / 127.0,
                            op0=A.mult, op1=A.mult)
            v.drain()
            v.tensor_scalar(out=sqscr[:], in0=resid[:], scalar1=lnst[:, 5:6],
                            scalar2=magic[:, 0:1], op0=A.mult, op1=A.add)
            v.drain()
            v.tensor_scalar(out=xq1TM[:, it, :], in0=sqscr[:], scalar1=magic[:, 0:1],
                            scalar2=None, op0=A.subtract).then_inc(dve.h, 1)
            dve.inc()
            q1_dve[it] = dve.n

        @Sc
        def _(sc, it=it):
            sc.wait_ge(dve.h, q1_dve[it] - 1)
            sc.wait_ge(dS.h, dS.n)
            sc.dma_start(out=x1S[it], in_=resid[:]).then_inc(dS.h, 16)
            dS.inc(16)
        x1sp[it] = dS.n
    A3_DVE = q1_dve[NIT - 1]

    @Sy
    def _(sy):
        sy.wait_ge(pe.h, ctmm[NIT - 1])
        sy.wait_ge(dG.h, P1_W1Q)
        sy.wait_ge(dF.h, dF.n)
        sy.dma_start(out=wbufF[:, :, :], in_=w1re[:, :, :]).then_inc(dF.h, 16)
        dF.inc(16)
    WBF_L = dF.n

    f0pe, f0ev = {}, {}
    for it in range(NIT):
        for h2 in range(2):
            idx = 2 * it + h2

            @Te
            def _(t, it=it, h2=h2, idx=idx):
                t.wait_ge(dve.h, q1_dve[it])
                if idx >= 2:
                    t.wait_ge(act.h, f0ev[idx - 2])
                mm = None
                for e4 in range(4):
                    ec = 4 * h2 + e4
                    mm = t.transpose(ps_bf[:, idx % 2, e4 * 128:(e4 + 1) * 128],
                                     xq1TM[:, it, ec * 128:(ec + 1) * 128], identb[:])
                mm.then_inc(pe.h, 1)
                pe.inc()
            f0pe[idx] = pe.n

            @Sc
            def _(sc, it=it, h2=h2, idx=idx):
                sc.wait_ge(pe.h, f0pe[idx])
                sc.copy(out=xq1FM[:, 4 * h2:4 * h2 + 4, it * 128:(it + 1) * 128],
                        in_=ps_bf[:, idx % 2].rearrange("p (a b) -> p a b", a=4)
                        ).then_inc(act.h, 1)
                act.inc()
            f0ev[idx] = act.n
    XQ1_ACT = act.n
    a3.close()
    if UPTO <= 4:
        wfs.close()
        fs.close()
        es.close()
        return nc

    ff = ExitStack()
    harena = ff.enter_context(nc.sbuf_tensor("harena", [128, HC, 258], F32))
    tmpa = ff.enter_context(nc.sbuf_tensor("tmpa", [128, 8, TT], F32))
    xq2sb = ff.enter_context(nc.sbuf_tensor("xq2sb", [128, HC, TT], BF16))
    mabs = ff.enter_context(nc.sbuf_tensor("mabs", [128, TT], F32))
    mx4 = ff.enter_context(nc.sbuf_tensor("mx4", [128, 8], F32))
    s1bc = ff.enter_context(nc.sbuf_tensor("s1bc", [128, T], F32))
    s1row = ff.enter_context(nc.sbuf_tensor("s1row", [1, T], F32))
    s2sb = ff.enter_context(nc.sbuf_tensor("s2sb", [1, 256], F32))
    sgb2 = ff.enter_context(nc.sbuf_tensor("sgb2", [128, TT], F32))
    ps_h = ff.enter_context(nc.psum_tensor("ps_h", [128, 2, 512], F32))
    ps_sg = ff.enter_context(nc.psum_tensor("ps_sg", [128, 512], F32))
    ps_mx = ff.enter_context(nc.psum_tensor("ps_mx", [128, 2, 128], F32))
    ps_bc = ff.enter_context(nc.psum_tensor("ps_bc", [128, 256], F32))

    xq2re = xq2S.rearrange("kc p c -> p kc c")

    @G
    def _(g):
        g.wait_ge(dve.h, A3_DVE)
        g.wait_ge(dS.h, dS.n)
        g.wait_ge(dG.h, dG.n)
        g.dma_start(out=s1row[0:1, :].rearrange("o (p it) -> o p it", p=128),
                    in_=sg1tm[:]).then_inc(dG.h, 16)
        dG.inc(16)
        g.wait_ge(dG.h, dG.n)
        g.engine_nop().then_inc(gp.h, 1)
        gp.inc()
    S1ROW_GP = gp.n

    s1v = s1row[0:1, :].rearrange("o (p it) -> o it p", p=128)
    s1mm, s1ev = {}, {}
    for q in range(4):
        @Te
        def _(t, q=q):
            if q == 0:
                t.wait_ge(gp.h, S1ROW_GP)
            else:
                t.wait_ge(dve.h, s1ev[q - 1])
            t.matmul(ps_sg[:, 0:512], ones_row[0:1, :], s1v[:, 4 * q:4 * q + 4, :],
                     start=True, stop=True).then_inc(pe.h, 1)
            pe.inc()
        s1mm[q] = pe.n

        @V
        def _(v, q=q):
            v.wait_ge(pe.h, s1mm[q])
            v.tensor_copy(out=s1bc[:, q * 512:(q + 1) * 512],
                          in_=ps_sg[:, 0:512]).then_inc(dve.h, 1)
            dve.inc()
        s1ev[q] = dve.n
    S1BC_DVE = dve.n

    @V
    def _(v):
        v.memset(harena[:, :, 0:1], 0.0)
        v.drain().then_inc(dve.h, 1)
        dve.inc()
    HZ_DVE = dve.n

    he_dve, w1mm, xq2sp = {}, {}, {}
    FF_SILU = [None]
    for j in range(NFT):
        t0 = j * TT
        lo = t0 - 1 if j else t0
        NN = 258 if 0 < j < NFT - 1 else 257
        hoff = 1 if j == 0 else 0

        for hc in range(HC):
            hg = HC * j + hc
            bank = hc % 2

            @Te
            def _(t, j=j, hc=hc, hg=hg, bank=bank, lo=lo, NN=NN):
                if hg == 0:
                    t.wait_ge(act.h, XQ1_ACT)
                    t.wait_ge(dF.h, WBF_L)
                if hg >= 2:
                    t.wait_ge(dve.h, he_dve[hg - 2])
                mm = None
                for kc in range(EC):
                    mm = t.matmul(ps_h[:, bank, 0:NN],
                                  wbufF[:, kc, hc * 128:(hc + 1) * 128],
                                  xq1FM[:, kc, lo:lo + NN],
                                  start=(kc == 0), stop=(kc == EC - 1))
                mm.then_inc(pe.h, 1)
                pe.inc()
            w1mm[hg] = pe.n

            @V
            def _(v, j=j, hc=hc, hg=hg, bank=bank, lo=lo, NN=NN, hoff=hoff):
                v.wait_ge(pe.h, w1mm[hg])
                if hc == 0:
                    v.drain()
                if hg == 0:
                    v.wait_ge(dve.h, max(S1BC_DVE, HZ_DVE))
                v.tensor_tensor(out=harena[:, hc, hoff:hoff + NN],
                                in0=ps_h[:, bank, 0:NN], in1=s1bc[:, lo:lo + NN],
                                op=A.mult).then_inc(dve.h, 1)
                dve.inc()
            he_dve[hg] = dve.n

        if j == NFT - 1:
            @V
            def _(v):
                v.drain()
                v.memset(harena[:, :, 257:258], 0.0)
                v.drain().then_inc(dve.h, 1)
                dve.inc()

        conv_done, silu_done = {}, {}
        prev_silu = FF_SILU[0]
        for grp in range(4):
            pv = prev_silu if grp == 0 else silu_done[grp - 1]

            @V
            def _(v, grp=grp, pv=pv):
                v.drain()
                if pv is not None:
                    v.wait_ge(act.h, pv)
                for i in range(8):
                    hc = 8 * grp + i
                    v.tensor_scalar(out=tmpa[:, i, :], in0=harena[:, hc, 0:TT],
                                    scalar1=cws[:, hc, 0:1], scalar2=None, op0=A.mult)
                v.drain()
                for i in range(8):
                    hc = 8 * grp + i
                    v.scalar_tensor_tensor(out=tmpa[:, i, :], in0=harena[:, hc, 1:1 + TT],
                                           scalar=cws[:, hc, 1:2], in1=tmpa[:, i, :],
                                           op0=A.mult, op1=A.add)
                v.drain()
                mm = None
                for i in range(8):
                    hc = 8 * grp + i
                    mm = v.scalar_tensor_tensor(out=tmpa[:, i, :],
                                                in0=harena[:, hc, 2:2 + TT],
                                                scalar=cws[:, hc, 2:3], in1=tmpa[:, i, :],
                                                op0=A.mult, op1=A.add)
                mm.then_inc(dve.h, 1)
                dve.inc()
            conv_done[grp] = dve.n

            @Sc
            def _(sc, j=j, grp=grp):
                sc.wait_ge(dve.h, conv_done[grp])
                if j == 0 and grp == 0:
                    sc.wait_ge(dP0.h, P0_LOADS)
                mm = None
                for i in range(8):
                    hc = 8 * grp + i
                    mm = sc.activation(out=harena[:, hc, 0:TT], in_=tmpa[:, i, :],
                                       func=AF.Silu, bias=cbs[:, hc:hc + 1])
                mm.then_inc(act.h, 1)
                act.inc()
            silu_done[grp] = act.n
        FF_SILU[0] = silu_done[3]

        @V
        def _(v, j=j):
            v.wait_ge(act.h, silu_done[3])
            v.tensor_reduce(out=mabs[:], in_=harena[:, :, 0:TT].rearrange("p h t -> p t h"),
                            axis=AX.X, op=A.max,
                            apply_absolute_value=True).then_inc(dve.h, 1)
            dve.inc()
        MABS_DVE = dve.n

        @Te
        def _(t, j=j, MABS_DVE=MABS_DVE):
            t.wait_ge(dve.h, MABS_DVE)
            mm = None
            for b in range(2):
                mm = t.transpose(ps_mx[:, b, :], mabs[:, b * 128:(b + 1) * 128], ident[:])
            mm.then_inc(pe.h, 1)
            pe.inc()
        MXT_PE = pe.n

        @V
        def _(v, j=j, MXT_PE=MXT_PE):
            v.wait_ge(pe.h, MXT_PE)
            if j == 0:
                v.wait_ge(dve2.h, P1_W2INV)
            for b in range(2):
                v.tensor_reduce(out=mx4[:, b:b + 1], in_=ps_mx[:, b, :], axis=AX.X,
                                op=A.max)
            v.drain()
            v.tensor_scalar(out=mx4[:, 0:2], in0=mx4[:, 0:2], scalar1=1e-5,
                            scalar2=None, op0=A.max)
            v.drain()
            v.reciprocal(out=mx4[:, 2:4], in_=mx4[:, 0:2])
            v.drain()
            v.tensor_scalar(out=mx4[:, 4:6], in0=mx4[:, 2:4], scalar1=127.0,
                            scalar2=None, op0=A.mult)
            for b in range(2):
                v.tensor_scalar(out=sg2tm[:, 2 * j + b:2 * j + b + 1],
                                in0=mx4[:, b:b + 1], scalar1=sw2[:, 0:1],
                                scalar2=1.0 / 127.0, op0=A.mult, op1=A.mult)
            v.drain().then_inc(dve.h, 1)
            dve.inc()
        MX_DVE = dve.n

        @Te
        def _(t, j=j, MX_DVE=MX_DVE):
            t.wait_ge(dve.h, MX_DVE)
            mm = None
            for b in range(2):
                mm = t.transpose(ps_bc[0:1, b * 128:(b + 1) * 128],
                                 mx4[:, 4 + b:5 + b], ident[:])
            mm.then_inc(pe.h, 1)
            pe.inc()
        S2T_PE = pe.n

        @V
        def _(v, j=j, S2T_PE=S2T_PE):
            v.wait_ge(pe.h, S2T_PE)
            v.tensor_copy(out=s2sb[0:1, :], in_=ps_bc[0:1, :]).then_inc(dve.h, 1)
            dve.inc()
        S2E_DVE = dve.n

        @Te
        def _(t, j=j, S2E_DVE=S2E_DVE):
            t.wait_ge(dve.h, S2E_DVE)
            t.matmul(ps_mx[:].rearrange("p a b -> p (a b)"), ones_row[0:1, :],
                     s2sb[0:1, :], start=True, stop=True).then_inc(pe.h, 1)
            pe.inc()
        S2B_PE = pe.n

        @V
        def _(v, j=j, t0=t0, S2B_PE=S2B_PE):
            v.wait_ge(pe.h, S2B_PE)
            v.tensor_copy(out=sgb2[:], in_=ps_mx[:].rearrange("p a b -> p (a b)"))
            v.drain()
            v.tensor_tensor(out=harena[:, :, 0:TT], in0=harena[:, :, 0:TT],
                            in1=sgb2[:].unsqueeze(1).broadcast_to([128, HC, TT]),
                            op=A.mult)
            v.drain()
            if j >= 1:
                v.wait_ge(dS.h, xq2sp[j - 1])
            v.tensor_scalar(out=xq2sb[:], in0=harena[:, :, 0:TT],
                            scalar1=magic[:, 0:1], scalar2=magic[:, 0:1],
                            op0=A.add, op1=A.subtract).then_inc(dve.h, 1)
            dve.inc()
        XQ2_DVE = dve.n

        @Sc
        def _(sc, j=j, t0=t0, XQ2_DVE=XQ2_DVE):
            sc.wait_ge(dve.h, XQ2_DVE)
            sc.wait_ge(dS.h, dS.n)
            sc.dma_start(out=xq2re[:, :, t0:t0 + TT], in_=xq2sb[:]).then_inc(dS.h, 16)
            dS.inc(16)
        xq2sp[j] = dS.n

    @Sc
    def _(sc):
        sc.wait_ge(dS.h, dS.n)
        sc.drain().then_inc(act.h, 1)
        act.inc()
    FF_ACT = act.n
    ff.close()
    if UPTO <= 5:
        wfs.close()
        fs.close()
        es.close()
        return nc

    wfs.close()
    fn = ExitStack()
    wbuf2 = fn.enter_context(nc.sbuf_tensor("wbuf2", [128, HC, E], BF16))
    xq2it = fn.enter_context(nc.sbuf_tensor("xq2it", [128, HC, 128], BF16))
    xq2it2 = fn.enter_context(nc.sbuf_tensor("xq2it2", [128, HC, 128], BF16))
    x1t = fn.enter_context(nc.sbuf_tensor("x1t", [128, E], F32))
    x1t2 = fn.enter_context(nc.sbuf_tensor("x1t2", [128, E], F32))
    outsb = fn.enter_context(nc.sbuf_tensor("outsb", [128, E], F32))
    resid2 = fn.enter_context(nc.sbuf_tensor("resid2", [128, E], F32))
    sq2 = fn.enter_context(nc.sbuf_tensor("sq2", [128, E], F32))
    ln2 = fn.enter_context(nc.sbuf_tensor("ln2", [128, 8], F32))
    ps_f = [fn.enter_context(nc.psum_tensor(f"ps_f{i}", [128, 2, 512], F32))
            for i in range(2)]

    w2re = w2qS.rearrange("kc p c -> p kc c")
    loads_for, w2chunk = {}, {}

    @Sy
    def _(sy):
        sy.wait_ge(act.h, FF_ACT)
        sy.wait_ge(dG.h, P1_W2Q)
        sy.wait_ge(dFf[0].h, dFf[0].n)
        for q in range(4):
            sy.dma_start(out=wbuf2[:, 8 * q:8 * q + 8, :],
                         in_=w2re[:, 8 * q:8 * q + 8, :]).then_inc(dFf[0].h, 16)
            dFf[0].inc(16)
            w2chunk[q] = dFf[0].n
        sy.dma_start(out=xq2it[:], in_=xq2re[:, :, 0:128]).then_inc(dFf[0].h, 16)
        sy.dma_start(out=x1t[:], in_=x1S[0]).then_inc(dFf[0].h, 16)
        dFf[0].inc(32)
    loads_for[0] = dFf[0].n

    w2pe, r2_dve, ysp = {}, {}, {}
    for it in range(NIT):
        ps = ps_f[it % 2]
        xq = xq2it if it % 2 == 0 else xq2it2
        x1b = x1t if it % 2 == 0 else x1t2

        if it + 1 < NIT:
            dfn = dFf[(it + 1) % 2]

            @Sy
            def _(sy, it=it, dfn=dfn):
                sy.wait_ge(dfn.h, dfn.n)
                if it >= 1:
                    sy.wait_ge(pe.h, w2pe[it - 1])
                    sy.wait_ge(dve.h, r2_dve[it - 1])
                xqn = xq2it2 if it % 2 == 0 else xq2it
                x1n = x1t2 if it % 2 == 0 else x1t
                sy.dma_start(out=xqn[:],
                             in_=xq2re[:, :, (it + 1) * 128:(it + 2) * 128]).then_inc(dfn.h, 16)
                sy.dma_start(out=x1n[:], in_=x1S[it + 1]).then_inc(dfn.h, 16)
                dfn.inc(32)
            loads_for[it + 1] = dfn.n

        @Te
        def _(t, it=it, ps=ps, xq=xq):
            t.wait_ge(dFf[it % 2].h, loads_for[it])
            if it >= 2:
                t.wait_ge(dve.h, r2_dve[it - 2])
            mm = None
            for kc in range(HC):
                for nh in range(2):
                    mm = t.matmul(ps[:, nh], xq[:, kc, :],
                                  wbuf2[:, kc, nh * 512:(nh + 1) * 512],
                                  start=(kc == 0), stop=(kc == HC - 1))
            mm.then_inc(pe.h, 1)
            pe.inc()
        w2pe[it] = pe.n

        @V
        def _(v, it=it, ps=ps, x1b=x1b):
            v.wait_ge(pe.h, w2pe[it])
            if it >= 1:
                v.wait_ge(dS.h, ysp[it - 1])
            v.scalar_tensor_tensor(out=resid2[:], in0=ps[:].rearrange("p a b -> p (a b)"),
                                   scalar=sg2tm[:, it:it + 1], in1=x1b[:],
                                   op0=A.mult, op1=A.add,
                                   accum_out=ln2[:, 0:1]).then_inc(dve.h, 1)
            dve.inc()
            r2_dve[it] = dve.n
            v.drain()
            v.tensor_tensor(out=sq2[:], in0=resid2[:], in1=resid2[:], op=A.mult)
            v.drain()
            v.tensor_reduce(out=ln2[:, 1:2], in_=sq2[:], axis=AX.X, op=A.add)
            v.drain()
            v.tensor_scalar(out=ln2[:, 0:2], in0=ln2[:, 0:2], scalar1=1.0 / E,
                            scalar2=None, op0=A.mult)
            v.drain()
            v.tensor_tensor(out=ln2[:, 2:3], in0=ln2[:, 0:1], in1=ln2[:, 0:1], op=A.mult)
            v.drain()
            v.tensor_tensor(out=ln2[:, 1:2], in0=ln2[:, 1:2], in1=ln2[:, 2:3], op=A.subtract)
            v.drain()
            v.tensor_scalar(out=ln2[:, 1:2], in0=ln2[:, 1:2], scalar1=1e-5,
                            scalar2=None, op0=A.add).then_inc(dve.h, 1)
            dve.inc()
        ST2_DVE = dve.n

        @Sc
        def _(sc, it=it, ST2_DVE=ST2_DVE):
            sc.wait_ge(dve.h, ST2_DVE)
            sc.activation(out=ln2[:, 2:3], in_=ln2[:, 1:2],
                          func=AF.Sqrt).then_inc(act.h, 1)
            act.inc()
        SQT2_ACT = act.n

        @V
        def _(v, it=it, SQT2_ACT=SQT2_ACT):
            v.wait_ge(act.h, SQT2_ACT)
            v.reciprocal(out=ln2[:, 3:4], in_=ln2[:, 2:3])
            v.drain()
            v.tensor_scalar(out=outsb[:], in0=resid2[:], scalar1=ln2[:, 0:1],
                            scalar2=ln2[:, 3:4], op0=A.subtract,
                            op1=A.mult).then_inc(dve.h, 1)
            dve.inc()
        OUT_DVE = dve.n

        @Sc
        def _(sc, it=it, OUT_DVE=OUT_DVE):
            sc.wait_ge(dve.h, OUT_DVE)
            sc.wait_ge(dS.h, dS.n)
            sc.dma_start(out=y_d.rearrange("(it p) e -> it p e", p=128)[it],
                         in_=outsb[:]).then_inc(dS.h, 16)
            dS.inc(16)
        ysp[it] = dS.n

    @Sc
    def _(sc):
        sc.wait_ge(dS.h, dS.n)
    fn.close()
    fs.close()

    es.close()
    return nc


_NC_CACHE = {}


def kernel(L0_attn_x, L0_attn_wq, L0_attn_bq, L0_attn_wk, L0_attn_bk,
           L0_attn_wv, L0_attn_bv, L0_attn_wo, L0_attn_bo,
           L0_ffn_w1, L0_ffn_conv_w, L0_ffn_conv_b, L0_ffn_w2):
    if "nc" not in _NC_CACHE:
        _NC_CACHE["nc"] = build()
    nc = _NC_CACHE["nc"]

    def c(a):
        return np.ascontiguousarray(np.asarray(a, dtype=np.float32))

    base = {
        "wqT": c(np.asarray(L0_attn_wq).T), "wkT": c(np.asarray(L0_attn_wk).T),
        "wvT": c(np.asarray(L0_attn_wv).T), "woT": c(np.asarray(L0_attn_wo).T),
        "bqP": c(np.asarray(L0_attn_bq).reshape(EC, 128).T),
        "bkP": c(np.asarray(L0_attn_bk).reshape(EC, 128).T),
        "bv": c(L0_attn_bv), "bo": c(L0_attn_bo),
        "w1T": c(np.asarray(L0_ffn_w1).T),
        "cw": c(np.asarray(L0_ffn_conv_w).reshape(HC, 128, 3).transpose(1, 0, 2)),
        "cb": c(np.asarray(L0_ffn_conv_b).reshape(HC, 128).T),
        "w2T": c(np.asarray(L0_ffn_w2).T),
    }
    import jax

    @jax.jit
    def _pbf(dist):
        diff = dist.astype("float32")
        import jax.numpy as jnp
        return jnp.exp(-0.5 * (diff / 2.0) ** 2) + 1.0 * jnp.cos(2.0 * jnp.pi * diff / 401.0)

    def pb32(d_int):
        with jax.default_device(jax.devices("cpu")[0]):
            return np.asarray(jax.device_get(_pbf(np.asarray(d_int, dtype=np.int32))))

    ii = np.arange(128)
    base["pbB"] = c(pb32(np.arange(W)[None, :] - 32 - ii[:, None]))
    base["pbC"] = c(pb32(128 * np.arange(NIT)[None, :] + ii[:, None]))
    base["pbR"] = c(np.tile(pb32(-np.arange(T))[None, :], (16, 1)))

    xs = np.asarray(L0_attn_x, dtype=np.float32)
    in_maps = [dict(base, x=c(xs[b])) for b in range(8)]
    res = run_bass_kernel_spmd(nc, in_maps, core_ids=list(range(8)))
    out = np.stack([res.results[b]["y"] for b in range(8)], axis=0)
    return out.astype(np.float32)


# version 08
# speedup vs baseline: 1.8335x; 1.0070x over previous
import numpy as np
from contextlib import ExitStack

import concourse.bass as bass
import concourse.mybir as mybir
from concourse.bass_utils import run_bass_kernel_spmd

F32 = mybir.dt.float32
BF16 = mybir.dt.bfloat16
A = mybir.AluOpType
AF = mybir.ActivationFunctionType
AX = mybir.AxisListType

T, E, H, D, HID = 2048, 1024, 16, 64, 4096
EC, HC, NIT = 8, 32, 16
TT, NFT = 256, 8
W, WP = 192, 96
C52 = 1.5 * 2.0 ** 52
C23 = 1.5 * 2.0 ** 23
PI = float(np.pi)


class Sem:
    def __init__(self, h):
        self.h = h
        self.n = 0

    def inc(self, k=1):
        self.n += k
        return self.n


def cut512(lo, hi):
    out = []
    while lo < hi:
        nxt = min(hi, (lo // 512 + 1) * 512)
        out.append((lo, nxt))
        lo = nxt
    return out


UPTO = 99
A2_NG = 4
A2_DEPTH = 9
A2_COL0 = True
A2_EXP = True
A2_ROW0 = True
A2_TAIL = True


def build():
    nc = bass.Bass("TRN2", target_bir_lowering=False)
    es = ExitStack()

    x_d = nc.dram_tensor("x", [T, E], F32, kind="ExternalInput").ap()
    wT_d = {w: nc.dram_tensor(w, [E, E], F32, kind="ExternalInput").ap()
            for w in ("wqT", "wkT", "wvT", "woT")}
    bqk_d = {b: nc.dram_tensor(b, [128, EC], F32, kind="ExternalInput").ap()
             for b in ("bqP", "bkP")}
    b_d = {b: nc.dram_tensor(b, [E], F32, kind="ExternalInput").ap()
           for b in ("bv", "bo")}
    w1T_d = nc.dram_tensor("w1T", [E, HID], F32, kind="ExternalInput").ap()
    cw_d = nc.dram_tensor("cw", [128, HC, 3], F32, kind="ExternalInput").ap()
    cb_d = nc.dram_tensor("cb", [128, HC], F32, kind="ExternalInput").ap()
    w2T_d = nc.dram_tensor("w2T", [HID, E], F32, kind="ExternalInput").ap()
    pbB_d = nc.dram_tensor("pbB", [128, W], F32, kind="ExternalInput").ap()
    pbC_d = nc.dram_tensor("pbC", [128, NIT], F32, kind="ExternalInput").ap()
    pbR_d = nc.dram_tensor("pbR", [16, T], F32, kind="ExternalInput").ap()
    y_d = nc.dram_tensor("y", [T, E], F32, kind="ExternalOutput").ap()

    qS = nc.dram_tensor("qS", [EC, 128, T], F32, kind="Internal").ap()
    kS = nc.dram_tensor("kS", [EC, 128, T], F32, kind="Internal").ap()
    vS = nc.dram_tensor("vS", [NIT, 128, E], F32, kind="Internal").ap()
    w1qS = nc.dram_tensor("w1qS", [EC, 128, HID], BF16, kind="Internal").ap()
    w2qS = nc.dram_tensor("w2qS", [HC, 128, E], BF16, kind="Internal").ap()
    x1S = nc.dram_tensor("x1S", [NIT, 128, E], F32, kind="Internal").ap()
    xq2S = nc.dram_tensor("xq2S", [HC, 128, T], BF16, kind="Internal").ap()

    ident = es.enter_context(nc.sbuf_tensor("ident", [128, 128], F32))
    identb = es.enter_context(nc.sbuf_tensor("identb", [128, 128], BF16))
    ones_col = es.enter_context(nc.sbuf_tensor("ones_col", [128, 1], F32))
    ones_row = es.enter_context(nc.sbuf_tensor("ones_row", [1, 128], F32))
    zrow = es.enter_context(nc.sbuf_tensor("zrow", [1, 512], F32))
    magic = es.enter_context(nc.sbuf_tensor("magic", [128, 1], F32))
    negmagic = es.enter_context(nc.sbuf_tensor("negmagic", [128, 1], F32))
    eB = es.enter_context(nc.sbuf_tensor("eB", [128, 3, W], F32))
    efc = es.enter_context(nc.sbuf_tensor("efc", [128, NIT], F32))
    bqs = es.enter_context(nc.sbuf_tensor("bqs", [128, EC], F32))
    bks = es.enter_context(nc.sbuf_tensor("bks", [128, EC], F32))
    bob = es.enter_context(nc.sbuf_tensor("bob", [128, E], F32))
    cws = es.enter_context(nc.sbuf_tensor("cws", [128, HC, 3], F32))
    cbs = es.enter_context(nc.sbuf_tensor("cbs", [128, HC], F32))
    sw1 = es.enter_context(nc.sbuf_tensor("sw1", [128, 1], F32))
    sw2 = es.enter_context(nc.sbuf_tensor("sw2", [128, 1], F32))
    S_all = es.enter_context(nc.sbuf_tensor("S_all", [128, NIT, H], F32))
    pc_st = es.enter_context(nc.sbuf_tensor("pc_st", [128, NIT, H], F32))
    sg1tm = es.enter_context(nc.sbuf_tensor("sg1tm", [128, NIT], F32))
    sg2tm = es.enter_context(nc.sbuf_tensor("sg2tm", [128, NIT], F32))
    p0 = ExitStack()
    brow = p0.enter_context(nc.sbuf_tensor("brow", [1, 2, E], F32))

    sems = {}
    for name in ("dL", "dS", "dG", "pe", "dve", "act", "gp", "pe2", "dve2", "gpq",
                 "fv", "fa", "dP0", "dX", "dW", "dA2", "dCt0", "dCt1", "dWo",
                 "dF", "dFf0", "dFf1"):
        sems[name] = Sem(es.enter_context(nc.semaphore(name)))
    dL, dS, dG = sems["dL"], sems["dS"], sems["dG"]
    dP0, dX, dW, dA2 = sems["dP0"], sems["dX"], sems["dW"], sems["dA2"]
    dCt = [sems["dCt0"], sems["dCt1"]]
    dWo, dF = sems["dWo"], sems["dF"]
    dFf = [sems["dFf0"], sems["dFf1"]]
    pe, dve, act, gp = sems["pe"], sems["dve"], sems["act"], sems["gp"]
    pe2, dve2, gpq = sems["pe2"], sems["dve2"], sems["gpq"]
    fv, fa = sems["fv"], sems["fa"]

    block = es.enter_context(nc.Block())
    V, Sc, Te, G, Sy = block.vector, block.scalar, block.tensor, block.gpsimd, block.sync

    @Sy
    def _(sy):
        sy.dma_start(out=bqs[:], in_=bqk_d["bqP"][:, :]).then_inc(dP0.h, 16)
        sy.dma_start(out=bks[:], in_=bqk_d["bkP"][:, :]).then_inc(dP0.h, 16)
        sy.dma_start(out=brow[0:1, 0, :], in_=b_d["bv"][:]).then_inc(dP0.h, 16)
        sy.dma_start(out=brow[0:1, 1, :], in_=b_d["bo"][:]).then_inc(dP0.h, 16)
        sy.dma_start(out=cws[:], in_=cw_d[:, :, :]).then_inc(dP0.h, 16)
        sy.dma_start(out=cbs[:], in_=cb_d[:, :]).then_inc(dP0.h, 16)
        sy.dma_start(out=eB[:, 0, :], in_=pbB_d[:, :]).then_inc(dP0.h, 16)
        sy.dma_start(out=efc[:], in_=pbC_d[:, :]).then_inc(dP0.h, 16)
        dP0.inc(16 * 8)
    P0_LOADS = dP0.n

    @G
    def _(g):
        g.iota(ident[:], [[-1, 128]], base=0, channel_multiplier=1,
               allow_small_or_imprecise_dtypes=True)
        g.drain().then_inc(gp.h, 1)
        gp.inc()
    IOTA_GP = gp.n

    @V
    def _(v):
        v.memset(magic[:], 0.75)
        v.drain()
        v.tensor_scalar(out=magic[:], in0=magic[:], scalar1=C52, scalar2=C52,
                        op0=A.add, op1=A.subtract)
        v.drain()
        v.tensor_scalar(out=magic[:], in0=magic[:], scalar1=(C52 - C23), scalar2=C23,
                        op0=A.mult, op1=A.add)
        v.drain()
        v.tensor_scalar(out=negmagic[:], in0=magic[:], scalar1=-1.0, scalar2=None,
                        op0=A.mult)
        v.memset(ones_col[:], 1.0)
        v.memset(ones_row[:], 1.0)
        v.memset(zrow[:], 0.0)
        v.drain()
        v.wait_ge(gp.h, IOTA_GP)
        v.tensor_single_scalar(out=ident[:], in_=ident[:], scalar=0.0, op=A.is_equal)
        v.drain()
        v.tensor_copy(out=identb[:], in_=ident[:])
        v.drain()
        v.engine_nop().then_inc(dve.h, 1)
        dve.inc()
    MAGIC_DVE = dve.n

    c_scr = p0.enter_context(nc.sbuf_tensor("c_scr", [128, T], F32))

    @Sc
    def _(sc):
        sc.wait_ge(dP0.h, P0_LOADS)
        sc.activation(out=eB[:, 0, :], in_=eB[:, 0, :], func=AF.Exp)
        sc.drain()
        sc.activation(out=efc[:], in_=efc[:], func=AF.Exp)
        sc.drain()
        sc.drain().then_inc(fa.h, 1)
    fa.inc()
    FPIPE_FA = fa.n

    @G
    def _(g):
        g.wait_ge(fa.h, FPIPE_FA)
        g.iota(c_scr[:, 0:W], [[1, W]], base=-32, channel_multiplier=-1,
               allow_small_or_imprecise_dtypes=True)
        g.drain().then_inc(gp.h, 1)
        gp.inc()
    D2_GP = gp.n

    @V
    def _(v):
        v.wait_ge(gp.h, D2_GP)
        v.tensor_tensor(out=c_scr[:, 0:W], in0=c_scr[:, 0:W], in1=c_scr[:, 0:W], op=A.mult)
        v.drain()
        v.tensor_single_scalar(out=c_scr[:, 0:W], in_=c_scr[:, 0:W], scalar=4.5, op=A.is_le)
        v.drain()
        v.tensor_tensor(out=eB[:, 0, :], in0=eB[:, 0, :], in1=c_scr[:, 0:W], op=A.mult)
        v.drain()
        v.tensor_copy(out=eB[:, 1, :], in_=eB[:, 0, :])
        v.drain()
        v.memset(eB[:, 1, 0:32], 0.0)
        v.drain()
        v.tensor_copy(out=eB[:, 1, 32:33], in_=efc[:, 0:1])
        v.drain()
        v.memset(eB[0:1, 1, :], 0.0)
        v.drain()
        v.tensor_copy(out=eB[:, 2, :], in_=eB[:, 0, :])
        v.drain()
        v.memset(eB[:, 2, 160:192], 0.0)
        v.drain()
        v.memset(efc[:, 0:1], 0.0)
        v.drain()
        v.engine_nop().then_inc(dve.h, 1)
        dve.inc()
    CONSTS_DVE = dve.n


    p0.close()
    if UPTO <= 1:
        es.close()
        return nc

    a1 = ExitStack()
    xhi = a1.enter_context(nc.sbuf_tensor("xhi", [128, EC, T], BF16))
    xlo = a1.enter_context(nc.sbuf_tensor("xlo", [128, EC, T], BF16))
    xtile = [a1.enter_context(nc.sbuf_tensor(f"xtile{i}", [128, E], F32)) for i in range(2)]
    wslF = [a1.enter_context(nc.sbuf_tensor(f"wslF{i}", [128, EC, 128], F32)) for i in range(2)]
    wslH = [a1.enter_context(nc.sbuf_tensor(f"wslH{i}", [128, EC, 128], BF16)) for i in range(2)]
    wslL = [a1.enter_context(nc.sbuf_tensor(f"wslL{i}", [128, EC, 128], BF16)) for i in range(2)]
    wv_h = a1.enter_context(nc.sbuf_tensor("wv_h", [128, EC, E], BF16))
    wv_l = a1.enter_context(nc.sbuf_tensor("wv_l", [128, EC, E], BF16))
    qsl = a1.enter_context(nc.sbuf_tensor("qsl", [128, T], F32))
    vtsl_s = a1.enter_context(nc.sbuf_tensor("vtsl", [128, E], F32))
    vtsl = [vtsl_s, vtsl_s]
    wqbuf = [a1.enter_context(nc.sbuf_tensor(f"wqbuf{i}", [128, 1024], F32)) for i in range(2)]
    wqrnd = [a1.enter_context(nc.sbuf_tensor(f"wqrnd{i}", [128, 1024], F32)) for i in range(2)]
    wqout = [a1.enter_context(nc.sbuf_tensor(f"wqout{i}", [128, 1024], BF16)) for i in range(2)]
    wqpart = a1.enter_context(nc.sbuf_tensor("wqpart", [128, 64], F32))
    wqsum = a1.enter_context(nc.sbuf_tensor("wqsum", [128, 4], F32))
    bvb = a1.enter_context(nc.sbuf_tensor("bvb", [128, E], F32))
    ps_w = a1.enter_context(nc.psum_tensor("ps_w", [128, 64], F32))
    ps_q = a1.enter_context(nc.psum_tensor("ps_q", [128, 2, 512], F32))
    axs = ExitStack()
    ps_x = [axs.enter_context(nc.psum_tensor(f"ps_x{i}", [128, 2, 512], F32)) for i in range(2)]

    with nc.psum_tensor("ps_b", [128, 512], F32) as ps_b:
        for (bi, dst) in ((0, bvb), (1, bob)):
            for half in range(2):
                @Te
                def _(t, bi=bi, half=half):
                    t.wait_ge(dP0.h, P0_LOADS)
                    t.wait_ge(dve.h, MAGIC_DVE)
                    t.wait_ge(act.h, act.n)
                    t.matmul(ps_b[:, 0:512], ones_row[0:1, :],
                             brow[0:1, bi, half * 512:(half + 1) * 512],
                             start=True, stop=True)
                    t.drain().then_inc(pe.h, 1)
                    pe.inc()

                @Sc
                def _(sc, dst=dst, half=half):
                    sc.wait_ge(pe.h, pe.n)
                    sc.copy(out=dst[:, half * 512:(half + 1) * 512], in_=ps_b[:, 0:512])
                    sc.drain().then_inc(act.h, 1)
                    act.inc()
    BCAST_ACT = act.n

    w1re_d = w1T_d.rearrange("(kc p) c -> kc p c", p=128)
    w2re_d = w2T_d.rearrange("(kc p) c -> kc p c", p=128)
    p1f = {"s": 0, "buf": [None, None], "rnd": [None, None], "out": [None, None]}
    p1_jobs = []

    def p1_pass1(w_re, kc, cc, si):
        def job():
            s = p1f["s"]; p1f["s"] += 1
            bi = s % 2
            free = p1f["buf"][bi]

            @G
            def _(g):
                if free is not None:
                    g.wait_ge(*free)
                g.wait_ge(dG.h, dG.n)
                g.dma_start(out=wqbuf[bi][:],
                            in_=w_re[kc, :, cc * 1024:(cc + 1) * 1024]).then_inc(dG.h, 16)
                dG.inc(16)
            LD = dG.n

            @V
            def _(v):
                v.wait_ge(dG.h, LD)
                v.tensor_reduce(out=wqpart[:, si:si + 1], in_=wqbuf[bi][:], axis=AX.X,
                                op=A.add, apply_absolute_value=True).then_inc(dve2.h, 1)
                dve2.inc()
            p1f["buf"][bi] = (dve2.h, dve2.n)
        return job

    def p1_fin(si0, ns, n_elems, mcol, sumpos, invcol, sw_dst, key):
        def job():
            @Te
            def _(t):
                t.wait_ge(dve2.h, dve2.n)
                t.wait_ge(dve.h, MAGIC_DVE)
                t.matmul(ps_w[0:1, si0:si0 + ns], ones_col[:, 0:1], wqpart[:, si0:si0 + ns],
                         start=True, stop=True).then_inc(pe2.h, 1)
                pe2.inc()

            @V
            def _(v):
                v.wait_ge(pe2.h, pe2.n)
                v.tensor_reduce(out=wqsum[0:1, sumpos:sumpos + 1],
                                in_=ps_w[0:1, si0:si0 + ns], axis=AX.X, op=A.add)
                v.drain()
                v.tensor_scalar(out=wqsum[0:1, sumpos:sumpos + 1],
                                in0=wqsum[0:1, sumpos:sumpos + 1],
                                scalar1=1.0 / n_elems, scalar2=1e-5,
                                op0=A.mult, op1=A.max)
                v.drain().then_inc(dve2.h, 1)
                dve2.inc()

            @Te
            def _(t):
                t.wait_ge(dve2.h, dve2.n)
                t.matmul(ps_w[:, mcol:mcol + 1], ones_row[0:1, :],
                         wqsum[0:1, sumpos:sumpos + 1],
                         start=True, stop=True).then_inc(pe2.h, 1)
                pe2.inc()

            @V
            def _(v):
                v.wait_ge(pe2.h, pe2.n)
                v.tensor_copy(out=sw_dst[:], in_=ps_w[:, mcol:mcol + 1])
                v.drain()
                v.reciprocal(out=wqsum[:, invcol:invcol + 1], in_=sw_dst[:])
                v.drain().then_inc(dve2.h, 1)
                dve2.inc()
            p1f[key] = dve2.n
        return job

    def p1_pass2(w_re, scr_dd, kc, cc, invcol, last_key=None):
        def job():
            s = p1f["s"]; p1f["s"] += 1
            bi = s % 2
            bfree, rfree, ofree = p1f["buf"][bi], p1f["rnd"][bi], p1f["out"][bi]

            @G
            def _(g):
                if bfree is not None:
                    g.wait_ge(*bfree)
                g.wait_ge(dG.h, dG.n)
                g.dma_start(out=wqbuf[bi][:],
                            in_=w_re[kc, :, cc * 1024:(cc + 1) * 1024]).then_inc(dG.h, 16)
                dG.inc(16)
            LD = dG.n

            @V
            def _(v):
                v.wait_ge(dG.h, LD)
                if rfree is not None:
                    v.wait_ge(*rfree)
                v.tensor_scalar(out=wqrnd[bi][:], in0=wqbuf[bi][:],
                                scalar1=wqsum[:, invcol:invcol + 1], scalar2=magic[:, 0:1],
                                op0=A.mult, op1=A.add).then_inc(dve2.h, 1)
                dve2.inc()
            T1 = dve2.n

            @Sc
            def _(sc):
                sc.wait_ge(dve2.h, T1)
                sc.add(out=wqbuf[bi][:], in_=wqrnd[bi][:],
                       add=negmagic[:, 0:1]).then_inc(gpq.h, 1)
                gpq.inc()
            RND = gpq.n
            p1f["rnd"][bi] = (gpq.h, RND)

            @V
            def _(v):
                v.wait_ge(gpq.h, RND)
                if ofree is not None:
                    v.wait_ge(*ofree)
                v.tensor_scalar(out=wqout[bi][:], in0=wqbuf[bi][:], scalar1=-1.0,
                                scalar2=1.0, op0=A.max, op1=A.min).then_inc(dve2.h, 1)
                dve2.inc()
            CL = dve2.n
            p1f["buf"][bi] = (dve2.h, CL)

            @G
            def _(g):
                g.wait_ge(dve2.h, CL)
                g.wait_ge(dG.h, dG.n)
                g.dma_start(out=scr_dd[kc, :, cc * 1024:(cc + 1) * 1024],
                            in_=wqout[bi][:]).then_inc(dG.h, 16)
                dG.inc(16)
            p1f["out"][bi] = (dG.h, dG.n)
            if last_key:
                p1f[last_key] = dG.n
        return job

    for kc in range(EC):
        for cc in range(4):
            p1_jobs.append(p1_pass1(w1re_d, kc, cc, kc * 4 + cc))
    p1_jobs.append(p1_fin(0, 32, E * HID, 0, 0, 1, sw1, "w1inv"))
    for kc in range(EC):
        for cc in range(4):
            p1_jobs.append(p1_pass2(w1re_d, w1qS, kc, cc, 1,
                                    "w1q" if (kc == EC - 1 and cc == 3) else None))
    for kc in range(HC):
        p1_jobs.append(p1_pass1(w2re_d, kc, 0, 32 + kc))
    p1_jobs.append(p1_fin(32, 32, HID * E, 1, 2, 3, sw2, "w2inv"))
    for kc in range(HC):
        p1_jobs.append(p1_pass2(w2re_d, w2qS, kc, 0, 3,
                                "w2q" if kc == HC - 1 else None))

    def p1_step(k=1):
        for _ in range(k):
            if p1_jobs:
                p1_jobs.pop(0)()

    tr_done, sp_done = {}, {}
    for it in range(NIT):
        bi = it % 2

        @Sy
        def _(sy, it=it, bi=bi):
            sy.wait_ge(dX.h, dX.n)
            if it >= 2:
                sy.wait_ge(pe.h, tr_done[it - 2])
            sy.dma_start(out=xtile[bi][:],
                         in_=x_d.rearrange("(it p) e -> it p e", p=128)[it]).then_inc(dX.h, 16)
            dX.inc(16)
        XT_L = dX.n

        @Te
        def _(t, it=it, bi=bi, XT_L=XT_L):
            t.wait_ge(dX.h, XT_L)
            if it == 0:
                t.wait_ge(dve.h, CONSTS_DVE)
                t.wait_ge(act.h, BCAST_ACT)
            if it >= 2:
                t.wait_ge(dve.h, sp_done[it - 2])
            mm = None
            for ec in range(EC):
                mm = t.transpose(ps_x[bi][:, ec // 4, (ec % 4) * 128:(ec % 4) * 128 + 128],
                                 xtile[bi][:, ec * 128:(ec + 1) * 128], ident[:])
            mm.then_inc(pe.h, 1)
            pe.inc()
            tr_done[it] = pe.n

        @V
        def _(v, it=it, bi=bi):
            v.wait_ge(pe.h, tr_done[it])
            for h in range(2):
                v.tensor_copy(out=xhi[:, 4 * h:4 * h + 4, it * 128:(it + 1) * 128],
                              in_=ps_x[bi][:, h].rearrange("p (a b) -> p a b", a=4))
            v.drain()
            for h in range(2):
                mm = v.tensor_tensor(out=xlo[:, 4 * h:4 * h + 4, it * 128:(it + 1) * 128],
                                     in0=ps_x[bi][:, h].rearrange("p (a b) -> p a b", a=4),
                                     in1=xhi[:, 4 * h:4 * h + 4, it * 128:(it + 1) * 128],
                                     op=A.subtract)
            mm.then_inc(dve.h, 1)
            dve.inc()
            sp_done[it] = dve.n
        p1_step(4)
    SPLIT_DVE = dve.n

    wslF_free = {0: None, 1: None}
    wslHL_free = {0: None, 1: None}
    psq_evac = {0: None, 1: None}
    qsl_free = [None]

    def proj_qk(wkey, bias_sb, spill):
        w_re = wT_d[wkey].rearrange("(kc p) o -> p kc o", p=128)
        wsplit, mm_done, evac_done, spill_dS = {}, {}, {}, {}
        for oc in range(EC):
            bi = oc % 2
            ffre, hfre = wslF_free[bi], wslHL_free[bi]

            @Sy
            def _(sy, oc=oc, bi=bi, ffre=ffre):
                sy.wait_ge(dW.h, dW.n)
                if ffre is not None:
                    sy.wait_ge(dve.h, ffre)
                sy.dma_start(out=wslF[bi][:],
                             in_=w_re[:, :, oc * 128:(oc + 1) * 128]).then_inc(dW.h, 16)
                dW.inc(16)
            WL = dW.n

            @V
            def _(v, oc=oc, bi=bi, WL=WL, hfre=hfre):
                v.wait_ge(dW.h, WL)
                if hfre is not None:
                    v.wait_ge(pe.h, hfre)
                v.tensor_copy(out=wslH[bi][:], in_=wslF[bi][:])
                v.drain()
                v.tensor_tensor(out=wslL[bi][:], in0=wslF[bi][:], in1=wslH[bi][:],
                                op=A.subtract).then_inc(dve.h, 1)
                dve.inc()
            wsplit[oc] = dve.n
            wslF_free[bi] = dve.n

            act_base = act.n
            for tt in range(4):
                g = 4 * oc + tt
                bfree = psq_evac[tt % 2]

                @Te
                def _(t, oc=oc, tt=tt, g=g, bi=bi, act_base=act_base, bfree=bfree):
                    if tt == 0:
                        t.wait_ge(dve.h, wsplit[oc])
                    if tt >= 2:
                        t.wait_ge(act.h, act_base + tt - 1)
                    elif bfree is not None:
                        t.wait_ge(act.h, bfree)
                    first = True
                    mm = None
                    for (aw, bx) in ((wslH[bi], xhi), (wslH[bi], xlo), (wslL[bi], xhi)):
                        for kc in range(EC):
                            mm = t.matmul(ps_q[:, tt % 2], aw[:, kc, :],
                                          bx[:, kc, tt * 512:(tt + 1) * 512],
                                          start=first,
                                          stop=(aw is wslL[bi] and kc == EC - 1))
                            first = False
                    mm.then_inc(pe.h, 1)
                    pe.inc()
                mm_done[g] = pe.n
            wslHL_free[bi] = mm_done[4 * oc + 3]

            qfree = qsl_free[0]

            @Sc
            def _(sc, oc=oc, qfree=qfree):
                if oc == 0:
                    sc.wait_ge(dP0.h, P0_LOADS)
                if qfree is not None:
                    sc.wait_ge(dS.h, qfree)
                for tt in range(4):
                    g = 4 * oc + tt
                    sc.wait_ge(pe.h, mm_done[g])
                    sc.add(out=qsl[:, tt * 512:(tt + 1) * 512], in_=ps_q[:, tt % 2],
                           add=bias_sb[:, oc:oc + 1]).then_inc(act.h, 1)
                    act.inc()
                    evac_done[g] = act.n
                sc.drain()
                sc.wait_ge(dS.h, dS.n)
                sc.dma_start(out=spill[oc], in_=qsl[:]).then_inc(dS.h, 16)
                dS.inc(16)
            spill_dS[oc] = dS.n
            qsl_free[0] = dS.n
            psq_evac[0] = evac_done[4 * oc + 2]
            psq_evac[1] = evac_done[4 * oc + 3]
            p1_step(4)

    proj_qk("wqT", bqs, [qS[oc] for oc in range(EC)])
    proj_qk("wkT", bks, [kS[oc] for oc in range(EC)])

    wv_re = wT_d["wvT"].rearrange("(kc p) o -> p kc o", p=128)
    wv_done = {}
    for oc in range(EC):
        bi = oc % 2

        ffre = wslF_free[bi]

        @Sy
        def _(sy, oc=oc, bi=bi, ffre=ffre):
            sy.wait_ge(dW.h, dW.n)
            if ffre is not None:
                sy.wait_ge(dve.h, ffre)
            sy.dma_start(out=wslF[bi][:],
                         in_=wv_re[:, :, oc * 128:(oc + 1) * 128]).then_inc(dW.h, 16)
            dW.inc(16)
        WL = dW.n

        @V
        def _(v, oc=oc, bi=bi, WL=WL):
            v.wait_ge(dW.h, WL)
            v.tensor_copy(out=wv_h[:, :, oc * 128:(oc + 1) * 128], in_=wslF[bi][:])
            v.drain()
            v.tensor_tensor(out=wv_l[:, :, oc * 128:(oc + 1) * 128], in0=wslF[bi][:],
                            in1=wv_h[:, :, oc * 128:(oc + 1) * 128],
                            op=A.subtract).then_inc(dve.h, 1)
            dve.inc()
        wv_done[oc] = dve.n
        wslF_free[bi] = dve.n
        p1_step(2)
    WV_DVE = dve.n

    axs.close()
    ps_v = [a1.enter_context(nc.psum_tensor(f"ps_v{i}", [128, 2, 512], F32)) for i in range(2)]

    vt_mm, vt_evac, vspill = {}, {}, {}
    for it in range(NIT):
        bi = it % 2

        @Te
        def _(t, it=it, bi=bi):
            if it == 0:
                t.wait_ge(dve.h, WV_DVE)
            if it >= 2:
                t.wait_ge(dve.h, vt_evac[it - 2])
            mm = None
            for nh in range(2):
                first = True
                for (ax, bw) in ((xhi, wv_h), (xhi, wv_l), (xlo, wv_h)):
                    for kc in range(EC):
                        mm = t.matmul(ps_v[bi][:, nh], ax[:, kc, it * 128:(it + 1) * 128],
                                      bw[:, kc, nh * 512:(nh + 1) * 512],
                                      start=first, stop=(ax is xlo and kc == EC - 1))
                        first = False
            mm.then_inc(pe.h, 1)
            pe.inc()
        vt_mm[it] = pe.n

        @V
        def _(v, it=it, bi=bi):
            v.wait_ge(pe.h, vt_mm[it])
            if it == 0:
                v.wait_ge(act.h, BCAST_ACT)
            if it >= 1:
                v.wait_ge(dS.h, vspill[it - 1])
            v.tensor_add(out=vtsl[bi][:], in0=ps_v[bi][:].rearrange("p a b -> p (a b)"),
                         in1=bvb[:]).then_inc(dve.h, 1)
            dve.inc()
        vt_evac[it] = dve.n

        @Sc
        def _(sc, it=it, bi=bi):
            sc.wait_ge(dve.h, vt_evac[it])
            sc.wait_ge(dS.h, dS.n)
            sc.dma_start(out=vS[it], in_=vtsl[bi][:]).then_inc(dS.h, 16)
            dS.inc(16)
        vspill[it] = dS.n
        p1_step(2)

    p1_step(len(p1_jobs))
    assert not p1_jobs
    P1_W1INV, P1_W2INV = p1f["w1inv"], p1f["w2inv"]
    P1_W1Q, P1_W2Q = p1f["w1q"], p1f["w2q"]

    @Sc
    def _(sc):
        sc.wait_ge(dS.h, dS.n)
        sc.wait_ge(dG.h, P1_W2Q)
        sc.drain().then_inc(act.h, 1)
        act.inc()
    A1_DONE_ACT = act.n
    a1.close()
    fs = ExitStack()
    ctxflat = fs.enter_context(nc.sbuf_tensor("ctxflat", [128, 2 * EC * T], BF16))
    cxAll = ctxflat.rearrange("p (d e t) -> p d e t", d=2, e=EC)
    wbufF = ctxflat.rearrange("p (kc c) -> p kc c", kc=EC)
    s2row = fs.enter_context(nc.sbuf_tensor("s2row", [1, T], F32))
    if UPTO <= 2:
        fs.close()
        es.close()
        return nc

    a2 = ExitStack()
    efrow = a2.enter_context(nc.sbuf_tensor("efrow", [16, T], F32))
    qzA = a2.enter_context(nc.sbuf_tensor("qzA", [128, 2, T], F32))
    qzB = a2.enter_context(nc.sbuf_tensor("qzB", [128, 2, T], F32))
    kg = a2.enter_context(nc.sbuf_tensor("kg", [128, 2, T + 64], F32))
    gzA = a2.enter_context(nc.sbuf_tensor("gzA", [128, NIT, 128], F32))
    gzB = a2.enter_context(nc.sbuf_tensor("gzB", [128, NIT, 128], F32))
    pbuf = a2.enter_context(nc.sbuf_tensor("pbuf", [128, 2, 2, WP + 196], F32))
    pT = a2.enter_context(nc.sbuf_tensor("pT", [128, 2, 3, W], F32))
    Rt = a2.enter_context(nc.sbuf_tensor("Rt", [128, 2], F32))
    e0sb = a2.enter_context(nc.sbuf_tensor("e0sb", [2, 4, 512], F32))
    p0T = a2.enter_context(nc.sbuf_tensor("p0T", [128, NIT, 2], F32))
    S0sb = a2.enter_context(nc.sbuf_tensor("S0sb", [2, 2], F32))
    iszsc = a2.enter_context(nc.sbuf_tensor("iszsc", [128, NIT, 2], F32))
    q0sel = a2.enter_context(nc.sbuf_tensor("q0sel", [128, 2], F32))
    pcI = a2.enter_context(nc.sbuf_tensor("pcI", [1, 2 * T], F32))
    izI = a2.enter_context(nc.sbuf_tensor("izI", [1, 2 * T], F32))
    mvzI = a2.enter_context(nc.sbuf_tensor("mvzI", [1, 2, 128], F32))
    c0fix = a2.enter_context(nc.sbuf_tensor("c0fix", [128, 3], F32))

    @Sy
    def _(sy):
        sy.wait_ge(act.h, A1_DONE_ACT)
        sy.wait_ge(dA2.h, dA2.n)
        sy.dma_start(out=efrow[:], in_=pbR_d[:, :]).then_inc(dA2.h, 16)
        dA2.inc(16)
    EFROW_L = dA2.n

    @Sc
    def _(sc):
        sc.wait_ge(dA2.h, EFROW_L)
        sc.activation(out=efrow[:], in_=efrow[:], func=AF.Exp).then_inc(act.h, 1)
        act.inc()
    EFROW_ACT = act.n

    @V
    def _(v):
        v.wait_ge(act.h, A1_DONE_ACT)
        v.memset(kg[:, :, 0:32], 0.0)
        v.memset(kg[:, :, T + 32:T + 64], 0.0)
        v.memset(qzA[64:128, :, :], 0.0)
        v.memset(qzB[0:64, :, :], 0.0)
        v.memset(gzA[:], 0.0)
        v.memset(gzB[:], 0.0)
        v.memset(pbuf[:], 0.0)
        v.memset(pT[:], 0.0)
        v.memset(S_all[:], 0.0)
        v.drain().then_inc(dve.h, 1)
        dve.inc()
    A2_INIT_DVE = dve.n

    vre = vS.rearrange("it p d -> p it d")
    pbuf_last = {0: None, 1: None}

    for g_i in range(A2_NG):
        @Sy
        def _(sy, g_i=g_i):
            sy.wait_ge(dA2.h, dA2.n)
            sy.wait_ge(pe.h, pe.n)
            sy.wait_ge(dve.h, dve.n)
            if g_i == 0:
                sy.wait_ge(dS.h, dS.n)
            for cc in range(2):
                sy.dma_start(out=qzA[0:64, cc, :],
                             in_=qS[2 * g_i + cc][0:64, :]).then_inc(dA2.h, 16)
                sy.dma_start(out=qzB[64:128, cc, :],
                             in_=qS[2 * g_i + cc][64:128, :]).then_inc(dA2.h, 16)
                sy.dma_start(out=kg[:, cc, 32:32 + T], in_=kS[2 * g_i + cc]).then_inc(dA2.h, 16)
                dA2.inc(48)
            sy.dma_start(out=gzA[:, :, 0:64],
                         in_=vre[:, :, 256 * g_i:256 * g_i + 64]).then_inc(dA2.h, 16)
            sy.dma_start(out=gzB[:, :, 64:128],
                         in_=vre[:, :, 256 * g_i + 64:256 * g_i + 128]).then_inc(dA2.h, 16)
            dA2.inc(32)
        A2_G_LOADS = dA2.n

        for p_i in range(2):
            gh = 4 * g_i + 2 * p_i
            cc = p_i

            if p_i == 1:
                @Sy
                def _(sy, g_i=g_i):
                    sy.wait_ge(dA2.h, dA2.n)
                    sy.wait_ge(pe.h, pe.n)
                    sy.dma_start(out=gzA[:, :, 0:64],
                                 in_=vre[:, :, 256 * g_i + 128:256 * g_i + 192]).then_inc(dA2.h, 16)
                    sy.dma_start(out=gzB[:, :, 64:128],
                                 in_=vre[:, :, 256 * g_i + 192:256 * g_i + 256]).then_inc(dA2.h, 16)
                    dA2.inc(32)
            PAIR_LOADS = dA2.n

            pair = ExitStack()
            sfx = f"_{g_i}_{p_i}"
            ps_sc = [pair.enter_context(nc.psum_tensor(f"ps_sc{i}" + sfx, [128, 2, 256], F32))
                     for i in range(2)]
            ps_trA = pair.enter_context(nc.psum_tensor("ps_trA" + sfx, [128, 512], F32))
            ps_trB = pair.enter_context(nc.psum_tensor("ps_trB" + sfx, [128, 512], F32))
            ps_ctx = pair.enter_context(nc.psum_tensor("ps_ctx" + sfx, [128, 4, 512], F32))

            def ctx_mm(t, lhsT, rhs, lo, hi, stop_banks=()):
                mm = None
                for (aa, bb) in cut512(lo, hi):
                    bank = aa // 512
                    piece = rhs if (aa == lo and bb == hi) else rhs[:, aa - lo:bb - lo]
                    mm = t.matmul(ps_ctx[:, bank, aa - bank * 512:bb - bank * 512],
                                  lhsT, piece, start=False,
                                  stop=(bank in stop_banks))
                return mm

            @Te
            def _(t):
                t.wait_ge(dve.h, dve.n)
                t.wait_ge(act.h, act.n)
                mm = None
                for bank in range(4):
                    mm = t.matmul(ps_ctx[:, bank, :], ones_row[0:1, :], zrow[0:1, :],
                                  start=True, stop=False)
                mm.then_inc(pe.h, 1)
                pe.inc()

            ACT_BASE = act.n
            sc_pe, tr_pe, epi_dve = {}, {}, {}
            for it in range(NIT):
                i0 = it * 128
                buf = it % 2

                @Te
                def _(t, it=it, i0=i0, cc=cc, buf=buf):
                    t.wait_ge(dA2.h, PAIR_LOADS)
                    if it < 2:
                        t.wait_ge(act.h, ACT_BASE)
                    else:
                        t.wait_ge(act.h, ACT_BASE + it - 1)
                    if it == 0:
                        t.wait_ge(dve.h, max(A2_INIT_DVE, dve.n))
                    mm = None
                    for hh, qz in ((0, qzA), (1, qzB)):
                        q_l = qz[:, cc, i0:i0 + 128]
                        t.matmul(ps_sc[buf][:, hh, 0:192], q_l,
                                 kg[:, cc, i0:i0 + 192], start=True, stop=True)
                        mm = t.matmul(ps_sc[buf][:, hh, 192:193], q_l,
                                      kg[:, cc, 32:33], start=True, stop=True)
                    mm.then_inc(pe.h, 1)
                    pe.inc()
                sc_pe[it] = pe.n

                @V
                def _(v, it=it, gh=gh, buf=buf):
                    v.wait_ge(act.h, ACT_BASE + it + 1)
                    var = 1 if it == 0 else (2 if it == NIT - 1 else 0)
                    v.tensor_tensor(out=pbuf[:, buf, :, WP:WP + 192],
                                    in0=pbuf[:, buf, :, WP:WP + 192],
                                    in1=eB[:, var, :].unsqueeze(1).broadcast_to([128, 2, W]),
                                    op=A.mult)
                    v.tensor_tensor(out=pbuf[:, buf, :, WP + 192:WP + 193],
                                    in0=pbuf[:, buf, :, WP + 192:WP + 193],
                                    in1=efc[:, it:it + 1].unsqueeze(1).broadcast_to([128, 2, 1]),
                                    op=A.mult)
                    v.drain()
                    for hh in range(2):
                        v.scalar_tensor_tensor(out=pbuf[:, buf, hh, WP:WP + 193],
                                               in0=pbuf[:, buf, hh, WP:WP + 193], scalar=1.0,
                                               in1=pbuf[:, buf, hh, WP:WP + 193],
                                               op0=A.is_ge, op1=A.mult,
                                               accum_out=S_all[:, it, gh + hh:gh + hh + 1])
                    v.drain()
                    v.tensor_single_scalar(out=Rt[:], in_=S_all[:, it, gh:gh + 2],
                                           scalar=0.0, op=A.is_equal)
                    v.drain()
                    v.tensor_add(out=Rt[:], in0=Rt[:], in1=S_all[:, it, gh:gh + 2])
                    v.drain()
                    v.reciprocal(out=Rt[:], in_=Rt[:])
                    v.drain()
                    v.tensor_tensor(out=pbuf[:, buf, :, WP:WP + 193],
                                    in0=pbuf[:, buf, :, WP:WP + 193],
                                    in1=Rt[:].unsqueeze(2).broadcast_to([128, 2, 193]),
                                    op=A.mult)
                    v.drain()
                    v.tensor_copy(out=pc_st[:, it, gh:gh + 2],
                                  in_=pbuf[:, buf, :, WP + 192:WP + 193]).then_inc(dve.h, 1)
                    dve.inc()
                epi_dve[it] = dve.n

                @Te
                def _(t, it=it, buf=buf):
                    t.wait_ge(dve.h, epi_dve[it])
                    mm = None
                    for hh, ps_tr in ((0, ps_trA), (1, ps_trB)):
                        t.transpose(ps_tr[:, 0:128], pbuf[:, buf, hh, 0:128], ident[:])
                        t.transpose(ps_tr[:, 128:256], pbuf[:, buf, hh, 128:256], ident[:])
                        mm = t.transpose(ps_tr[0:33, 256:384], pbuf[:, buf, hh, 256:289],
                                         ident[:])
                    mm.then_inc(pe.h, 1)
                    pe.inc()
                tr_pe[it] = pe.n

                @V
                def _(v, it=it):
                    v.wait_ge(pe.h, tr_pe[it])
                    for hh, ps_tr in ((0, ps_trA), (1, ps_trB)):
                        v.tensor_copy(out=pT[:, hh, it % 3, 32:160], in_=ps_tr[:, 128:256])
                        if it > 0:
                            v.tensor_copy(out=pT[96:128, hh, (it - 1) % 3, 160:192],
                                          in_=ps_tr[96:128, 0:32])
                        if it < NIT - 1:
                            v.tensor_copy(out=pT[0:32, hh, (it + 1) % 3, 0:32],
                                          in_=ps_tr[0:32, 256 + 96:256 + 128])
                    v.drain().then_inc(dve.h, 1)
                    dve.inc()
                ASM_DVE = dve.n

                if it >= 1:
                    tt_ = it - 1

                    @Te
                    def _(t, tt_=tt_, ASM_DVE=ASM_DVE):
                        t.wait_ge(dve.h, ASM_DVE)
                        mm = None
                        for hh, gz in ((0, gzA), (1, gzB)):
                            lo = max(0, 128 * tt_ - 32)
                            hi = min(T, 128 * tt_ + 160)
                            rhs = pT[:, hh, tt_ % 3,
                                     (lo - (128 * tt_ - 32)):(hi - (128 * tt_ - 32))]
                            mm = ctx_mm(t, gz[:, tt_, :], rhs, lo, hi)
                        mm.then_inc(pe.h, 1)
                        pe.inc()

            @Sc
            def _(sc, gh=gh, ACT_BASE=ACT_BASE, pb0=pbuf_last[0], pb1=pbuf_last[1]):
                pbl = {0: pb0, 1: pb1}
                for it in range(NIT):
                    buf = it % 2
                    sc.wait_ge(pe.h, sc_pe[it])
                    if it >= 2:
                        sc.wait_ge(pe.h, tr_pe[it - 2])
                    elif pbl[buf] is not None:
                        sc.wait_ge(pe.h, pbl[buf])
                    sc.activation(out=pbuf[:, buf, :, WP:WP + 193],
                                  in_=ps_sc[buf][:, :, 0:193], func=AF.Exp,
                                  scale=0.125).then_inc(act.h, 1)
                    act.inc()
            assert act.n == ACT_BASE + NIT
            pbuf_last[0] = tr_pe[NIT - 2]
            pbuf_last[1] = tr_pe[NIT - 1]

            @Te
            def _(t):
                t.wait_ge(dve.h, dve.n)
                tt_ = NIT - 1
                mm = None
                for hh, gz in ((0, gzA), (1, gzB)):
                    lo, hi = 128 * tt_ - 32, T
                    rhs = pT[:, hh, tt_ % 3, 0:hi - lo]
                    mm = ctx_mm(t, gz[:, tt_, :], rhs, lo, hi)
                mm.then_inc(pe.h, 1)
                pe.inc()
            CTXW_PE = pe.n

            @V
            def _(v, cc=cc):
                v.wait_ge(pe.h, CTXW_PE)
                v.memset(q0sel[:], 0.0)
                v.drain()
                v.tensor_copy(out=q0sel[0:64, 0:1], in_=qzA[0:64, cc, 0:1])
                v.tensor_copy(out=q0sel[64:128, 1:2], in_=qzB[64:128, cc, 0:1])
                v.drain().then_inc(dve.h, 1)
                dve.inc()
            Q0_DVE = dve.n

            e0_pe = {}
            for nt in range(4):
                @Te
                def _(t, nt=nt, cc=cc):
                    t.wait_ge(dve.h, Q0_DVE)
                    if nt > 0:
                        t.wait_ge(act.h, act.n + nt)
                    t.matmul(ps_trA[0:2, 0:512], q0sel[:],
                             kg[:, cc, 32 + nt * 512:32 + (nt + 1) * 512],
                             start=True, stop=True).then_inc(pe.h, 1)
                    pe.inc()
                e0_pe[nt] = pe.n

            @Sc
            def _(sc):
                for nt in range(4):
                    sc.wait_ge(pe.h, e0_pe[nt])
                    sc.activation(out=e0sb[0:2, nt, :], in_=ps_trA[0:2, 0:512],
                                  func=AF.Exp, scale=0.125).then_inc(act.h, 1)
                    act.inc()
            E0_ACT = act.n

            @V
            def _(v):
                v.wait_ge(act.h, E0_ACT)
                v.tensor_tensor(out=e0sb[0:2, :, :], in0=e0sb[0:2, :, :],
                                in1=efrow[0:2, :].rearrange("p (a b) -> p a b", a=4),
                                op=A.mult)
                v.drain()
                v.scalar_tensor_tensor(out=e0sb[0:2, :, :], in0=e0sb[0:2, :, :],
                                       scalar=1.0, in1=e0sb[0:2, :, :],
                                       op0=A.is_ge, op1=A.mult,
                                       accum_out=S0sb[0:2, 0:1])
                v.drain()
                v.tensor_single_scalar(out=S0sb[0:2, 1:2], in_=S0sb[0:2, 0:1],
                                       scalar=0.0, op=A.is_equal)
                v.drain()
                v.tensor_add(out=S0sb[0:2, 1:2], in0=S0sb[0:2, 1:2], in1=S0sb[0:2, 0:1])
                v.drain()
                v.reciprocal(out=S0sb[0:2, 1:2], in_=S0sb[0:2, 1:2])
                v.drain()
                v.tensor_scalar(out=e0sb[0:2, :, :], in0=e0sb[0:2, :, :],
                                scalar1=S0sb[0:2, 1:2], scalar2=None, op0=A.mult)
                v.drain().then_inc(dve.h, 1)
                dve.inc()
            E0_DVE = dve.n

            @Te
            def _(t):
                t.wait_ge(dve.h, E0_DVE)
                mm = None
                for jt in range(NIT):
                    mm = t.transpose(ps_trB[:, 2 * jt:2 * jt + 2],
                                     e0sb[0:2, jt // 4, (jt % 4) * 128:(jt % 4) * 128 + 128],
                                     ident[0:2, 0:2])
                mm.then_inc(pe.h, 1)
                pe.inc()
            P0T_PE = pe.n

            @V
            def _(v):
                v.wait_ge(pe.h, P0T_PE)
                v.tensor_copy(out=p0T[:].rearrange("p a b -> p (a b)"),
                              in_=ps_trB[:, 0:32]).then_inc(dve.h, 1)
                dve.inc()
            P0T_DVE = dve.n

            @Te
            def _(t):
                t.wait_ge(dve.h, P0T_DVE)
                t.transpose(ps_trA[0:1, 0:2], S0sb[0:2, 0:1],
                            ident[0:2, 0:2]).then_inc(pe.h, 1)
                pe.inc()
            S0T_PE = pe.n

            @V
            def _(v, gh=gh):
                v.wait_ge(pe.h, S0T_PE)
                v.tensor_add(out=S_all[0:1, 0, gh:gh + 2], in0=S_all[0:1, 0, gh:gh + 2],
                             in1=ps_trA[0:1, 0:2])
                v.drain()
                v.tensor_single_scalar(out=iszsc[:], in_=S_all[:, :, gh:gh + 2],
                                       scalar=0.0, op=A.is_equal)
                v.drain().then_inc(dve.h, 1)
                dve.inc()
            ISZ_DVE = dve.n

            mv_pe = {}
            for hh, gz in ((0, gzA), (1, gzB)):
                @Te
                def _(t, hh=hh, gz=gz):
                    t.wait_ge(dve.h, ISZ_DVE)
                    if hh == 1:
                        t.wait_ge(dve.h, dve.n)
                    mm = None
                    for jt in range(NIT):
                        mm = t.matmul(ps_trA[0:1, 128:256], ones_col[:, 0:1], gz[:, jt, :],
                                      start=(jt == 0), stop=(jt == NIT - 1))
                    mm.then_inc(pe.h, 1)
                    pe.inc()
                mv_pe[hh] = pe.n

                @V
                def _(v, hh=hh):
                    v.wait_ge(pe.h, mv_pe[hh])
                    v.tensor_scalar(out=mvzI[0:1, hh, :], in0=ps_trA[0:1, 128:256],
                                    scalar1=1.0 / T, scalar2=None,
                                    op0=A.mult).then_inc(dve.h, 1)
                    dve.inc()
            MV_DVE = dve.n

            @G
            def _(g, gh=gh):
                g.wait_ge(dve.h, ISZ_DVE)
                g.wait_ge(dG.h, dG.n)
                g.dma_start(out=pcI[0:1, :], in_=pc_st[:, :, gh:gh + 2]).then_inc(dG.h, 16)
                dG.inc(16)
                g.dma_start(out=izI[0:1, :], in_=iszsc[:]).then_inc(dG.h, 16)
                dG.inc(16)
                g.wait_ge(dG.h, dG.n)
                g.engine_nop().then_inc(gp.h, 1)
                gp.inc()
            ROWS_GP = gp.n

            @Te
            def _(t, gh=gh):
                t.wait_ge(gp.h, ROWS_GP)
                t.wait_ge(dve.h, MV_DVE)
                pcv = pcI[0:1, :].rearrange("o (p it h) -> o it h p", p=128, it=NIT)
                izv = izI[0:1, :].rearrange("o (p it h) -> o it h p", p=128, it=NIT)
                mm = None
                for hh, gz in ((0, gzA), (1, gzB)):
                    last_h = hh == 1
                    for bank in range(4):
                        ctx_mm(t, gz[0:1, 0, :], pcv[:, 4 * bank:4 * bank + 4, hh, :],
                               bank * 512, (bank + 1) * 512)
                    for bank in range(4):
                        ctx_mm(t, mvzI[0:1, hh, :],
                               izv[:, 4 * bank:4 * bank + 4, hh, :],
                               bank * 512, (bank + 1) * 512,
                               stop_banks=(0, 1, 2, 3) if last_h else ())
                    for jt in range(NIT):
                        mm = t.matmul(ps_trB[:, hh:hh + 1], gz[:, jt, :],
                                      p0T[:, jt, hh:hh + 1],
                                      start=(jt == 0), stop=(jt == NIT - 1))
                mm.then_inc(pe.h, 1)
                pe.inc()
            CTX_PE = pe.n

            @V
            def _(v, g_i=g_i, p_i=p_i):
                v.wait_ge(pe.h, CTX_PE)
                dc = 2 * g_i + p_i
                v.tensor_copy(out=c0fix[:, 0:2], in_=ps_trB[:, 0:2])
                v.drain()
                v.tensor_add(out=c0fix[:, 2:3], in0=ps_ctx[:, 0, 0:1], in1=c0fix[:, 0:1])
                v.drain()
                v.tensor_add(out=c0fix[:, 2:3], in0=c0fix[:, 2:3], in1=c0fix[:, 1:2])
                v.tensor_copy(out=cxAll[:, 0, dc, :],
                              in_=ps_ctx[:].rearrange("p a b -> p (a b)"))
                v.drain()
                v.tensor_copy(out=cxAll[:, 0, dc, 0:1], in_=c0fix[:, 2:3])
                v.tensor_tensor(out=cxAll[:, 1, dc, 1:T],
                                in0=ps_ctx[:].rearrange("p a b -> p (a b)")[:, 1:T],
                                in1=cxAll[:, 0, dc, 1:T], op=A.subtract)
                v.drain()
                v.tensor_tensor(out=cxAll[:, 1, dc, 0:1], in0=c0fix[:, 2:3],
                                in1=cxAll[:, 0, dc, 0:1], op=A.subtract)
                v.drain().then_inc(dve.h, 1)
                dve.inc()
            CXL_DVE = dve.n

            pair.close()

    @Sc
    def _(sc):
        sc.wait_ge(dve.h, CXL_DVE)
        sc.drain().then_inc(act.h, 1)
        act.inc()
    A2_DONE_ACT = act.n
    A2_CX_DVE = CXL_DVE
    a2.close()
    if UPTO <= 3:
        fs.close()
        es.close()
        return nc

    wfs = ExitStack()
    xq1FM = wfs.enter_context(nc.sbuf_tensor("xq1FM", [128, EC, T], BF16))
    a3 = ExitStack()
    wo_h = a3.enter_context(nc.sbuf_tensor("wo_h", [128, EC, E], BF16))
    wo_l = a3.enter_context(nc.sbuf_tensor("wo_l", [128, EC, E], BF16))
    xq1TM = a3.enter_context(nc.sbuf_tensor("xq1TM", [128, NIT, E], BF16))
    xt3 = a3.enter_context(nc.sbuf_tensor("xt3", [128, E], F32))
    resid = a3.enter_context(nc.sbuf_tensor("resid", [128, E], F32))
    sqscr = a3.enter_context(nc.sbuf_tensor("sqscr", [128, E], F32))
    lnst = a3.enter_context(nc.sbuf_tensor("lnst", [128, 8], F32))
    wslo = sqscr[:, :].rearrange("p (a b) -> p a b", a=EC)
    ps_o = [a3.enter_context(nc.psum_tensor(f"ps_o{i}", [128, 2, 512], F32))
            for i in range(2)]
    ps_bf = a3.enter_context(nc.psum_tensor("ps_bf", [128, 2, 512], BF16))

    w1re = w1qS.rearrange("kc p c -> p kc c")

    wo_done = {}
    for oc in range(EC):
        @Sy
        def _(sy, oc=oc):
            sy.wait_ge(dWo.h, dWo.n)
            if oc == 0:
                sy.wait_ge(act.h, A2_DONE_ACT)
            else:
                sy.wait_ge(dve.h, wo_done[oc - 1])
            sy.dma_start(out=wslo,
                         in_=wT_d["woT"].rearrange("(kc p) o -> p kc o", p=128)
                         [:, :, oc * 128:(oc + 1) * 128]).then_inc(dWo.h, 16)
            dWo.inc(16)
        WL = dWo.n

        @V
        def _(v, oc=oc, WL=WL):
            v.wait_ge(dWo.h, WL)
            v.tensor_copy(out=wo_h[:, :, oc * 128:(oc + 1) * 128], in_=wslo)
            v.drain()
            v.tensor_tensor(out=wo_l[:, :, oc * 128:(oc + 1) * 128], in0=wslo,
                            in1=wo_h[:, :, oc * 128:(oc + 1) * 128],
                            op=A.subtract).then_inc(dve.h, 1)
            dve.inc()
        wo_done[oc] = dve.n
    WO_DVE = dve.n

    ctmm, res_dve, q1_dve, x1sp = {}, {}, {}, {}
    for it in range(NIT):
        ps = ps_o[it % 2]
        dci = dCt[it % 2]

        @Sy
        def _(sy, it=it, dci=dci):
            sy.wait_ge(dci.h, dci.n)
            if it >= 1:
                sy.wait_ge(dve.h, res_dve[it - 1])
            sy.dma_start(out=xt3[:],
                         in_=x_d.rearrange("(it p) e -> it p e", p=128)[it]).then_inc(dci.h, 16)
            dci.inc(16)
        CT_L = dci.n

        @Te
        def _(t, it=it, ps=ps, dci=dci, CT_L=CT_L):
            t.wait_ge(dci.h, CT_L)
            if it == 0:
                t.wait_ge(dve.h, A2_CX_DVE)
            t.wait_ge(dve.h, max(WO_DVE, res_dve.get(it - 2, 0)))
            i0 = it * 128
            mm = None
            for nh in range(2):
                first = True
                for (hl, bw) in ((0, wo_h), (0, wo_l), (1, wo_h)):
                    for kc in range(EC):
                        mm = t.matmul(ps[:, nh], cxAll[:, hl, kc, i0:i0 + 128],
                                      bw[:, kc, nh * 512:(nh + 1) * 512],
                                      start=first, stop=(hl == 1 and kc == EC - 1))
                        first = False
            mm.then_inc(pe.h, 1)
            pe.inc()
        ctmm[it] = pe.n

        @V
        def _(v, it=it, ps=ps):
            v.wait_ge(pe.h, ctmm[it])
            if it == 0:
                v.wait_ge(act.h, BCAST_ACT)
            v.drain()
            if it >= 1:
                v.wait_ge(dS.h, x1sp[it - 1])
            v.tensor_add(out=sqscr[:], in0=ps[:].rearrange("p a b -> p (a b)"), in1=bob[:])
            v.drain()
            v.scalar_tensor_tensor(out=resid[:], in0=sqscr[:], scalar=0.0, in1=xt3[:],
                                   op0=A.add, op1=A.add,
                                   accum_out=lnst[:, 0:1]).then_inc(dve.h, 1)
            dve.inc()
            res_dve[it] = dve.n
            v.drain()
            v.tensor_tensor(out=sqscr[:], in0=resid[:], in1=resid[:], op=A.mult)
            v.drain()
            v.tensor_reduce(out=lnst[:, 1:2], in_=sqscr[:], axis=AX.X, op=A.add)
            v.drain()
            v.tensor_scalar(out=lnst[:, 0:2], in0=lnst[:, 0:2], scalar1=1.0 / E,
                            scalar2=None, op0=A.mult)
            v.drain()
            v.tensor_tensor(out=lnst[:, 2:3], in0=lnst[:, 0:1], in1=lnst[:, 0:1], op=A.mult)
            v.drain()
            v.tensor_tensor(out=lnst[:, 1:2], in0=lnst[:, 1:2], in1=lnst[:, 2:3],
                            op=A.subtract)
            v.drain()
            v.tensor_scalar(out=lnst[:, 1:2], in0=lnst[:, 1:2], scalar1=1e-5,
                            scalar2=None, op0=A.add).then_inc(dve.h, 1)
            dve.inc()
        ST_DVE = dve.n

        @Sc
        def _(sc, it=it, ST_DVE=ST_DVE):
            sc.wait_ge(dve.h, ST_DVE)
            sc.activation(out=lnst[:, 2:3], in_=lnst[:, 1:2],
                          func=AF.Sqrt).then_inc(act.h, 1)
            act.inc()
        SQT_ACT = act.n

        @V
        def _(v, it=it, SQT_ACT=SQT_ACT):
            v.wait_ge(act.h, SQT_ACT)
            v.reciprocal(out=lnst[:, 3:4], in_=lnst[:, 2:3])
            v.drain()
            v.tensor_scalar(out=resid[:], in0=resid[:], scalar1=lnst[:, 0:1],
                            scalar2=lnst[:, 3:4], op0=A.subtract,
                            op1=A.mult).then_inc(dve.h, 1)
            dve.inc()
            X1_DVE = dve.n
            v.drain()
            if it == 0:
                v.wait_ge(dve2.h, P1_W1INV)
            v.reduce_max(out=lnst[:, 4:5], in_=resid[:], axis=AX.X,
                         apply_absolute_value=True)
            v.drain()
            v.tensor_scalar(out=lnst[:, 4:5], in0=lnst[:, 4:5], scalar1=1e-5,
                            scalar2=None, op0=A.max)
            v.drain()
            v.reciprocal(out=lnst[:, 5:6], in_=lnst[:, 4:5])
            v.drain()
            v.tensor_scalar(out=lnst[:, 5:6], in0=lnst[:, 5:6], scalar1=127.0,
                            scalar2=None, op0=A.mult)
            v.tensor_scalar(out=sg1tm[:, it:it + 1], in0=lnst[:, 4:5],
                            scalar1=sw1[:, 0:1], scalar2=1.0 / 127.0,
                            op0=A.mult, op1=A.mult)
            v.drain()
            v.tensor_scalar(out=sqscr[:], in0=resid[:], scalar1=lnst[:, 5:6],
                            scalar2=magic[:, 0:1], op0=A.mult, op1=A.add)
            v.drain()
            v.tensor_scalar(out=xq1TM[:, it, :], in0=sqscr[:], scalar1=magic[:, 0:1],
                            scalar2=None, op0=A.subtract).then_inc(dve.h, 1)
            dve.inc()
            q1_dve[it] = dve.n

        @Sc
        def _(sc, it=it):
            sc.wait_ge(dve.h, q1_dve[it] - 1)
            sc.wait_ge(dS.h, dS.n)
            sc.dma_start(out=x1S[it], in_=resid[:]).then_inc(dS.h, 16)
            dS.inc(16)
        x1sp[it] = dS.n
    A3_DVE = q1_dve[NIT - 1]

    @Sy
    def _(sy):
        sy.wait_ge(pe.h, ctmm[NIT - 1])
        sy.wait_ge(dG.h, P1_W1Q)
        sy.wait_ge(dF.h, dF.n)
        sy.dma_start(out=wbufF[:, :, :], in_=w1re[:, :, :]).then_inc(dF.h, 16)
        dF.inc(16)
    WBF_L = dF.n

    f0pe, f0ev = {}, {}
    for it in range(NIT):
        for h2 in range(2):
            idx = 2 * it + h2

            @Te
            def _(t, it=it, h2=h2, idx=idx):
                t.wait_ge(dve.h, q1_dve[it])
                if idx >= 2:
                    t.wait_ge(act.h, f0ev[idx - 2])
                mm = None
                for e4 in range(4):
                    ec = 4 * h2 + e4
                    mm = t.transpose(ps_bf[:, idx % 2, e4 * 128:(e4 + 1) * 128],
                                     xq1TM[:, it, ec * 128:(ec + 1) * 128], identb[:])
                mm.then_inc(pe.h, 1)
                pe.inc()
            f0pe[idx] = pe.n

            @Sc
            def _(sc, it=it, h2=h2, idx=idx):
                sc.wait_ge(pe.h, f0pe[idx])
                sc.copy(out=xq1FM[:, 4 * h2:4 * h2 + 4, it * 128:(it + 1) * 128],
                        in_=ps_bf[:, idx % 2].rearrange("p (a b) -> p a b", a=4)
                        ).then_inc(act.h, 1)
                act.inc()
            f0ev[idx] = act.n
    XQ1_ACT = act.n
    a3.close()
    if UPTO <= 4:
        wfs.close()
        fs.close()
        es.close()
        return nc

    ff = ExitStack()
    harena = ff.enter_context(nc.sbuf_tensor("harena", [128, HC, 258], F32))
    tmpa = ff.enter_context(nc.sbuf_tensor("tmpa", [128, 8, TT], F32))
    xq2sb = ff.enter_context(nc.sbuf_tensor("xq2sb", [128, HC, TT], BF16))
    mabs = ff.enter_context(nc.sbuf_tensor("mabs", [128, TT], F32))
    mx4 = ff.enter_context(nc.sbuf_tensor("mx4", [128, 8], F32))
    s1bc = ff.enter_context(nc.sbuf_tensor("s1bc", [128, T], F32))
    s1row = ff.enter_context(nc.sbuf_tensor("s1row", [1, T], F32))
    s2sb = ff.enter_context(nc.sbuf_tensor("s2sb", [1, 256], F32))
    sgb2 = ff.enter_context(nc.sbuf_tensor("sgb2", [128, TT], F32))
    ps_h = ff.enter_context(nc.psum_tensor("ps_h", [128, 2, 512], F32))
    ps_sg = ff.enter_context(nc.psum_tensor("ps_sg", [128, 512], F32))
    ps_mx = ff.enter_context(nc.psum_tensor("ps_mx", [128, 2, 128], F32))
    ps_bc = ff.enter_context(nc.psum_tensor("ps_bc", [128, 256], F32))

    xq2re = xq2S.rearrange("kc p c -> p kc c")

    @G
    def _(g):
        g.wait_ge(dve.h, A3_DVE)
        g.wait_ge(dS.h, dS.n)
        g.wait_ge(dG.h, dG.n)
        g.dma_start(out=s1row[0:1, :].rearrange("o (p it) -> o p it", p=128),
                    in_=sg1tm[:]).then_inc(dG.h, 16)
        dG.inc(16)
        g.wait_ge(dG.h, dG.n)
        g.engine_nop().then_inc(gp.h, 1)
        gp.inc()
    S1ROW_GP = gp.n

    s1v = s1row[0:1, :].rearrange("o (p it) -> o it p", p=128)
    s1mm, s1ev = {}, {}
    for q in range(4):
        @Te
        def _(t, q=q):
            if q == 0:
                t.wait_ge(gp.h, S1ROW_GP)
            else:
                t.wait_ge(dve.h, s1ev[q - 1])
            t.matmul(ps_sg[:, 0:512], ones_row[0:1, :], s1v[:, 4 * q:4 * q + 4, :],
                     start=True, stop=True).then_inc(pe.h, 1)
            pe.inc()
        s1mm[q] = pe.n

        @V
        def _(v, q=q):
            v.wait_ge(pe.h, s1mm[q])
            v.tensor_copy(out=s1bc[:, q * 512:(q + 1) * 512],
                          in_=ps_sg[:, 0:512]).then_inc(dve.h, 1)
            dve.inc()
        s1ev[q] = dve.n
    S1BC_DVE = dve.n

    @V
    def _(v):
        v.memset(harena[:, :, 0:1], 0.0)
        v.drain().then_inc(dve.h, 1)
        dve.inc()
    HZ_DVE = dve.n

    he_dve, w1mm, xq2sp = {}, {}, {}
    FF_SILU = [None]
    for j in range(NFT):
        t0 = j * TT
        lo = t0 - 1 if j else t0
        NN = 258 if 0 < j < NFT - 1 else 257
        hoff = 1 if j == 0 else 0

        for hc in range(HC):
            hg = HC * j + hc
            bank = hc % 2

            @Te
            def _(t, j=j, hc=hc, hg=hg, bank=bank, lo=lo, NN=NN):
                if hg == 0:
                    t.wait_ge(act.h, XQ1_ACT)
                    t.wait_ge(dF.h, WBF_L)
                if hg >= 2:
                    t.wait_ge(dve.h, he_dve[hg - 2])
                mm = None
                for kc in range(EC):
                    mm = t.matmul(ps_h[:, bank, 0:NN],
                                  wbufF[:, kc, hc * 128:(hc + 1) * 128],
                                  xq1FM[:, kc, lo:lo + NN],
                                  start=(kc == 0), stop=(kc == EC - 1))
                mm.then_inc(pe.h, 1)
                pe.inc()
            w1mm[hg] = pe.n

            @V
            def _(v, j=j, hc=hc, hg=hg, bank=bank, lo=lo, NN=NN, hoff=hoff):
                v.wait_ge(pe.h, w1mm[hg])
                if hc == 0:
                    v.drain()
                if hg == 0:
                    v.wait_ge(dve.h, max(S1BC_DVE, HZ_DVE))
                v.tensor_tensor(out=harena[:, hc, hoff:hoff + NN],
                                in0=ps_h[:, bank, 0:NN], in1=s1bc[:, lo:lo + NN],
                                op=A.mult).then_inc(dve.h, 1)
                dve.inc()
            he_dve[hg] = dve.n

        if j == NFT - 1:
            @V
            def _(v):
                v.drain()
                v.memset(harena[:, :, 257:258], 0.0)
                v.drain().then_inc(dve.h, 1)
                dve.inc()

        conv_done, silu_done = {}, {}
        prev_silu = FF_SILU[0]
        for grp in range(4):
            pv = prev_silu if grp == 0 else silu_done[grp - 1]

            @V
            def _(v, grp=grp, pv=pv):
                v.drain()
                if pv is not None:
                    v.wait_ge(act.h, pv)
                for i in range(8):
                    hc = 8 * grp + i
                    v.tensor_scalar(out=tmpa[:, i, :], in0=harena[:, hc, 0:TT],
                                    scalar1=cws[:, hc, 0:1], scalar2=None, op0=A.mult)
                v.drain()
                for i in range(8):
                    hc = 8 * grp + i
                    v.scalar_tensor_tensor(out=tmpa[:, i, :], in0=harena[:, hc, 1:1 + TT],
                                           scalar=cws[:, hc, 1:2], in1=tmpa[:, i, :],
                                           op0=A.mult, op1=A.add)
                v.drain()
                mm = None
                for i in range(8):
                    hc = 8 * grp + i
                    mm = v.scalar_tensor_tensor(out=tmpa[:, i, :],
                                                in0=harena[:, hc, 2:2 + TT],
                                                scalar=cws[:, hc, 2:3], in1=tmpa[:, i, :],
                                                op0=A.mult, op1=A.add)
                mm.then_inc(dve.h, 1)
                dve.inc()
            conv_done[grp] = dve.n

            @Sc
            def _(sc, j=j, grp=grp):
                sc.wait_ge(dve.h, conv_done[grp])
                if j == 0 and grp == 0:
                    sc.wait_ge(dP0.h, P0_LOADS)
                mm = None
                for i in range(8):
                    hc = 8 * grp + i
                    mm = sc.activation(out=harena[:, hc, 0:TT], in_=tmpa[:, i, :],
                                       func=AF.Silu, bias=cbs[:, hc:hc + 1])
                mm.then_inc(act.h, 1)
                act.inc()
            silu_done[grp] = act.n
        FF_SILU[0] = silu_done[3]

        @V
        def _(v, j=j):
            v.wait_ge(act.h, silu_done[3])
            v.tensor_reduce(out=mabs[:], in_=harena[:, :, 0:TT].rearrange("p h t -> p t h"),
                            axis=AX.X, op=A.max,
                            apply_absolute_value=True).then_inc(dve.h, 1)
            dve.inc()
        MABS_DVE = dve.n

        @Te
        def _(t, j=j, MABS_DVE=MABS_DVE):
            t.wait_ge(dve.h, MABS_DVE)
            mm = None
            for b in range(2):
                mm = t.transpose(ps_mx[:, b, :], mabs[:, b * 128:(b + 1) * 128], ident[:])
            mm.then_inc(pe.h, 1)
            pe.inc()
        MXT_PE = pe.n

        @V
        def _(v, j=j, MXT_PE=MXT_PE):
            v.wait_ge(pe.h, MXT_PE)
            if j == 0:
                v.wait_ge(dve2.h, P1_W2INV)
            for b in range(2):
                v.tensor_reduce(out=mx4[:, b:b + 1], in_=ps_mx[:, b, :], axis=AX.X,
                                op=A.max)
            v.drain()
            v.tensor_scalar(out=mx4[:, 0:2], in0=mx4[:, 0:2], scalar1=1e-5,
                            scalar2=None, op0=A.max)
            v.drain()
            v.reciprocal(out=mx4[:, 2:4], in_=mx4[:, 0:2])
            v.drain()
            v.tensor_scalar(out=mx4[:, 4:6], in0=mx4[:, 2:4], scalar1=127.0,
                            scalar2=None, op0=A.mult)
            for b in range(2):
                v.tensor_scalar(out=sg2tm[:, 2 * j + b:2 * j + b + 1],
                                in0=mx4[:, b:b + 1], scalar1=sw2[:, 0:1],
                                scalar2=1.0 / 127.0, op0=A.mult, op1=A.mult)
            v.drain().then_inc(dve.h, 1)
            dve.inc()
        MX_DVE = dve.n

        @Te
        def _(t, j=j, MX_DVE=MX_DVE):
            t.wait_ge(dve.h, MX_DVE)
            mm = None
            for b in range(2):
                mm = t.transpose(ps_bc[0:1, b * 128:(b + 1) * 128],
                                 mx4[:, 4 + b:5 + b], ident[:])
            mm.then_inc(pe.h, 1)
            pe.inc()
        S2T_PE = pe.n

        @V
        def _(v, j=j, S2T_PE=S2T_PE):
            v.wait_ge(pe.h, S2T_PE)
            v.tensor_copy(out=s2sb[0:1, :], in_=ps_bc[0:1, :]).then_inc(dve.h, 1)
            dve.inc()
        S2E_DVE = dve.n

        @Te
        def _(t, j=j, S2E_DVE=S2E_DVE):
            t.wait_ge(dve.h, S2E_DVE)
            t.matmul(ps_mx[:].rearrange("p a b -> p (a b)"), ones_row[0:1, :],
                     s2sb[0:1, :], start=True, stop=True).then_inc(pe.h, 1)
            pe.inc()
        S2B_PE = pe.n

        @V
        def _(v, j=j, t0=t0, S2B_PE=S2B_PE):
            v.wait_ge(pe.h, S2B_PE)
            v.tensor_copy(out=sgb2[:], in_=ps_mx[:].rearrange("p a b -> p (a b)"))
            v.drain()
            v.tensor_tensor(out=harena[:, :, 0:TT], in0=harena[:, :, 0:TT],
                            in1=sgb2[:].unsqueeze(1).broadcast_to([128, HC, TT]),
                            op=A.mult)
            v.drain()
            if j >= 1:
                v.wait_ge(dS.h, xq2sp[j - 1])
            v.tensor_scalar(out=xq2sb[:], in0=harena[:, :, 0:TT],
                            scalar1=magic[:, 0:1], scalar2=magic[:, 0:1],
                            op0=A.add, op1=A.subtract).then_inc(dve.h, 1)
            dve.inc()
        XQ2_DVE = dve.n

        @Sc
        def _(sc, j=j, t0=t0, XQ2_DVE=XQ2_DVE):
            sc.wait_ge(dve.h, XQ2_DVE)
            sc.wait_ge(dS.h, dS.n)
            sc.dma_start(out=xq2re[:, :, t0:t0 + TT], in_=xq2sb[:]).then_inc(dS.h, 16)
            dS.inc(16)
        xq2sp[j] = dS.n

    @Sc
    def _(sc):
        sc.wait_ge(dS.h, dS.n)
        sc.drain().then_inc(act.h, 1)
        act.inc()
    FF_ACT = act.n
    ff.close()
    if UPTO <= 5:
        wfs.close()
        fs.close()
        es.close()
        return nc

    wfs.close()
    fn = ExitStack()
    wbuf2 = fn.enter_context(nc.sbuf_tensor("wbuf2", [128, HC, E], BF16))
    xq2it = fn.enter_context(nc.sbuf_tensor("xq2it", [128, HC, 128], BF16))
    xq2it2 = fn.enter_context(nc.sbuf_tensor("xq2it2", [128, HC, 128], BF16))
    x1t = fn.enter_context(nc.sbuf_tensor("x1t", [128, E], F32))
    x1t2 = fn.enter_context(nc.sbuf_tensor("x1t2", [128, E], F32))
    outsb = fn.enter_context(nc.sbuf_tensor("outsb", [128, E], F32))
    resid2 = fn.enter_context(nc.sbuf_tensor("resid2", [128, E], F32))
    sq2 = fn.enter_context(nc.sbuf_tensor("sq2", [128, E], F32))
    ln2 = fn.enter_context(nc.sbuf_tensor("ln2", [128, 8], F32))
    ps_f = [fn.enter_context(nc.psum_tensor(f"ps_f{i}", [128, 2, 512], F32))
            for i in range(2)]

    w2re = w2qS.rearrange("kc p c -> p kc c")
    loads_for, w2chunk = {}, {}

    @Sy
    def _(sy):
        sy.wait_ge(act.h, FF_ACT)
        sy.wait_ge(dG.h, P1_W2Q)
        sy.wait_ge(dFf[0].h, dFf[0].n)
        for q in range(4):
            sy.dma_start(out=wbuf2[:, 8 * q:8 * q + 8, :],
                         in_=w2re[:, 8 * q:8 * q + 8, :]).then_inc(dFf[0].h, 16)
            dFf[0].inc(16)
            w2chunk[q] = dFf[0].n
        sy.dma_start(out=xq2it[:], in_=xq2re[:, :, 0:128]).then_inc(dFf[0].h, 16)
        sy.dma_start(out=x1t[:], in_=x1S[0]).then_inc(dFf[0].h, 16)
        dFf[0].inc(32)
    loads_for[0] = dFf[0].n

    w2pe, r2_dve, ysp = {}, {}, {}
    for it in range(NIT):
        ps = ps_f[it % 2]
        xq = xq2it if it % 2 == 0 else xq2it2
        x1b = x1t if it % 2 == 0 else x1t2

        if it + 1 < NIT:
            dfn = dFf[(it + 1) % 2]

            @Sy
            def _(sy, it=it, dfn=dfn):
                sy.wait_ge(dfn.h, dfn.n)
                if it >= 1:
                    sy.wait_ge(pe.h, w2pe[it - 1])
                    sy.wait_ge(dve.h, r2_dve[it - 1])
                xqn = xq2it2 if it % 2 == 0 else xq2it
                x1n = x1t2 if it % 2 == 0 else x1t
                sy.dma_start(out=xqn[:],
                             in_=xq2re[:, :, (it + 1) * 128:(it + 2) * 128]).then_inc(dfn.h, 16)
                sy.dma_start(out=x1n[:], in_=x1S[it + 1]).then_inc(dfn.h, 16)
                dfn.inc(32)
            loads_for[it + 1] = dfn.n

        @Te
        def _(t, it=it, ps=ps, xq=xq):
            t.wait_ge(dFf[it % 2].h, loads_for[it])
            if it >= 2:
                t.wait_ge(dve.h, r2_dve[it - 2])
            mm = None
            for kc in range(HC):
                for nh in range(2):
                    mm = t.matmul(ps[:, nh], xq[:, kc, :],
                                  wbuf2[:, kc, nh * 512:(nh + 1) * 512],
                                  start=(kc == 0), stop=(kc == HC - 1))
            mm.then_inc(pe.h, 1)
            pe.inc()
        w2pe[it] = pe.n

        @V
        def _(v, it=it, ps=ps, x1b=x1b):
            v.wait_ge(pe.h, w2pe[it])
            if it >= 1:
                v.wait_ge(dS.h, ysp[it - 1])
            v.scalar_tensor_tensor(out=resid2[:], in0=ps[:].rearrange("p a b -> p (a b)"),
                                   scalar=sg2tm[:, it:it + 1], in1=x1b[:],
                                   op0=A.mult, op1=A.add,
                                   accum_out=ln2[:, 0:1]).then_inc(dve.h, 1)
            dve.inc()
            r2_dve[it] = dve.n
            v.drain()
            v.tensor_tensor(out=sq2[:], in0=resid2[:], in1=resid2[:], op=A.mult)
            v.drain()
            v.tensor_reduce(out=ln2[:, 1:2], in_=sq2[:], axis=AX.X, op=A.add)
            v.drain()
            v.tensor_scalar(out=ln2[:, 0:2], in0=ln2[:, 0:2], scalar1=1.0 / E,
                            scalar2=None, op0=A.mult)
            v.drain()
            v.tensor_tensor(out=ln2[:, 2:3], in0=ln2[:, 0:1], in1=ln2[:, 0:1], op=A.mult)
            v.drain()
            v.tensor_tensor(out=ln2[:, 1:2], in0=ln2[:, 1:2], in1=ln2[:, 2:3], op=A.subtract)
            v.drain()
            v.tensor_scalar(out=ln2[:, 1:2], in0=ln2[:, 1:2], scalar1=1e-5,
                            scalar2=None, op0=A.add).then_inc(dve.h, 1)
            dve.inc()
        ST2_DVE = dve.n

        @Sc
        def _(sc, it=it, ST2_DVE=ST2_DVE):
            sc.wait_ge(dve.h, ST2_DVE)
            sc.activation(out=ln2[:, 2:3], in_=ln2[:, 1:2],
                          func=AF.Sqrt).then_inc(act.h, 1)
            act.inc()
        SQT2_ACT = act.n

        @V
        def _(v, it=it, SQT2_ACT=SQT2_ACT):
            v.wait_ge(act.h, SQT2_ACT)
            v.reciprocal(out=ln2[:, 3:4], in_=ln2[:, 2:3])
            v.drain()
            v.tensor_scalar(out=outsb[:], in0=resid2[:], scalar1=ln2[:, 0:1],
                            scalar2=ln2[:, 3:4], op0=A.subtract,
                            op1=A.mult).then_inc(dve.h, 1)
            dve.inc()
        OUT_DVE = dve.n

        @Sc
        def _(sc, it=it, OUT_DVE=OUT_DVE):
            sc.wait_ge(dve.h, OUT_DVE)
            sc.wait_ge(dS.h, dS.n)
            sc.dma_start(out=y_d.rearrange("(it p) e -> it p e", p=128)[it],
                         in_=outsb[:]).then_inc(dS.h, 16)
            dS.inc(16)
        ysp[it] = dS.n

    @Sc
    def _(sc):
        sc.wait_ge(dS.h, dS.n)
    fn.close()
    fs.close()

    es.close()
    return nc


_NC_CACHE = {}


def kernel(L0_attn_x, L0_attn_wq, L0_attn_bq, L0_attn_wk, L0_attn_bk,
           L0_attn_wv, L0_attn_bv, L0_attn_wo, L0_attn_bo,
           L0_ffn_w1, L0_ffn_conv_w, L0_ffn_conv_b, L0_ffn_w2):
    if "nc" not in _NC_CACHE:
        _NC_CACHE["nc"] = build()
    nc = _NC_CACHE["nc"]

    def c(a):
        return np.ascontiguousarray(np.asarray(a, dtype=np.float32))

    base = {
        "wqT": c(np.asarray(L0_attn_wq).T), "wkT": c(np.asarray(L0_attn_wk).T),
        "wvT": c(np.asarray(L0_attn_wv).T), "woT": c(np.asarray(L0_attn_wo).T),
        "bqP": c(np.asarray(L0_attn_bq).reshape(EC, 128).T),
        "bkP": c(np.asarray(L0_attn_bk).reshape(EC, 128).T),
        "bv": c(L0_attn_bv), "bo": c(L0_attn_bo),
        "w1T": c(np.asarray(L0_ffn_w1).T),
        "cw": c(np.asarray(L0_ffn_conv_w).reshape(HC, 128, 3).transpose(1, 0, 2)),
        "cb": c(np.asarray(L0_ffn_conv_b).reshape(HC, 128).T),
        "w2T": c(np.asarray(L0_ffn_w2).T),
    }
    import jax

    @jax.jit
    def _pbf(dist):
        diff = dist.astype("float32")
        import jax.numpy as jnp
        return jnp.exp(-0.5 * (diff / 2.0) ** 2) + 1.0 * jnp.cos(2.0 * jnp.pi * diff / 401.0)

    def pb32(d_int):
        with jax.default_device(jax.devices("cpu")[0]):
            return np.asarray(jax.device_get(_pbf(np.asarray(d_int, dtype=np.int32))))

    ii = np.arange(128)
    base["pbB"] = c(pb32(np.arange(W)[None, :] - 32 - ii[:, None]))
    base["pbC"] = c(pb32(128 * np.arange(NIT)[None, :] + ii[:, None]))
    base["pbR"] = c(np.tile(pb32(-np.arange(T))[None, :], (16, 1)))

    xs = np.asarray(L0_attn_x, dtype=np.float32)
    in_maps = [dict(base, x=c(xs[b])) for b in range(8)]
    res = run_bass_kernel_spmd(nc, in_maps, core_ids=list(range(8)))
    out = np.stack([res.results[b]["y"] for b in range(8)], axis=0)
    return out.astype(np.float32)


# version 09
# speedup vs baseline: 1.8534x; 1.0108x over previous
import numpy as np
from contextlib import ExitStack

import concourse.bass as bass
import concourse.mybir as mybir
from concourse.bass_utils import run_bass_kernel_spmd

F32 = mybir.dt.float32
BF16 = mybir.dt.bfloat16
A = mybir.AluOpType
AF = mybir.ActivationFunctionType
AX = mybir.AxisListType

T, E, H, D, HID = 2048, 1024, 16, 64, 4096
EC, HC, NIT = 8, 32, 16
TT, NFT = 256, 8
W, WP = 192, 96
C52 = 1.5 * 2.0 ** 52
C23 = 1.5 * 2.0 ** 23
PI = float(np.pi)


class Sem:
    def __init__(self, h):
        self.h = h
        self.n = 0

    def inc(self, k=1):
        self.n += k
        return self.n


def cut512(lo, hi):
    out = []
    while lo < hi:
        nxt = min(hi, (lo // 512 + 1) * 512)
        out.append((lo, nxt))
        lo = nxt
    return out


UPTO = 99
A2_NG = 4
A2_DEPTH = 9
A2_COL0 = True
A2_EXP = True
A2_ROW0 = True
A2_TAIL = True


def build():
    nc = bass.Bass("TRN2", target_bir_lowering=False)
    es = ExitStack()

    x_d = nc.dram_tensor("x", [T, E], F32, kind="ExternalInput").ap()
    wT_d = {w: nc.dram_tensor(w, [E, E], F32, kind="ExternalInput").ap()
            for w in ("wqT", "wkT", "wvT", "woT")}
    bqk_d = {b: nc.dram_tensor(b, [128, EC], F32, kind="ExternalInput").ap()
             for b in ("bqP", "bkP")}
    b_d = {b: nc.dram_tensor(b, [E], F32, kind="ExternalInput").ap()
           for b in ("bv", "bo")}
    w1T_d = nc.dram_tensor("w1T", [E, HID], F32, kind="ExternalInput").ap()
    cw_d = nc.dram_tensor("cw", [128, HC, 3], F32, kind="ExternalInput").ap()
    cb_d = nc.dram_tensor("cb", [128, HC], F32, kind="ExternalInput").ap()
    w2T_d = nc.dram_tensor("w2T", [HID, E], F32, kind="ExternalInput").ap()
    pbB_d = nc.dram_tensor("pbB", [128, W], F32, kind="ExternalInput").ap()
    pbC_d = nc.dram_tensor("pbC", [128, NIT], F32, kind="ExternalInput").ap()
    pbR_d = nc.dram_tensor("pbR", [16, T], F32, kind="ExternalInput").ap()
    y_d = nc.dram_tensor("y", [T, E], F32, kind="ExternalOutput").ap()

    qS = nc.dram_tensor("qS", [EC, 128, T], F32, kind="Internal").ap()
    kS = nc.dram_tensor("kS", [EC, 128, T], F32, kind="Internal").ap()
    vS = nc.dram_tensor("vS", [NIT, 128, E], F32, kind="Internal").ap()
    w1qS = nc.dram_tensor("w1qS", [EC, 128, HID], BF16, kind="Internal").ap()
    w2qS = nc.dram_tensor("w2qS", [HC, 128, E], BF16, kind="Internal").ap()
    x1S = nc.dram_tensor("x1S", [NIT, 128, E], F32, kind="Internal").ap()
    xq2S = nc.dram_tensor("xq2S", [HC, 128, T], BF16, kind="Internal").ap()

    ident = es.enter_context(nc.sbuf_tensor("ident", [128, 128], F32))
    identb = es.enter_context(nc.sbuf_tensor("identb", [128, 128], BF16))
    ones_col = es.enter_context(nc.sbuf_tensor("ones_col", [128, 1], F32))
    ones_row = es.enter_context(nc.sbuf_tensor("ones_row", [1, 128], F32))
    zrow = es.enter_context(nc.sbuf_tensor("zrow", [1, 512], F32))
    magic = es.enter_context(nc.sbuf_tensor("magic", [128, 1], F32))
    negmagic = es.enter_context(nc.sbuf_tensor("negmagic", [128, 1], F32))
    eB = es.enter_context(nc.sbuf_tensor("eB", [128, 3, W], F32))
    efc = es.enter_context(nc.sbuf_tensor("efc", [128, NIT], F32))
    bqs = es.enter_context(nc.sbuf_tensor("bqs", [128, EC], F32))
    bks = es.enter_context(nc.sbuf_tensor("bks", [128, EC], F32))
    bob = es.enter_context(nc.sbuf_tensor("bob", [128, E], F32))
    cws = es.enter_context(nc.sbuf_tensor("cws", [128, HC, 3], F32))
    cbs = es.enter_context(nc.sbuf_tensor("cbs", [128, HC], F32))
    sw1 = es.enter_context(nc.sbuf_tensor("sw1", [128, 1], F32))
    sw2 = es.enter_context(nc.sbuf_tensor("sw2", [128, 1], F32))
    S_all = es.enter_context(nc.sbuf_tensor("S_all", [128, NIT, H], F32))
    pc_st = es.enter_context(nc.sbuf_tensor("pc_st", [128, NIT, H], F32))
    sg1tm = es.enter_context(nc.sbuf_tensor("sg1tm", [128, NIT], F32))
    sg2tm = es.enter_context(nc.sbuf_tensor("sg2tm", [128, NIT], F32))
    p0 = ExitStack()
    brow = p0.enter_context(nc.sbuf_tensor("brow", [1, 2, E], F32))

    sems = {}
    for name in ("dL", "dS", "dG", "pe", "dve", "act", "gp", "pe2", "dve2", "gpq",
                 "fv", "fa", "dP0", "dX", "dW", "dA2", "dCt0", "dCt1", "dWo",
                 "dF", "dFf0", "dFf1"):
        sems[name] = Sem(es.enter_context(nc.semaphore(name)))
    dL, dS, dG = sems["dL"], sems["dS"], sems["dG"]
    dP0, dX, dW, dA2 = sems["dP0"], sems["dX"], sems["dW"], sems["dA2"]
    dCt = [sems["dCt0"], sems["dCt1"]]
    dWo, dF = sems["dWo"], sems["dF"]
    dFf = [sems["dFf0"], sems["dFf1"]]
    pe, dve, act, gp = sems["pe"], sems["dve"], sems["act"], sems["gp"]
    pe2, dve2, gpq = sems["pe2"], sems["dve2"], sems["gpq"]
    fv, fa = sems["fv"], sems["fa"]

    block = es.enter_context(nc.Block())
    V, Sc, Te, G, Sy = block.vector, block.scalar, block.tensor, block.gpsimd, block.sync

    @Sy
    def _(sy):
        sy.dma_start(out=bqs[:], in_=bqk_d["bqP"][:, :]).then_inc(dP0.h, 16)
        sy.dma_start(out=bks[:], in_=bqk_d["bkP"][:, :]).then_inc(dP0.h, 16)
        sy.dma_start(out=brow[0:1, 0, :], in_=b_d["bv"][:]).then_inc(dP0.h, 16)
        sy.dma_start(out=brow[0:1, 1, :], in_=b_d["bo"][:]).then_inc(dP0.h, 16)
        sy.dma_start(out=cws[:], in_=cw_d[:, :, :]).then_inc(dP0.h, 16)
        sy.dma_start(out=cbs[:], in_=cb_d[:, :]).then_inc(dP0.h, 16)
        sy.dma_start(out=eB[:, 0, :], in_=pbB_d[:, :]).then_inc(dP0.h, 16)
        sy.dma_start(out=efc[:], in_=pbC_d[:, :]).then_inc(dP0.h, 16)
        dP0.inc(16 * 8)
    P0_LOADS = dP0.n

    @G
    def _(g):
        g.iota(ident[:], [[-1, 128]], base=0, channel_multiplier=1,
               allow_small_or_imprecise_dtypes=True)
        g.drain().then_inc(gp.h, 1)
        gp.inc()
    IOTA_GP = gp.n

    @V
    def _(v):
        v.memset(magic[:], 0.75)
        v.drain()
        v.tensor_scalar(out=magic[:], in0=magic[:], scalar1=C52, scalar2=C52,
                        op0=A.add, op1=A.subtract)
        v.drain()
        v.tensor_scalar(out=magic[:], in0=magic[:], scalar1=(C52 - C23), scalar2=C23,
                        op0=A.mult, op1=A.add)
        v.drain()
        v.tensor_scalar(out=negmagic[:], in0=magic[:], scalar1=-1.0, scalar2=None,
                        op0=A.mult)
        v.memset(ones_col[:], 1.0)
        v.memset(ones_row[:], 1.0)
        v.memset(zrow[:], 0.0)
        v.drain()
        v.wait_ge(gp.h, IOTA_GP)
        v.tensor_single_scalar(out=ident[:], in_=ident[:], scalar=0.0, op=A.is_equal)
        v.drain()
        v.tensor_copy(out=identb[:], in_=ident[:])
        v.drain()
        v.engine_nop().then_inc(dve.h, 1)
        dve.inc()
    MAGIC_DVE = dve.n

    c_scr = p0.enter_context(nc.sbuf_tensor("c_scr", [128, T], F32))

    @Sc
    def _(sc):
        sc.wait_ge(dP0.h, P0_LOADS)
        sc.activation(out=eB[:, 0, :], in_=eB[:, 0, :], func=AF.Exp)
        sc.drain()
        sc.activation(out=efc[:], in_=efc[:], func=AF.Exp)
        sc.drain()
        sc.drain().then_inc(fa.h, 1)
    fa.inc()
    FPIPE_FA = fa.n

    @G
    def _(g):
        g.wait_ge(fa.h, FPIPE_FA)
        g.iota(c_scr[:, 0:W], [[1, W]], base=-32, channel_multiplier=-1,
               allow_small_or_imprecise_dtypes=True)
        g.drain().then_inc(gp.h, 1)
        gp.inc()
    D2_GP = gp.n

    @V
    def _(v):
        v.wait_ge(gp.h, D2_GP)
        v.tensor_tensor(out=c_scr[:, 0:W], in0=c_scr[:, 0:W], in1=c_scr[:, 0:W], op=A.mult)
        v.drain()
        v.tensor_single_scalar(out=c_scr[:, 0:W], in_=c_scr[:, 0:W], scalar=4.5, op=A.is_le)
        v.drain()
        v.tensor_tensor(out=eB[:, 0, :], in0=eB[:, 0, :], in1=c_scr[:, 0:W], op=A.mult)
        v.drain()
        v.tensor_copy(out=eB[:, 1, :], in_=eB[:, 0, :])
        v.drain()
        v.memset(eB[:, 1, 0:32], 0.0)
        v.drain()
        v.tensor_copy(out=eB[:, 1, 32:33], in_=efc[:, 0:1])
        v.drain()
        v.memset(eB[0:1, 1, :], 0.0)
        v.drain()
        v.tensor_copy(out=eB[:, 2, :], in_=eB[:, 0, :])
        v.drain()
        v.memset(eB[:, 2, 160:192], 0.0)
        v.drain()
        v.memset(efc[:, 0:1], 0.0)
        v.drain()
        v.engine_nop().then_inc(dve.h, 1)
        dve.inc()
    CONSTS_DVE = dve.n


    p0.close()
    if UPTO <= 1:
        es.close()
        return nc

    a1 = ExitStack()
    xhi = a1.enter_context(nc.sbuf_tensor("xhi", [128, EC, T], BF16))
    xlo = a1.enter_context(nc.sbuf_tensor("xlo", [128, EC, T], BF16))
    xtile = [a1.enter_context(nc.sbuf_tensor(f"xtile{i}", [128, E], F32)) for i in range(2)]
    wslF = [a1.enter_context(nc.sbuf_tensor(f"wslF{i}", [128, EC, 128], F32)) for i in range(2)]
    wslH = [a1.enter_context(nc.sbuf_tensor(f"wslH{i}", [128, EC, 128], BF16)) for i in range(2)]
    wslL = [a1.enter_context(nc.sbuf_tensor(f"wslL{i}", [128, EC, 128], BF16)) for i in range(2)]
    wv_h = a1.enter_context(nc.sbuf_tensor("wv_h", [128, EC, E], BF16))
    wv_l = a1.enter_context(nc.sbuf_tensor("wv_l", [128, EC, E], BF16))
    qsl = a1.enter_context(nc.sbuf_tensor("qsl", [128, T], F32))
    vtsl_s = a1.enter_context(nc.sbuf_tensor("vtsl", [128, E], F32))
    vtsl = [vtsl_s, vtsl_s]
    wqbuf = [a1.enter_context(nc.sbuf_tensor(f"wqbuf{i}", [128, 1024], F32)) for i in range(2)]
    wqrnd = [a1.enter_context(nc.sbuf_tensor(f"wqrnd{i}", [128, 1024], F32)) for i in range(2)]
    wqout = [a1.enter_context(nc.sbuf_tensor(f"wqout{i}", [128, 1024], BF16)) for i in range(2)]
    wqpart = a1.enter_context(nc.sbuf_tensor("wqpart", [128, 64], F32))
    wqsum = a1.enter_context(nc.sbuf_tensor("wqsum", [128, 4], F32))
    bvb = a1.enter_context(nc.sbuf_tensor("bvb", [128, E], F32))
    ps_w = a1.enter_context(nc.psum_tensor("ps_w", [128, 64], F32))
    ps_q = a1.enter_context(nc.psum_tensor("ps_q", [128, 2, 512], F32))
    axs = ExitStack()
    ps_x = [axs.enter_context(nc.psum_tensor(f"ps_x{i}", [128, 2, 512], F32)) for i in range(2)]

    with nc.psum_tensor("ps_b", [128, 512], F32) as ps_b:
        for (bi, dst) in ((0, bvb), (1, bob)):
            for half in range(2):
                @Te
                def _(t, bi=bi, half=half):
                    t.wait_ge(dP0.h, P0_LOADS)
                    t.wait_ge(dve.h, MAGIC_DVE)
                    t.wait_ge(act.h, act.n)
                    t.matmul(ps_b[:, 0:512], ones_row[0:1, :],
                             brow[0:1, bi, half * 512:(half + 1) * 512],
                             start=True, stop=True)
                    t.drain().then_inc(pe.h, 1)
                    pe.inc()

                @Sc
                def _(sc, dst=dst, half=half):
                    sc.wait_ge(pe.h, pe.n)
                    sc.copy(out=dst[:, half * 512:(half + 1) * 512], in_=ps_b[:, 0:512])
                    sc.drain().then_inc(act.h, 1)
                    act.inc()
    BCAST_ACT = act.n

    w1re_d = w1T_d.rearrange("(kc p) c -> kc p c", p=128)
    w2re_d = w2T_d.rearrange("(kc p) c -> kc p c", p=128)
    p1f = {"s": 0, "buf": [None, None], "rnd": [None, None], "out": [None, None]}
    p1_jobs = []

    def p1_pass1(w_re, kc, cc, si):
        def job():
            s = p1f["s"]; p1f["s"] += 1
            bi = s % 2
            free = p1f["buf"][bi]

            @G
            def _(g):
                if free is not None:
                    g.wait_ge(*free)
                g.wait_ge(dG.h, dG.n)
                g.dma_start(out=wqbuf[bi][:],
                            in_=w_re[kc, :, cc * 1024:(cc + 1) * 1024]).then_inc(dG.h, 16)
                dG.inc(16)
            LD = dG.n

            @V
            def _(v):
                v.wait_ge(dG.h, LD)
                v.tensor_reduce(out=wqpart[:, si:si + 1], in_=wqbuf[bi][:], axis=AX.X,
                                op=A.add, apply_absolute_value=True).then_inc(dve2.h, 1)
                dve2.inc()
            p1f["buf"][bi] = (dve2.h, dve2.n)
        return job

    def p1_fin(si0, ns, n_elems, mcol, sumpos, invcol, sw_dst, key):
        def job():
            @Te
            def _(t):
                t.wait_ge(dve2.h, dve2.n)
                t.wait_ge(dve.h, MAGIC_DVE)
                t.matmul(ps_w[0:1, si0:si0 + ns], ones_col[:, 0:1], wqpart[:, si0:si0 + ns],
                         start=True, stop=True).then_inc(pe2.h, 1)
                pe2.inc()

            @V
            def _(v):
                v.wait_ge(pe2.h, pe2.n)
                v.tensor_reduce(out=wqsum[0:1, sumpos:sumpos + 1],
                                in_=ps_w[0:1, si0:si0 + ns], axis=AX.X, op=A.add)
                v.drain()
                v.tensor_scalar(out=wqsum[0:1, sumpos:sumpos + 1],
                                in0=wqsum[0:1, sumpos:sumpos + 1],
                                scalar1=1.0 / n_elems, scalar2=1e-5,
                                op0=A.mult, op1=A.max)
                v.drain().then_inc(dve2.h, 1)
                dve2.inc()

            @Te
            def _(t):
                t.wait_ge(dve2.h, dve2.n)
                t.matmul(ps_w[:, mcol:mcol + 1], ones_row[0:1, :],
                         wqsum[0:1, sumpos:sumpos + 1],
                         start=True, stop=True).then_inc(pe2.h, 1)
                pe2.inc()

            @V
            def _(v):
                v.wait_ge(pe2.h, pe2.n)
                v.tensor_copy(out=sw_dst[:], in_=ps_w[:, mcol:mcol + 1])
                v.drain()
                v.reciprocal(out=wqsum[:, invcol:invcol + 1], in_=sw_dst[:])
                v.drain().then_inc(dve2.h, 1)
                dve2.inc()
            p1f[key] = dve2.n
        return job

    def p1_pass2(w_re, scr_dd, kc, cc, invcol, last_key=None):
        def job():
            s = p1f["s"]; p1f["s"] += 1
            bi = s % 2
            bfree, rfree, ofree = p1f["buf"][bi], p1f["rnd"][bi], p1f["out"][bi]

            @G
            def _(g):
                if bfree is not None:
                    g.wait_ge(*bfree)
                g.wait_ge(dG.h, dG.n)
                g.dma_start(out=wqbuf[bi][:],
                            in_=w_re[kc, :, cc * 1024:(cc + 1) * 1024]).then_inc(dG.h, 16)
                dG.inc(16)
            LD = dG.n

            @V
            def _(v):
                v.wait_ge(dG.h, LD)
                if rfree is not None:
                    v.wait_ge(*rfree)
                v.tensor_scalar(out=wqrnd[bi][:], in0=wqbuf[bi][:],
                                scalar1=wqsum[:, invcol:invcol + 1], scalar2=magic[:, 0:1],
                                op0=A.mult, op1=A.add).then_inc(dve2.h, 1)
                dve2.inc()
            T1 = dve2.n

            @Sc
            def _(sc):
                sc.wait_ge(dve2.h, T1)
                sc.add(out=wqbuf[bi][:], in_=wqrnd[bi][:],
                       add=negmagic[:, 0:1]).then_inc(gpq.h, 1)
                gpq.inc()
            RND = gpq.n
            p1f["rnd"][bi] = (gpq.h, RND)

            @V
            def _(v):
                v.wait_ge(gpq.h, RND)
                if ofree is not None:
                    v.wait_ge(*ofree)
                v.tensor_scalar(out=wqout[bi][:], in0=wqbuf[bi][:], scalar1=-1.0,
                                scalar2=1.0, op0=A.max, op1=A.min).then_inc(dve2.h, 1)
                dve2.inc()
            CL = dve2.n
            p1f["buf"][bi] = (dve2.h, CL)

            @G
            def _(g):
                g.wait_ge(dve2.h, CL)
                g.wait_ge(dG.h, dG.n)
                g.dma_start(out=scr_dd[kc, :, cc * 1024:(cc + 1) * 1024],
                            in_=wqout[bi][:]).then_inc(dG.h, 16)
                dG.inc(16)
            p1f["out"][bi] = (dG.h, dG.n)
            if last_key:
                p1f[last_key] = dG.n
        return job

    for kc in range(EC):
        for cc in range(4):
            p1_jobs.append(p1_pass1(w1re_d, kc, cc, kc * 4 + cc))
    p1_jobs.append(p1_fin(0, 32, E * HID, 0, 0, 1, sw1, "w1inv"))
    for kc in range(EC):
        for cc in range(4):
            p1_jobs.append(p1_pass2(w1re_d, w1qS, kc, cc, 1,
                                    "w1q" if (kc == EC - 1 and cc == 3) else None))
    for kc in range(HC):
        p1_jobs.append(p1_pass1(w2re_d, kc, 0, 32 + kc))
    p1_jobs.append(p1_fin(32, 32, HID * E, 1, 2, 3, sw2, "w2inv"))
    for kc in range(HC):
        p1_jobs.append(p1_pass2(w2re_d, w2qS, kc, 0, 3,
                                "w2q" if kc == HC - 1 else None))

    def p1_step(k=1):
        for _ in range(k):
            if p1_jobs:
                p1_jobs.pop(0)()

    tr_done, sp_done = {}, {}
    for it in range(NIT):
        bi = it % 2

        @Sy
        def _(sy, it=it, bi=bi):
            sy.wait_ge(dX.h, dX.n)
            if it >= 2:
                sy.wait_ge(pe.h, tr_done[it - 2])
            sy.dma_start(out=xtile[bi][:],
                         in_=x_d.rearrange("(it p) e -> it p e", p=128)[it]).then_inc(dX.h, 16)
            dX.inc(16)
        XT_L = dX.n

        @Te
        def _(t, it=it, bi=bi, XT_L=XT_L):
            t.wait_ge(dX.h, XT_L)
            if it == 0:
                t.wait_ge(dve.h, CONSTS_DVE)
                t.wait_ge(act.h, BCAST_ACT)
            if it >= 2:
                t.wait_ge(dve.h, sp_done[it - 2])
            mm = None
            for ec in range(EC):
                mm = t.transpose(ps_x[bi][:, ec // 4, (ec % 4) * 128:(ec % 4) * 128 + 128],
                                 xtile[bi][:, ec * 128:(ec + 1) * 128], ident[:])
            mm.then_inc(pe.h, 1)
            pe.inc()
            tr_done[it] = pe.n

        @V
        def _(v, it=it, bi=bi):
            v.wait_ge(pe.h, tr_done[it])
            for h in range(2):
                v.tensor_copy(out=xhi[:, 4 * h:4 * h + 4, it * 128:(it + 1) * 128],
                              in_=ps_x[bi][:, h].rearrange("p (a b) -> p a b", a=4))
            v.drain()
            for h in range(2):
                mm = v.tensor_tensor(out=xlo[:, 4 * h:4 * h + 4, it * 128:(it + 1) * 128],
                                     in0=ps_x[bi][:, h].rearrange("p (a b) -> p a b", a=4),
                                     in1=xhi[:, 4 * h:4 * h + 4, it * 128:(it + 1) * 128],
                                     op=A.subtract)
            mm.then_inc(dve.h, 1)
            dve.inc()
            sp_done[it] = dve.n
        p1_step(4)
    SPLIT_DVE = dve.n

    wslF_free = {0: None, 1: None}
    wslHL_free = {0: None, 1: None}
    psq_evac = {0: None, 1: None}
    qsl_free = [None]

    def proj_qk(wkey, bias_sb, spill):
        w_re = wT_d[wkey].rearrange("(kc p) o -> p kc o", p=128)
        wsplit, mm_done, evac_done, spill_dS = {}, {}, {}, {}
        for oc in range(EC):
            bi = oc % 2
            ffre, hfre = wslF_free[bi], wslHL_free[bi]

            @Sy
            def _(sy, oc=oc, bi=bi, ffre=ffre):
                sy.wait_ge(dW.h, dW.n)
                if ffre is not None:
                    sy.wait_ge(dve.h, ffre)
                sy.dma_start(out=wslF[bi][:],
                             in_=w_re[:, :, oc * 128:(oc + 1) * 128]).then_inc(dW.h, 16)
                dW.inc(16)
            WL = dW.n

            @V
            def _(v, oc=oc, bi=bi, WL=WL, hfre=hfre):
                v.wait_ge(dW.h, WL)
                if hfre is not None:
                    v.wait_ge(pe.h, hfre)
                v.tensor_copy(out=wslH[bi][:], in_=wslF[bi][:])
                v.drain()
                v.tensor_tensor(out=wslL[bi][:], in0=wslF[bi][:], in1=wslH[bi][:],
                                op=A.subtract).then_inc(dve.h, 1)
                dve.inc()
            wsplit[oc] = dve.n
            wslF_free[bi] = dve.n

            act_base = act.n
            for tt in range(4):
                g = 4 * oc + tt
                bfree = psq_evac[tt % 2]

                @Te
                def _(t, oc=oc, tt=tt, g=g, bi=bi, act_base=act_base, bfree=bfree):
                    if tt == 0:
                        t.wait_ge(dve.h, wsplit[oc])
                    if tt >= 2:
                        t.wait_ge(act.h, act_base + tt - 1)
                    elif bfree is not None:
                        t.wait_ge(act.h, bfree)
                    first = True
                    mm = None
                    for (aw, bx) in ((wslH[bi], xhi), (wslH[bi], xlo), (wslL[bi], xhi)):
                        for kc in range(EC):
                            mm = t.matmul(ps_q[:, tt % 2], aw[:, kc, :],
                                          bx[:, kc, tt * 512:(tt + 1) * 512],
                                          start=first,
                                          stop=(aw is wslL[bi] and kc == EC - 1))
                            first = False
                    mm.then_inc(pe.h, 1)
                    pe.inc()
                mm_done[g] = pe.n
            wslHL_free[bi] = mm_done[4 * oc + 3]

            qfree = qsl_free[0]

            @Sc
            def _(sc, oc=oc, qfree=qfree):
                if oc == 0:
                    sc.wait_ge(dP0.h, P0_LOADS)
                if qfree is not None:
                    sc.wait_ge(dS.h, qfree)
                for tt in range(4):
                    g = 4 * oc + tt
                    sc.wait_ge(pe.h, mm_done[g])
                    sc.add(out=qsl[:, tt * 512:(tt + 1) * 512], in_=ps_q[:, tt % 2],
                           add=bias_sb[:, oc:oc + 1]).then_inc(act.h, 1)
                    act.inc()
                    evac_done[g] = act.n
                sc.drain()
                sc.wait_ge(dS.h, dS.n)
                sc.dma_start(out=spill[oc], in_=qsl[:]).then_inc(dS.h, 16)
                dS.inc(16)
            spill_dS[oc] = dS.n
            qsl_free[0] = dS.n
            psq_evac[0] = evac_done[4 * oc + 2]
            psq_evac[1] = evac_done[4 * oc + 3]
            p1_step(4)

    proj_qk("wqT", bqs, [qS[oc] for oc in range(EC)])
    proj_qk("wkT", bks, [kS[oc] for oc in range(EC)])

    wv_re = wT_d["wvT"].rearrange("(kc p) o -> p kc o", p=128)
    wv_done = {}
    for oc in range(EC):
        bi = oc % 2

        ffre = wslF_free[bi]

        @Sy
        def _(sy, oc=oc, bi=bi, ffre=ffre):
            sy.wait_ge(dW.h, dW.n)
            if ffre is not None:
                sy.wait_ge(dve.h, ffre)
            sy.dma_start(out=wslF[bi][:],
                         in_=wv_re[:, :, oc * 128:(oc + 1) * 128]).then_inc(dW.h, 16)
            dW.inc(16)
        WL = dW.n

        @V
        def _(v, oc=oc, bi=bi, WL=WL):
            v.wait_ge(dW.h, WL)
            v.tensor_copy(out=wv_h[:, :, oc * 128:(oc + 1) * 128], in_=wslF[bi][:])
            v.drain()
            v.tensor_tensor(out=wv_l[:, :, oc * 128:(oc + 1) * 128], in0=wslF[bi][:],
                            in1=wv_h[:, :, oc * 128:(oc + 1) * 128],
                            op=A.subtract).then_inc(dve.h, 1)
            dve.inc()
        wv_done[oc] = dve.n
        wslF_free[bi] = dve.n
        p1_step(2)
    WV_DVE = dve.n

    axs.close()
    ps_v = [a1.enter_context(nc.psum_tensor(f"ps_v{i}", [128, 2, 512], F32)) for i in range(2)]

    vt_mm, vt_evac, vspill = {}, {}, {}
    for it in range(NIT):
        bi = it % 2

        @Te
        def _(t, it=it, bi=bi):
            if it == 0:
                t.wait_ge(dve.h, WV_DVE)
            if it >= 2:
                t.wait_ge(dve.h, vt_evac[it - 2])
            mm = None
            for nh in range(2):
                first = True
                for (ax, bw) in ((xhi, wv_h), (xhi, wv_l), (xlo, wv_h)):
                    for kc in range(EC):
                        mm = t.matmul(ps_v[bi][:, nh], ax[:, kc, it * 128:(it + 1) * 128],
                                      bw[:, kc, nh * 512:(nh + 1) * 512],
                                      start=first, stop=(ax is xlo and kc == EC - 1))
                        first = False
            mm.then_inc(pe.h, 1)
            pe.inc()
        vt_mm[it] = pe.n

        @V
        def _(v, it=it, bi=bi):
            v.wait_ge(pe.h, vt_mm[it])
            if it == 0:
                v.wait_ge(act.h, BCAST_ACT)
            if it >= 1:
                v.wait_ge(dS.h, vspill[it - 1])
            v.tensor_add(out=vtsl[bi][:], in0=ps_v[bi][:].rearrange("p a b -> p (a b)"),
                         in1=bvb[:]).then_inc(dve.h, 1)
            dve.inc()
        vt_evac[it] = dve.n

        @Sc
        def _(sc, it=it, bi=bi):
            sc.wait_ge(dve.h, vt_evac[it])
            sc.wait_ge(dS.h, dS.n)
            sc.dma_start(out=vS[it], in_=vtsl[bi][:]).then_inc(dS.h, 16)
            dS.inc(16)
        vspill[it] = dS.n
        p1_step(2)

    p1_step(len(p1_jobs))
    assert not p1_jobs
    P1_W1INV, P1_W2INV = p1f["w1inv"], p1f["w2inv"]
    P1_W1Q, P1_W2Q = p1f["w1q"], p1f["w2q"]

    @Sc
    def _(sc):
        sc.wait_ge(dS.h, dS.n)
        sc.wait_ge(dG.h, P1_W2Q)
        sc.drain().then_inc(act.h, 1)
        act.inc()
    A1_DONE_ACT = act.n
    a1.close()
    fs = ExitStack()
    ctxflat = fs.enter_context(nc.sbuf_tensor("ctxflat", [128, 2 * EC * T], BF16))
    cxAll = ctxflat.rearrange("p (d e t) -> p d e t", d=2, e=EC)
    wbufF = ctxflat.rearrange("p (kc c) -> p kc c", kc=EC)
    s2row = fs.enter_context(nc.sbuf_tensor("s2row", [1, T], F32))
    if UPTO <= 2:
        fs.close()
        es.close()
        return nc

    a2 = ExitStack()
    efrow = a2.enter_context(nc.sbuf_tensor("efrow", [16, T], F32))
    qzA = a2.enter_context(nc.sbuf_tensor("qzA", [128, T], F32))
    qzB = a2.enter_context(nc.sbuf_tensor("qzB", [128, T], F32))
    kg = a2.enter_context(nc.sbuf_tensor("kg", [128, 2, T + 64], F32))
    gzA = a2.enter_context(nc.sbuf_tensor("gzA", [128, NIT, 128], F32))
    gzB = a2.enter_context(nc.sbuf_tensor("gzB", [128, NIT, 128], F32))
    pbuf = a2.enter_context(nc.sbuf_tensor("pbuf", [128, 3, 2, WP + 196], F32))
    pT = a2.enter_context(nc.sbuf_tensor("pT", [128, 2, 3, W], F32))
    Rt = a2.enter_context(nc.sbuf_tensor("Rt", [128, 2], F32))
    e0sb = a2.enter_context(nc.sbuf_tensor("e0sb", [2, 4, 512], F32))
    p0T = a2.enter_context(nc.sbuf_tensor("p0T", [128, NIT, 2], F32))
    S0sb = a2.enter_context(nc.sbuf_tensor("S0sb", [2, 2], F32))
    iszsc = a2.enter_context(nc.sbuf_tensor("iszsc", [128, NIT, 2], F32))
    q0sel = a2.enter_context(nc.sbuf_tensor("q0sel", [128, 2], F32))
    pcI = a2.enter_context(nc.sbuf_tensor("pcI", [1, 2 * T], F32))
    izI = a2.enter_context(nc.sbuf_tensor("izI", [1, 2 * T], F32))
    mvzI = a2.enter_context(nc.sbuf_tensor("mvzI", [1, 2, 128], F32))
    c0fix = a2.enter_context(nc.sbuf_tensor("c0fix", [128, 3], F32))

    @Sy
    def _(sy):
        sy.wait_ge(act.h, A1_DONE_ACT)
        sy.wait_ge(dA2.h, dA2.n)
        sy.dma_start(out=efrow[:], in_=pbR_d[:, :]).then_inc(dA2.h, 16)
        dA2.inc(16)
    EFROW_L = dA2.n

    @Sc
    def _(sc):
        sc.wait_ge(dA2.h, EFROW_L)
        sc.activation(out=efrow[:], in_=efrow[:], func=AF.Exp).then_inc(act.h, 1)
        act.inc()
    EFROW_ACT = act.n

    @V
    def _(v):
        v.wait_ge(act.h, A1_DONE_ACT)
        v.memset(kg[:, :, 0:32], 0.0)
        v.memset(kg[:, :, T + 32:T + 64], 0.0)
        v.memset(qzA[64:128, :], 0.0)
        v.memset(qzB[0:64, :], 0.0)
        v.memset(gzA[:], 0.0)
        v.memset(gzB[:], 0.0)
        v.memset(pbuf[:], 0.0)
        v.memset(pT[:], 0.0)
        v.memset(S_all[:], 0.0)
        v.drain().then_inc(dve.h, 1)
        dve.inc()
    A2_INIT_DVE = dve.n

    vre = vS.rearrange("it p d -> p it d")
    pbuf_last = {0: None, 1: None, 2: None}
    prev_e0 = [None]
    prev_qfree = [None]
    prev_qsc = [None]

    for g_i in range(A2_NG):
        @Sy
        def _(sy, g_i=g_i):
            sy.wait_ge(dA2.h, dA2.n)
            sy.wait_ge(pe.h, pe.n)
            sy.wait_ge(dve.h, dve.n)
            if g_i == 0:
                sy.wait_ge(dS.h, dS.n)
            for cc in range(2):
                sy.dma_start(out=kg[:, cc, 32:32 + T], in_=kS[2 * g_i + cc]).then_inc(dA2.h, 16)
                dA2.inc(16)
            sy.dma_start(out=gzA[:, :, 0:64],
                         in_=vre[:, :, 256 * g_i:256 * g_i + 64]).then_inc(dA2.h, 16)
            sy.dma_start(out=gzB[:, :, 64:128],
                         in_=vre[:, :, 256 * g_i + 64:256 * g_i + 128]).then_inc(dA2.h, 16)
            dA2.inc(32)
        A2_G_LOADS = dA2.n

        for p_i in range(2):
            gh = 4 * g_i + 2 * p_i
            cc = p_i

            if p_i == 1:
                @Sy
                def _(sy, g_i=g_i):
                    sy.wait_ge(dA2.h, dA2.n)
                    sy.wait_ge(pe.h, pe.n)
                    sy.dma_start(out=gzA[:, :, 0:64],
                                 in_=vre[:, :, 256 * g_i + 128:256 * g_i + 192]).then_inc(dA2.h, 16)
                    sy.dma_start(out=gzB[:, :, 64:128],
                                 in_=vre[:, :, 256 * g_i + 192:256 * g_i + 256]).then_inc(dA2.h, 16)
                    dA2.inc(32)

            @Sy
            def _(sy, g_i=g_i, cc=cc, pq=prev_qfree[0], pqs=prev_qsc[0]):
                sy.wait_ge(dA2.h, dA2.n)
                if pq is not None:
                    sy.wait_ge(dve.h, pq)
                if pqs is not None:
                    sy.wait_ge(pe.h, pqs)
                sy.dma_start(out=qzA[0:64, :],
                             in_=qS[2 * g_i + cc][0:64, :]).then_inc(dA2.h, 16)
                sy.dma_start(out=qzB[64:128, :],
                             in_=qS[2 * g_i + cc][64:128, :]).then_inc(dA2.h, 16)
                dA2.inc(32)
            PAIR_LOADS = dA2.n

            pair = ExitStack()
            sfx = f"_{g_i}_{p_i}"
            ps_sc = [pair.enter_context(nc.psum_tensor(f"ps_sc{i}" + sfx, [128, 2, 256], F32))
                     for i in range(2)]
            ps_trA = pair.enter_context(nc.psum_tensor("ps_trA" + sfx, [128, 512], F32))
            ps_trB = pair.enter_context(nc.psum_tensor("ps_trB" + sfx, [128, 512], F32))
            ps_ctx = pair.enter_context(nc.psum_tensor("ps_ctx" + sfx, [128, 4, 512], F32))

            def ctx_mm(t, lhsT, rhs, lo, hi, stop_banks=()):
                mm = None
                for (aa, bb) in cut512(lo, hi):
                    bank = aa // 512
                    piece = rhs if (aa == lo and bb == hi) else rhs[:, aa - lo:bb - lo]
                    mm = t.matmul(ps_ctx[:, bank, aa - bank * 512:bb - bank * 512],
                                  lhsT, piece, start=False,
                                  stop=(bank in stop_banks))
                return mm

            @Te
            def _(t):
                t.wait_ge(dve.h, dve.n)
                t.wait_ge(act.h, act.n)
                mm = None
                for bank in range(4):
                    mm = t.matmul(ps_ctx[:, bank, :], ones_row[0:1, :], zrow[0:1, :],
                                  start=True, stop=False)
                mm.then_inc(pe.h, 1)
                pe.inc()

            ACT_BASE = act.n
            sc_pe, tr_pe, epi_dve = {}, {}, {}
            for it in range(NIT):
                i0 = it * 128
                buf = it % 3

                @Te
                def _(t, it=it, i0=i0, cc=cc, buf=buf):
                    t.wait_ge(dA2.h, PAIR_LOADS)
                    if it < 2:
                        t.wait_ge(act.h, ACT_BASE)
                    else:
                        t.wait_ge(act.h, ACT_BASE + it - 1)
                    if it == 0:
                        t.wait_ge(dve.h, max(A2_INIT_DVE, dve.n))
                    mm = None
                    for hh, qz in ((0, qzA), (1, qzB)):
                        q_l = qz[:, i0:i0 + 128]
                        t.matmul(ps_sc[it % 2][:, hh, 0:192], q_l,
                                 kg[:, cc, i0:i0 + 192], start=True, stop=True)
                        mm = t.matmul(ps_sc[it % 2][:, hh, 192:193], q_l,
                                      kg[:, cc, 32:33], start=True, stop=True)
                    mm.then_inc(pe.h, 1)
                    pe.inc()
                sc_pe[it] = pe.n

                @V
                def _(v, it=it, gh=gh, buf=buf):
                    v.wait_ge(act.h, ACT_BASE + it + 1)
                    var = 1 if it == 0 else (2 if it == NIT - 1 else 0)
                    v.tensor_tensor(out=pbuf[:, buf, :, WP:WP + 192],
                                    in0=pbuf[:, buf, :, WP:WP + 192],
                                    in1=eB[:, var, :].unsqueeze(1).broadcast_to([128, 2, W]),
                                    op=A.mult)
                    v.tensor_tensor(out=pbuf[:, buf, :, WP + 192:WP + 193],
                                    in0=pbuf[:, buf, :, WP + 192:WP + 193],
                                    in1=efc[:, it:it + 1].unsqueeze(1).broadcast_to([128, 2, 1]),
                                    op=A.mult)
                    v.drain()
                    for hh in range(2):
                        v.scalar_tensor_tensor(out=pbuf[:, buf, hh, WP:WP + 193],
                                               in0=pbuf[:, buf, hh, WP:WP + 193], scalar=1.0,
                                               in1=pbuf[:, buf, hh, WP:WP + 193],
                                               op0=A.is_ge, op1=A.mult,
                                               accum_out=S_all[:, it, gh + hh:gh + hh + 1])
                    v.drain()
                    v.tensor_scalar(out=Rt[:], in0=S_all[:, it, gh:gh + 2],
                                    scalar1=1.0, scalar2=None, op0=A.max)
                    v.drain()
                    v.reciprocal(out=Rt[:], in_=Rt[:])
                    v.drain()
                    v.tensor_tensor(out=pbuf[:, buf, :, WP:WP + 193],
                                    in0=pbuf[:, buf, :, WP:WP + 193],
                                    in1=Rt[:].unsqueeze(2).broadcast_to([128, 2, 193]),
                                    op=A.mult)
                    v.drain()
                    v.tensor_copy(out=pc_st[:, it, gh:gh + 2],
                                  in_=pbuf[:, buf, :, WP + 192:WP + 193]).then_inc(dve.h, 1)
                    dve.inc()
                epi_dve[it] = dve.n

                @Te
                def _(t, it=it, buf=buf):
                    t.wait_ge(dve.h, epi_dve[it])
                    mm = None
                    for hh, ps_tr in ((0, ps_trA), (1, ps_trB)):
                        t.transpose(ps_tr[:, 0:128], pbuf[:, buf, hh, 0:128], ident[:])
                        t.transpose(ps_tr[:, 128:256], pbuf[:, buf, hh, 128:256], ident[:])
                        mm = t.transpose(ps_tr[0:33, 256:384], pbuf[:, buf, hh, 256:289],
                                         ident[:])
                    mm.then_inc(pe.h, 1)
                    pe.inc()
                tr_pe[it] = pe.n

                @V
                def _(v, it=it):
                    v.wait_ge(pe.h, tr_pe[it])
                    for hh, ps_tr in ((0, ps_trA), (1, ps_trB)):
                        v.tensor_copy(out=pT[:, hh, it % 3, 32:160], in_=ps_tr[:, 128:256])
                        if it > 0:
                            v.tensor_copy(out=pT[96:128, hh, (it - 1) % 3, 160:192],
                                          in_=ps_tr[96:128, 0:32])
                        if it < NIT - 1:
                            v.tensor_copy(out=pT[0:32, hh, (it + 1) % 3, 0:32],
                                          in_=ps_tr[0:32, 256 + 96:256 + 128])
                    v.drain().then_inc(dve.h, 1)
                    dve.inc()
                ASM_DVE = dve.n

                if it >= 1:
                    tt_ = it - 1

                    @Te
                    def _(t, tt_=tt_, ASM_DVE=ASM_DVE):
                        t.wait_ge(dve.h, ASM_DVE)
                        mm = None
                        for hh, gz in ((0, gzA), (1, gzB)):
                            lo = max(0, 128 * tt_ - 32)
                            hi = min(T, 128 * tt_ + 160)
                            rhs = pT[:, hh, tt_ % 3,
                                     (lo - (128 * tt_ - 32)):(hi - (128 * tt_ - 32))]
                            mm = ctx_mm(t, gz[:, tt_, :], rhs, lo, hi)
                        mm.then_inc(pe.h, 1)
                        pe.inc()

            @Sc
            def _(sc, gh=gh, ACT_BASE=ACT_BASE, pbl=dict(pbuf_last)):
                for it in range(NIT):
                    buf = it % 3
                    sc.wait_ge(pe.h, sc_pe[it])
                    if it >= 3:
                        sc.wait_ge(pe.h, tr_pe[it - 3])
                    elif pbl[buf] is not None:
                        sc.wait_ge(pe.h, pbl[buf])
                    sc.activation(out=pbuf[:, buf, :, WP:WP + 193],
                                  in_=ps_sc[it % 2][:, :, 0:193], func=AF.Exp,
                                  scale=0.125).then_inc(act.h, 1)
                    act.inc()
            assert act.n == ACT_BASE + NIT
            pbuf_last[0] = tr_pe[NIT - 1]
            pbuf_last[1] = tr_pe[NIT - 3]
            pbuf_last[2] = tr_pe[NIT - 2]

            @Te
            def _(t):
                t.wait_ge(dve.h, dve.n)
                tt_ = NIT - 1
                mm = None
                for hh, gz in ((0, gzA), (1, gzB)):
                    lo, hi = 128 * tt_ - 32, T
                    rhs = pT[:, hh, tt_ % 3, 0:hi - lo]
                    mm = ctx_mm(t, gz[:, tt_, :], rhs, lo, hi)
                mm.then_inc(pe.h, 1)
                pe.inc()
            CTXW_PE = pe.n

            @V
            def _(v, cc=cc, pq=prev_e0[0]):
                if pq is not None:
                    v.wait_ge(pe.h, pq)
                v.memset(q0sel[:], 0.0)
                v.drain()
                v.tensor_copy(out=q0sel[0:64, 0:1], in_=qzA[0:64, 0:1])
                v.tensor_copy(out=q0sel[64:128, 1:2], in_=qzB[64:128, 0:1])
                v.drain().then_inc(dve.h, 1)
                dve.inc()
            Q0_DVE = dve.n
            prev_qfree[0] = Q0_DVE
            prev_qsc[0] = sc_pe[NIT - 1]

            e0_pe = {}
            for nt in range(4):
                @Te
                def _(t, nt=nt, cc=cc):
                    t.wait_ge(dve.h, Q0_DVE)
                    if nt > 0:
                        t.wait_ge(act.h, act.n + nt)
                    t.matmul(ps_trA[0:2, 0:512], q0sel[:],
                             kg[:, cc, 32 + nt * 512:32 + (nt + 1) * 512],
                             start=True, stop=True).then_inc(pe.h, 1)
                    pe.inc()
                e0_pe[nt] = pe.n

            prev_e0[0] = e0_pe[3]

            @Sc
            def _(sc):
                for nt in range(4):
                    sc.wait_ge(pe.h, e0_pe[nt])
                    sc.activation(out=e0sb[0:2, nt, :], in_=ps_trA[0:2, 0:512],
                                  func=AF.Exp, scale=0.125).then_inc(act.h, 1)
                    act.inc()
            E0_ACT = act.n

            @V
            def _(v):
                v.wait_ge(act.h, E0_ACT)
                v.tensor_tensor(out=e0sb[0:2, :, :], in0=e0sb[0:2, :, :],
                                in1=efrow[0:2, :].rearrange("p (a b) -> p a b", a=4),
                                op=A.mult)
                v.drain()
                v.scalar_tensor_tensor(out=e0sb[0:2, :, :], in0=e0sb[0:2, :, :],
                                       scalar=1.0, in1=e0sb[0:2, :, :],
                                       op0=A.is_ge, op1=A.mult,
                                       accum_out=S0sb[0:2, 0:1])
                v.drain()
                v.tensor_single_scalar(out=S0sb[0:2, 1:2], in_=S0sb[0:2, 0:1],
                                       scalar=0.0, op=A.is_equal)
                v.drain()
                v.tensor_add(out=S0sb[0:2, 1:2], in0=S0sb[0:2, 1:2], in1=S0sb[0:2, 0:1])
                v.drain()
                v.reciprocal(out=S0sb[0:2, 1:2], in_=S0sb[0:2, 1:2])
                v.drain()
                v.tensor_scalar(out=e0sb[0:2, :, :], in0=e0sb[0:2, :, :],
                                scalar1=S0sb[0:2, 1:2], scalar2=None, op0=A.mult)
                v.drain().then_inc(dve.h, 1)
                dve.inc()
            E0_DVE = dve.n

            @Te
            def _(t):
                t.wait_ge(dve.h, E0_DVE)
                mm = None
                for jt in range(NIT):
                    mm = t.transpose(ps_trB[:, 2 * jt:2 * jt + 2],
                                     e0sb[0:2, jt // 4, (jt % 4) * 128:(jt % 4) * 128 + 128],
                                     ident[0:2, 0:2])
                mm.then_inc(pe.h, 1)
                pe.inc()
            P0T_PE = pe.n

            @V
            def _(v):
                v.wait_ge(pe.h, P0T_PE)
                v.tensor_copy(out=p0T[:].rearrange("p a b -> p (a b)"),
                              in_=ps_trB[:, 0:32]).then_inc(dve.h, 1)
                dve.inc()
            P0T_DVE = dve.n

            @Te
            def _(t):
                t.wait_ge(dve.h, P0T_DVE)
                t.transpose(ps_trA[0:1, 0:2], S0sb[0:2, 0:1],
                            ident[0:2, 0:2]).then_inc(pe.h, 1)
                pe.inc()
            S0T_PE = pe.n

            @V
            def _(v, gh=gh):
                v.wait_ge(pe.h, S0T_PE)
                v.tensor_add(out=S_all[0:1, 0, gh:gh + 2], in0=S_all[0:1, 0, gh:gh + 2],
                             in1=ps_trA[0:1, 0:2])
                v.drain()
                v.tensor_single_scalar(out=iszsc[:], in_=S_all[:, :, gh:gh + 2],
                                       scalar=0.0, op=A.is_equal)
                v.drain().then_inc(dve.h, 1)
                dve.inc()
            ISZ_DVE = dve.n

            mv_pe = {}
            for hh, gz in ((0, gzA), (1, gzB)):
                @Te
                def _(t, hh=hh, gz=gz):
                    t.wait_ge(dve.h, ISZ_DVE)
                    if hh == 1:
                        t.wait_ge(dve.h, dve.n)
                    mm = None
                    for jt in range(NIT):
                        mm = t.matmul(ps_trA[0:1, 128:256], ones_col[:, 0:1], gz[:, jt, :],
                                      start=(jt == 0), stop=(jt == NIT - 1))
                    mm.then_inc(pe.h, 1)
                    pe.inc()
                mv_pe[hh] = pe.n

                @V
                def _(v, hh=hh):
                    v.wait_ge(pe.h, mv_pe[hh])
                    v.tensor_scalar(out=mvzI[0:1, hh, :], in0=ps_trA[0:1, 128:256],
                                    scalar1=1.0 / T, scalar2=None,
                                    op0=A.mult).then_inc(dve.h, 1)
                    dve.inc()
            MV_DVE = dve.n

            @G
            def _(g, gh=gh):
                g.wait_ge(dve.h, ISZ_DVE)
                g.wait_ge(dG.h, dG.n)
                g.dma_start(out=pcI[0:1, :], in_=pc_st[:, :, gh:gh + 2]).then_inc(dG.h, 16)
                dG.inc(16)
                g.dma_start(out=izI[0:1, :], in_=iszsc[:]).then_inc(dG.h, 16)
                dG.inc(16)
                g.wait_ge(dG.h, dG.n)
                g.engine_nop().then_inc(gp.h, 1)
                gp.inc()
            ROWS_GP = gp.n

            @Te
            def _(t, gh=gh):
                t.wait_ge(gp.h, ROWS_GP)
                t.wait_ge(dve.h, MV_DVE)
                pcv = pcI[0:1, :].rearrange("o (p it h) -> o it h p", p=128, it=NIT)
                izv = izI[0:1, :].rearrange("o (p it h) -> o it h p", p=128, it=NIT)
                mm = None
                for hh, gz in ((0, gzA), (1, gzB)):
                    last_h = hh == 1
                    for bank in range(4):
                        ctx_mm(t, gz[0:1, 0, :], pcv[:, 4 * bank:4 * bank + 4, hh, :],
                               bank * 512, (bank + 1) * 512)
                    for bank in range(4):
                        ctx_mm(t, mvzI[0:1, hh, :],
                               izv[:, 4 * bank:4 * bank + 4, hh, :],
                               bank * 512, (bank + 1) * 512,
                               stop_banks=(0, 1, 2, 3) if last_h else ())
                    for jt in range(NIT):
                        mm = t.matmul(ps_trB[:, hh:hh + 1], gz[:, jt, :],
                                      p0T[:, jt, hh:hh + 1],
                                      start=(jt == 0), stop=(jt == NIT - 1))
                mm.then_inc(pe.h, 1)
                pe.inc()
            CTX_PE = pe.n

            @V
            def _(v, g_i=g_i, p_i=p_i):
                v.wait_ge(pe.h, CTX_PE)
                dc = 2 * g_i + p_i
                v.tensor_copy(out=c0fix[:, 0:2], in_=ps_trB[:, 0:2])
                v.drain()
                v.tensor_add(out=c0fix[:, 2:3], in0=ps_ctx[:, 0, 0:1], in1=c0fix[:, 0:1])
                v.drain()
                v.tensor_add(out=c0fix[:, 2:3], in0=c0fix[:, 2:3], in1=c0fix[:, 1:2])
                v.tensor_copy(out=cxAll[:, 0, dc, :],
                              in_=ps_ctx[:].rearrange("p a b -> p (a b)"))
                v.drain()
                v.tensor_copy(out=cxAll[:, 0, dc, 0:1], in_=c0fix[:, 2:3])
                v.tensor_tensor(out=cxAll[:, 1, dc, 1:T],
                                in0=ps_ctx[:].rearrange("p a b -> p (a b)")[:, 1:T],
                                in1=cxAll[:, 0, dc, 1:T], op=A.subtract)
                v.drain()
                v.tensor_tensor(out=cxAll[:, 1, dc, 0:1], in0=c0fix[:, 2:3],
                                in1=cxAll[:, 0, dc, 0:1], op=A.subtract)
                v.drain().then_inc(dve.h, 1)
                dve.inc()
            CXL_DVE = dve.n

            pair.close()

    @Sc
    def _(sc):
        sc.wait_ge(dve.h, CXL_DVE)
        sc.drain().then_inc(act.h, 1)
        act.inc()
    A2_DONE_ACT = act.n
    A2_CX_DVE = CXL_DVE
    a2.close()
    if UPTO <= 3:
        fs.close()
        es.close()
        return nc

    wfs = ExitStack()
    xq1FM = wfs.enter_context(nc.sbuf_tensor("xq1FM", [128, EC, T], BF16))
    a3 = ExitStack()
    wo_h = a3.enter_context(nc.sbuf_tensor("wo_h", [128, EC, E], BF16))
    wo_l = a3.enter_context(nc.sbuf_tensor("wo_l", [128, EC, E], BF16))
    xq1TM = a3.enter_context(nc.sbuf_tensor("xq1TM", [128, NIT, E], BF16))
    xt3 = a3.enter_context(nc.sbuf_tensor("xt3", [128, E], F32))
    resid = a3.enter_context(nc.sbuf_tensor("resid", [128, E], F32))
    sqscr = a3.enter_context(nc.sbuf_tensor("sqscr", [128, E], F32))
    lnst = a3.enter_context(nc.sbuf_tensor("lnst", [128, 8], F32))
    wslo = sqscr[:, :].rearrange("p (a b) -> p a b", a=EC)
    ps_o = [a3.enter_context(nc.psum_tensor(f"ps_o{i}", [128, 2, 512], F32))
            for i in range(2)]
    ps_bf = a3.enter_context(nc.psum_tensor("ps_bf", [128, 2, 512], BF16))

    w1re = w1qS.rearrange("kc p c -> p kc c")

    wo_done = {}
    for oc in range(EC):
        @Sy
        def _(sy, oc=oc):
            sy.wait_ge(dWo.h, dWo.n)
            if oc == 0:
                sy.wait_ge(act.h, A2_DONE_ACT)
            else:
                sy.wait_ge(dve.h, wo_done[oc - 1])
            sy.dma_start(out=wslo,
                         in_=wT_d["woT"].rearrange("(kc p) o -> p kc o", p=128)
                         [:, :, oc * 128:(oc + 1) * 128]).then_inc(dWo.h, 16)
            dWo.inc(16)
        WL = dWo.n

        @V
        def _(v, oc=oc, WL=WL):
            v.wait_ge(dWo.h, WL)
            v.tensor_copy(out=wo_h[:, :, oc * 128:(oc + 1) * 128], in_=wslo)
            v.drain()
            v.tensor_tensor(out=wo_l[:, :, oc * 128:(oc + 1) * 128], in0=wslo,
                            in1=wo_h[:, :, oc * 128:(oc + 1) * 128],
                            op=A.subtract).then_inc(dve.h, 1)
            dve.inc()
        wo_done[oc] = dve.n
    WO_DVE = dve.n

    ctmm, res_dve, q1_dve, x1sp = {}, {}, {}, {}
    for it in range(NIT):
        ps = ps_o[it % 2]
        dci = dCt[it % 2]

        @Sy
        def _(sy, it=it, dci=dci):
            sy.wait_ge(dci.h, dci.n)
            if it >= 1:
                sy.wait_ge(dve.h, res_dve[it - 1])
            sy.dma_start(out=xt3[:],
                         in_=x_d.rearrange("(it p) e -> it p e", p=128)[it]).then_inc(dci.h, 16)
            dci.inc(16)
        CT_L = dci.n

        @Te
        def _(t, it=it, ps=ps, dci=dci, CT_L=CT_L):
            t.wait_ge(dci.h, CT_L)
            if it == 0:
                t.wait_ge(dve.h, A2_CX_DVE)
            t.wait_ge(dve.h, max(WO_DVE, res_dve.get(it - 2, 0)))
            i0 = it * 128
            mm = None
            for nh in range(2):
                first = True
                for (hl, bw) in ((0, wo_h), (0, wo_l), (1, wo_h)):
                    for kc in range(EC):
                        mm = t.matmul(ps[:, nh], cxAll[:, hl, kc, i0:i0 + 128],
                                      bw[:, kc, nh * 512:(nh + 1) * 512],
                                      start=first, stop=(hl == 1 and kc == EC - 1))
                        first = False
            mm.then_inc(pe.h, 1)
            pe.inc()
        ctmm[it] = pe.n

        @V
        def _(v, it=it, ps=ps):
            v.wait_ge(pe.h, ctmm[it])
            if it == 0:
                v.wait_ge(act.h, BCAST_ACT)
            v.drain()
            if it >= 1:
                v.wait_ge(dS.h, x1sp[it - 1])
            v.tensor_add(out=sqscr[:], in0=ps[:].rearrange("p a b -> p (a b)"), in1=bob[:])
            v.drain()
            v.scalar_tensor_tensor(out=resid[:], in0=sqscr[:], scalar=0.0, in1=xt3[:],
                                   op0=A.add, op1=A.add,
                                   accum_out=lnst[:, 0:1]).then_inc(dve.h, 1)
            dve.inc()
            res_dve[it] = dve.n
            v.drain()
            v.tensor_tensor(out=sqscr[:], in0=resid[:], in1=resid[:], op=A.mult)
            v.drain()
            v.tensor_reduce(out=lnst[:, 1:2], in_=sqscr[:], axis=AX.X, op=A.add)
            v.drain()
            v.tensor_scalar(out=lnst[:, 0:2], in0=lnst[:, 0:2], scalar1=1.0 / E,
                            scalar2=None, op0=A.mult)
            v.drain()
            v.tensor_tensor(out=lnst[:, 2:3], in0=lnst[:, 0:1], in1=lnst[:, 0:1], op=A.mult)
            v.drain()
            v.tensor_tensor(out=lnst[:, 1:2], in0=lnst[:, 1:2], in1=lnst[:, 2:3],
                            op=A.subtract)
            v.drain()
            v.tensor_scalar(out=lnst[:, 1:2], in0=lnst[:, 1:2], scalar1=1e-5,
                            scalar2=None, op0=A.add).then_inc(dve.h, 1)
            dve.inc()
        ST_DVE = dve.n

        @Sc
        def _(sc, it=it, ST_DVE=ST_DVE):
            sc.wait_ge(dve.h, ST_DVE)
            sc.activation(out=lnst[:, 2:3], in_=lnst[:, 1:2],
                          func=AF.Sqrt).then_inc(act.h, 1)
            act.inc()
        SQT_ACT = act.n

        @V
        def _(v, it=it, SQT_ACT=SQT_ACT):
            v.wait_ge(act.h, SQT_ACT)
            v.reciprocal(out=lnst[:, 3:4], in_=lnst[:, 2:3])
            v.drain()
            v.tensor_scalar(out=resid[:], in0=resid[:], scalar1=lnst[:, 0:1],
                            scalar2=lnst[:, 3:4], op0=A.subtract,
                            op1=A.mult).then_inc(dve.h, 1)
            dve.inc()
            X1_DVE = dve.n
            v.drain()
            if it == 0:
                v.wait_ge(dve2.h, P1_W1INV)
            v.reduce_max(out=lnst[:, 4:5], in_=resid[:], axis=AX.X,
                         apply_absolute_value=True)
            v.drain()
            v.tensor_scalar(out=lnst[:, 4:5], in0=lnst[:, 4:5], scalar1=1e-5,
                            scalar2=None, op0=A.max)
            v.drain()
            v.reciprocal(out=lnst[:, 5:6], in_=lnst[:, 4:5])
            v.drain()
            v.tensor_scalar(out=lnst[:, 5:6], in0=lnst[:, 5:6], scalar1=127.0,
                            scalar2=None, op0=A.mult)
            v.tensor_scalar(out=sg1tm[:, it:it + 1], in0=lnst[:, 4:5],
                            scalar1=sw1[:, 0:1], scalar2=1.0 / 127.0,
                            op0=A.mult, op1=A.mult)
            v.drain()
            v.tensor_scalar(out=sqscr[:], in0=resid[:], scalar1=lnst[:, 5:6],
                            scalar2=magic[:, 0:1], op0=A.mult, op1=A.add)
            v.drain()
            v.tensor_scalar(out=xq1TM[:, it, :], in0=sqscr[:], scalar1=magic[:, 0:1],
                            scalar2=None, op0=A.subtract).then_inc(dve.h, 1)
            dve.inc()
            q1_dve[it] = dve.n

        @Sc
        def _(sc, it=it):
            sc.wait_ge(dve.h, q1_dve[it] - 1)
            sc.wait_ge(dS.h, dS.n)
            sc.dma_start(out=x1S[it], in_=resid[:]).then_inc(dS.h, 16)
            dS.inc(16)
        x1sp[it] = dS.n
    A3_DVE = q1_dve[NIT - 1]

    @Sy
    def _(sy):
        sy.wait_ge(pe.h, ctmm[NIT - 1])
        sy.wait_ge(dG.h, P1_W1Q)
        sy.wait_ge(dF.h, dF.n)
        sy.dma_start(out=wbufF[:, :, :], in_=w1re[:, :, :]).then_inc(dF.h, 16)
        dF.inc(16)
    WBF_L = dF.n

    f0pe, f0ev = {}, {}
    for it in range(NIT):
        for h2 in range(2):
            idx = 2 * it + h2

            @Te
            def _(t, it=it, h2=h2, idx=idx):
                t.wait_ge(dve.h, q1_dve[it])
                if idx >= 2:
                    t.wait_ge(act.h, f0ev[idx - 2])
                mm = None
                for e4 in range(4):
                    ec = 4 * h2 + e4
                    mm = t.transpose(ps_bf[:, idx % 2, e4 * 128:(e4 + 1) * 128],
                                     xq1TM[:, it, ec * 128:(ec + 1) * 128], identb[:])
                mm.then_inc(pe.h, 1)
                pe.inc()
            f0pe[idx] = pe.n

            @Sc
            def _(sc, it=it, h2=h2, idx=idx):
                sc.wait_ge(pe.h, f0pe[idx])
                sc.copy(out=xq1FM[:, 4 * h2:4 * h2 + 4, it * 128:(it + 1) * 128],
                        in_=ps_bf[:, idx % 2].rearrange("p (a b) -> p a b", a=4)
                        ).then_inc(act.h, 1)
                act.inc()
            f0ev[idx] = act.n
    XQ1_ACT = act.n
    a3.close()
    if UPTO <= 4:
        wfs.close()
        fs.close()
        es.close()
        return nc

    ff = ExitStack()
    harena = ff.enter_context(nc.sbuf_tensor("harena", [128, HC, 258], F32))
    tmpa = ff.enter_context(nc.sbuf_tensor("tmpa", [128, 8, TT], F32))
    xq2sb = ff.enter_context(nc.sbuf_tensor("xq2sb", [128, HC, TT], BF16))
    mabs = ff.enter_context(nc.sbuf_tensor("mabs", [128, TT], F32))
    mx4 = ff.enter_context(nc.sbuf_tensor("mx4", [128, 8], F32))
    s1bc = ff.enter_context(nc.sbuf_tensor("s1bc", [128, T], F32))
    s1row = ff.enter_context(nc.sbuf_tensor("s1row", [1, T], F32))
    s2sb = ff.enter_context(nc.sbuf_tensor("s2sb", [1, 256], F32))
    sgb2 = ff.enter_context(nc.sbuf_tensor("sgb2", [128, TT], F32))
    ps_h = ff.enter_context(nc.psum_tensor("ps_h", [128, 2, 512], F32))
    ps_sg = ff.enter_context(nc.psum_tensor("ps_sg", [128, 512], F32))
    ps_mx = ff.enter_context(nc.psum_tensor("ps_mx", [128, 2, 128], F32))
    ps_bc = ff.enter_context(nc.psum_tensor("ps_bc", [128, 256], F32))

    xq2re = xq2S.rearrange("kc p c -> p kc c")

    @G
    def _(g):
        g.wait_ge(dve.h, A3_DVE)
        g.wait_ge(dS.h, dS.n)
        g.wait_ge(dG.h, dG.n)
        g.dma_start(out=s1row[0:1, :].rearrange("o (p it) -> o p it", p=128),
                    in_=sg1tm[:]).then_inc(dG.h, 16)
        dG.inc(16)
        g.wait_ge(dG.h, dG.n)
        g.engine_nop().then_inc(gp.h, 1)
        gp.inc()
    S1ROW_GP = gp.n

    s1v = s1row[0:1, :].rearrange("o (p it) -> o it p", p=128)
    s1mm, s1ev = {}, {}
    for q in range(4):
        @Te
        def _(t, q=q):
            if q == 0:
                t.wait_ge(gp.h, S1ROW_GP)
            else:
                t.wait_ge(dve.h, s1ev[q - 1])
            t.matmul(ps_sg[:, 0:512], ones_row[0:1, :], s1v[:, 4 * q:4 * q + 4, :],
                     start=True, stop=True).then_inc(pe.h, 1)
            pe.inc()
        s1mm[q] = pe.n

        @V
        def _(v, q=q):
            v.wait_ge(pe.h, s1mm[q])
            v.tensor_copy(out=s1bc[:, q * 512:(q + 1) * 512],
                          in_=ps_sg[:, 0:512]).then_inc(dve.h, 1)
            dve.inc()
        s1ev[q] = dve.n
    S1BC_DVE = dve.n

    @V
    def _(v):
        v.memset(harena[:, :, 0:1], 0.0)
        v.drain().then_inc(dve.h, 1)
        dve.inc()
    HZ_DVE = dve.n

    he_dve, w1mm, xq2sp = {}, {}, {}
    FF_SILU = [None]
    for j in range(NFT):
        t0 = j * TT
        lo = t0 - 1 if j else t0
        NN = 258 if 0 < j < NFT - 1 else 257
        hoff = 1 if j == 0 else 0

        for hc in range(HC):
            hg = HC * j + hc
            bank = hc % 2

            @Te
            def _(t, j=j, hc=hc, hg=hg, bank=bank, lo=lo, NN=NN):
                if hg == 0:
                    t.wait_ge(act.h, XQ1_ACT)
                    t.wait_ge(dF.h, WBF_L)
                if hg >= 2:
                    t.wait_ge(dve.h, he_dve[hg - 2])
                mm = None
                for kc in range(EC):
                    mm = t.matmul(ps_h[:, bank, 0:NN],
                                  wbufF[:, kc, hc * 128:(hc + 1) * 128],
                                  xq1FM[:, kc, lo:lo + NN],
                                  start=(kc == 0), stop=(kc == EC - 1))
                mm.then_inc(pe.h, 1)
                pe.inc()
            w1mm[hg] = pe.n

            @V
            def _(v, j=j, hc=hc, hg=hg, bank=bank, lo=lo, NN=NN, hoff=hoff):
                v.wait_ge(pe.h, w1mm[hg])
                if hc == 0:
                    v.drain()
                if hg == 0:
                    v.wait_ge(dve.h, max(S1BC_DVE, HZ_DVE))
                v.tensor_tensor(out=harena[:, hc, hoff:hoff + NN],
                                in0=ps_h[:, bank, 0:NN], in1=s1bc[:, lo:lo + NN],
                                op=A.mult).then_inc(dve.h, 1)
                dve.inc()
            he_dve[hg] = dve.n

        if j == NFT - 1:
            @V
            def _(v):
                v.drain()
                v.memset(harena[:, :, 257:258], 0.0)
                v.drain().then_inc(dve.h, 1)
                dve.inc()

        conv_done, silu_done = {}, {}
        prev_silu = FF_SILU[0]
        for grp in range(4):
            pv = prev_silu if grp == 0 else silu_done[grp - 1]

            @V
            def _(v, grp=grp, pv=pv):
                v.drain()
                if pv is not None:
                    v.wait_ge(act.h, pv)
                for i in range(8):
                    hc = 8 * grp + i
                    v.tensor_scalar(out=tmpa[:, i, :], in0=harena[:, hc, 0:TT],
                                    scalar1=cws[:, hc, 0:1], scalar2=None, op0=A.mult)
                v.drain()
                for i in range(8):
                    hc = 8 * grp + i
                    v.scalar_tensor_tensor(out=tmpa[:, i, :], in0=harena[:, hc, 1:1 + TT],
                                           scalar=cws[:, hc, 1:2], in1=tmpa[:, i, :],
                                           op0=A.mult, op1=A.add)
                v.drain()
                mm = None
                for i in range(8):
                    hc = 8 * grp + i
                    mm = v.scalar_tensor_tensor(out=tmpa[:, i, :],
                                                in0=harena[:, hc, 2:2 + TT],
                                                scalar=cws[:, hc, 2:3], in1=tmpa[:, i, :],
                                                op0=A.mult, op1=A.add)
                mm.then_inc(dve.h, 1)
                dve.inc()
            conv_done[grp] = dve.n

            @Sc
            def _(sc, j=j, grp=grp):
                sc.wait_ge(dve.h, conv_done[grp])
                if j == 0 and grp == 0:
                    sc.wait_ge(dP0.h, P0_LOADS)
                mm = None
                for i in range(8):
                    hc = 8 * grp + i
                    mm = sc.activation(out=harena[:, hc, 0:TT], in_=tmpa[:, i, :],
                                       func=AF.Silu, bias=cbs[:, hc:hc + 1])
                mm.then_inc(act.h, 1)
                act.inc()
            silu_done[grp] = act.n
        FF_SILU[0] = silu_done[3]

        @V
        def _(v, j=j):
            v.wait_ge(act.h, silu_done[3])
            v.tensor_reduce(out=mabs[:], in_=harena[:, :, 0:TT].rearrange("p h t -> p t h"),
                            axis=AX.X, op=A.max,
                            apply_absolute_value=True).then_inc(dve.h, 1)
            dve.inc()
        MABS_DVE = dve.n

        @Te
        def _(t, j=j, MABS_DVE=MABS_DVE):
            t.wait_ge(dve.h, MABS_DVE)
            mm = None
            for b in range(2):
                mm = t.transpose(ps_mx[:, b, :], mabs[:, b * 128:(b + 1) * 128], ident[:])
            mm.then_inc(pe.h, 1)
            pe.inc()
        MXT_PE = pe.n

        @V
        def _(v, j=j, MXT_PE=MXT_PE):
            v.wait_ge(pe.h, MXT_PE)
            if j == 0:
                v.wait_ge(dve2.h, P1_W2INV)
            for b in range(2):
                v.tensor_reduce(out=mx4[:, b:b + 1], in_=ps_mx[:, b, :], axis=AX.X,
                                op=A.max)
            v.drain()
            v.tensor_scalar(out=mx4[:, 0:2], in0=mx4[:, 0:2], scalar1=1e-5,
                            scalar2=None, op0=A.max)
            v.drain()
            v.reciprocal(out=mx4[:, 2:4], in_=mx4[:, 0:2])
            v.drain()
            v.tensor_scalar(out=mx4[:, 4:6], in0=mx4[:, 2:4], scalar1=127.0,
                            scalar2=None, op0=A.mult)
            for b in range(2):
                v.tensor_scalar(out=sg2tm[:, 2 * j + b:2 * j + b + 1],
                                in0=mx4[:, b:b + 1], scalar1=sw2[:, 0:1],
                                scalar2=1.0 / 127.0, op0=A.mult, op1=A.mult)
            v.drain().then_inc(dve.h, 1)
            dve.inc()
        MX_DVE = dve.n

        @Te
        def _(t, j=j, MX_DVE=MX_DVE):
            t.wait_ge(dve.h, MX_DVE)
            mm = None
            for b in range(2):
                mm = t.transpose(ps_bc[0:1, b * 128:(b + 1) * 128],
                                 mx4[:, 4 + b:5 + b], ident[:])
            mm.then_inc(pe.h, 1)
            pe.inc()
        S2T_PE = pe.n

        @V
        def _(v, j=j, S2T_PE=S2T_PE):
            v.wait_ge(pe.h, S2T_PE)
            v.tensor_copy(out=s2sb[0:1, :], in_=ps_bc[0:1, :]).then_inc(dve.h, 1)
            dve.inc()
        S2E_DVE = dve.n

        @Te
        def _(t, j=j, S2E_DVE=S2E_DVE):
            t.wait_ge(dve.h, S2E_DVE)
            t.matmul(ps_mx[:].rearrange("p a b -> p (a b)"), ones_row[0:1, :],
                     s2sb[0:1, :], start=True, stop=True).then_inc(pe.h, 1)
            pe.inc()
        S2B_PE = pe.n

        @V
        def _(v, j=j, t0=t0, S2B_PE=S2B_PE):
            v.wait_ge(pe.h, S2B_PE)
            v.tensor_copy(out=sgb2[:], in_=ps_mx[:].rearrange("p a b -> p (a b)"))
            v.drain()
            v.tensor_tensor(out=harena[:, :, 0:TT], in0=harena[:, :, 0:TT],
                            in1=sgb2[:].unsqueeze(1).broadcast_to([128, HC, TT]),
                            op=A.mult)
            v.drain()
            if j >= 1:
                v.wait_ge(dS.h, xq2sp[j - 1])
            v.tensor_scalar(out=xq2sb[:], in0=harena[:, :, 0:TT],
                            scalar1=magic[:, 0:1], scalar2=magic[:, 0:1],
                            op0=A.add, op1=A.subtract).then_inc(dve.h, 1)
            dve.inc()
        XQ2_DVE = dve.n

        @Sc
        def _(sc, j=j, t0=t0, XQ2_DVE=XQ2_DVE):
            sc.wait_ge(dve.h, XQ2_DVE)
            sc.wait_ge(dS.h, dS.n)
            sc.dma_start(out=xq2re[:, :, t0:t0 + TT], in_=xq2sb[:]).then_inc(dS.h, 16)
            dS.inc(16)
        xq2sp[j] = dS.n

    @Sc
    def _(sc):
        sc.wait_ge(dS.h, dS.n)
        sc.drain().then_inc(act.h, 1)
        act.inc()
    FF_ACT = act.n
    ff.close()
    if UPTO <= 5:
        wfs.close()
        fs.close()
        es.close()
        return nc

    wfs.close()
    fn = ExitStack()
    wbuf2 = fn.enter_context(nc.sbuf_tensor("wbuf2", [128, HC, E], BF16))
    xq2it = fn.enter_context(nc.sbuf_tensor("xq2it", [128, HC, 128], BF16))
    xq2it2 = fn.enter_context(nc.sbuf_tensor("xq2it2", [128, HC, 128], BF16))
    x1t = fn.enter_context(nc.sbuf_tensor("x1t", [128, E], F32))
    x1t2 = fn.enter_context(nc.sbuf_tensor("x1t2", [128, E], F32))
    outsb = fn.enter_context(nc.sbuf_tensor("outsb", [128, E], F32))
    resid2 = fn.enter_context(nc.sbuf_tensor("resid2", [128, E], F32))
    sq2 = fn.enter_context(nc.sbuf_tensor("sq2", [128, E], F32))
    ln2 = fn.enter_context(nc.sbuf_tensor("ln2", [128, 8], F32))
    ps_f = [fn.enter_context(nc.psum_tensor(f"ps_f{i}", [128, 2, 512], F32))
            for i in range(2)]

    w2re = w2qS.rearrange("kc p c -> p kc c")
    loads_for, w2chunk = {}, {}

    @Sy
    def _(sy):
        sy.wait_ge(act.h, FF_ACT)
        sy.wait_ge(dG.h, P1_W2Q)
        sy.wait_ge(dFf[0].h, dFf[0].n)
        for q in range(4):
            sy.dma_start(out=wbuf2[:, 8 * q:8 * q + 8, :],
                         in_=w2re[:, 8 * q:8 * q + 8, :]).then_inc(dFf[0].h, 16)
            dFf[0].inc(16)
            w2chunk[q] = dFf[0].n
        sy.dma_start(out=xq2it[:], in_=xq2re[:, :, 0:128]).then_inc(dFf[0].h, 16)
        sy.dma_start(out=x1t[:], in_=x1S[0]).then_inc(dFf[0].h, 16)
        dFf[0].inc(32)
    loads_for[0] = dFf[0].n

    w2pe, r2_dve, ysp = {}, {}, {}
    for it in range(NIT):
        ps = ps_f[it % 2]
        xq = xq2it if it % 2 == 0 else xq2it2
        x1b = x1t if it % 2 == 0 else x1t2

        if it + 1 < NIT:
            dfn = dFf[(it + 1) % 2]

            @Sy
            def _(sy, it=it, dfn=dfn):
                sy.wait_ge(dfn.h, dfn.n)
                if it >= 1:
                    sy.wait_ge(pe.h, w2pe[it - 1])
                    sy.wait_ge(dve.h, r2_dve[it - 1])
                xqn = xq2it2 if it % 2 == 0 else xq2it
                x1n = x1t2 if it % 2 == 0 else x1t
                sy.dma_start(out=xqn[:],
                             in_=xq2re[:, :, (it + 1) * 128:(it + 2) * 128]).then_inc(dfn.h, 16)
                sy.dma_start(out=x1n[:], in_=x1S[it + 1]).then_inc(dfn.h, 16)
                dfn.inc(32)
            loads_for[it + 1] = dfn.n

        @Te
        def _(t, it=it, ps=ps, xq=xq):
            t.wait_ge(dFf[it % 2].h, loads_for[it])
            if it >= 2:
                t.wait_ge(dve.h, r2_dve[it - 2])
            mm = None
            for kc in range(HC):
                for nh in range(2):
                    mm = t.matmul(ps[:, nh], xq[:, kc, :],
                                  wbuf2[:, kc, nh * 512:(nh + 1) * 512],
                                  start=(kc == 0), stop=(kc == HC - 1))
            mm.then_inc(pe.h, 1)
            pe.inc()
        w2pe[it] = pe.n

        @V
        def _(v, it=it, ps=ps, x1b=x1b):
            v.wait_ge(pe.h, w2pe[it])
            if it >= 1:
                v.wait_ge(dS.h, ysp[it - 1])
            v.scalar_tensor_tensor(out=resid2[:], in0=ps[:].rearrange("p a b -> p (a b)"),
                                   scalar=sg2tm[:, it:it + 1], in1=x1b[:],
                                   op0=A.mult, op1=A.add,
                                   accum_out=ln2[:, 0:1]).then_inc(dve.h, 1)
            dve.inc()
            r2_dve[it] = dve.n
            v.drain()
            v.tensor_tensor(out=sq2[:], in0=resid2[:], in1=resid2[:], op=A.mult)
            v.drain()
            v.tensor_reduce(out=ln2[:, 1:2], in_=sq2[:], axis=AX.X, op=A.add)
            v.drain()
            v.tensor_scalar(out=ln2[:, 0:2], in0=ln2[:, 0:2], scalar1=1.0 / E,
                            scalar2=None, op0=A.mult)
            v.drain()
            v.tensor_tensor(out=ln2[:, 2:3], in0=ln2[:, 0:1], in1=ln2[:, 0:1], op=A.mult)
            v.drain()
            v.tensor_tensor(out=ln2[:, 1:2], in0=ln2[:, 1:2], in1=ln2[:, 2:3], op=A.subtract)
            v.drain()
            v.tensor_scalar(out=ln2[:, 1:2], in0=ln2[:, 1:2], scalar1=1e-5,
                            scalar2=None, op0=A.add).then_inc(dve.h, 1)
            dve.inc()
        ST2_DVE = dve.n

        @Sc
        def _(sc, it=it, ST2_DVE=ST2_DVE):
            sc.wait_ge(dve.h, ST2_DVE)
            sc.activation(out=ln2[:, 2:3], in_=ln2[:, 1:2],
                          func=AF.Sqrt).then_inc(act.h, 1)
            act.inc()
        SQT2_ACT = act.n

        @V
        def _(v, it=it, SQT2_ACT=SQT2_ACT):
            v.wait_ge(act.h, SQT2_ACT)
            v.reciprocal(out=ln2[:, 3:4], in_=ln2[:, 2:3])
            v.drain()
            v.tensor_scalar(out=outsb[:], in0=resid2[:], scalar1=ln2[:, 0:1],
                            scalar2=ln2[:, 3:4], op0=A.subtract,
                            op1=A.mult).then_inc(dve.h, 1)
            dve.inc()
        OUT_DVE = dve.n

        @Sc
        def _(sc, it=it, OUT_DVE=OUT_DVE):
            sc.wait_ge(dve.h, OUT_DVE)
            sc.wait_ge(dS.h, dS.n)
            sc.dma_start(out=y_d.rearrange("(it p) e -> it p e", p=128)[it],
                         in_=outsb[:]).then_inc(dS.h, 16)
            dS.inc(16)
        ysp[it] = dS.n

    @Sc
    def _(sc):
        sc.wait_ge(dS.h, dS.n)
    fn.close()
    fs.close()

    es.close()
    return nc


_NC_CACHE = {}


def kernel(L0_attn_x, L0_attn_wq, L0_attn_bq, L0_attn_wk, L0_attn_bk,
           L0_attn_wv, L0_attn_bv, L0_attn_wo, L0_attn_bo,
           L0_ffn_w1, L0_ffn_conv_w, L0_ffn_conv_b, L0_ffn_w2):
    if "nc" not in _NC_CACHE:
        _NC_CACHE["nc"] = build()
    nc = _NC_CACHE["nc"]

    def c(a):
        return np.ascontiguousarray(np.asarray(a, dtype=np.float32))

    base = {
        "wqT": c(np.asarray(L0_attn_wq).T), "wkT": c(np.asarray(L0_attn_wk).T),
        "wvT": c(np.asarray(L0_attn_wv).T), "woT": c(np.asarray(L0_attn_wo).T),
        "bqP": c(np.asarray(L0_attn_bq).reshape(EC, 128).T),
        "bkP": c(np.asarray(L0_attn_bk).reshape(EC, 128).T),
        "bv": c(L0_attn_bv), "bo": c(L0_attn_bo),
        "w1T": c(np.asarray(L0_ffn_w1).T),
        "cw": c(np.asarray(L0_ffn_conv_w).reshape(HC, 128, 3).transpose(1, 0, 2)),
        "cb": c(np.asarray(L0_ffn_conv_b).reshape(HC, 128).T),
        "w2T": c(np.asarray(L0_ffn_w2).T),
    }
    import jax

    @jax.jit
    def _pbf(dist):
        diff = dist.astype("float32")
        import jax.numpy as jnp
        return jnp.exp(-0.5 * (diff / 2.0) ** 2) + 1.0 * jnp.cos(2.0 * jnp.pi * diff / 401.0)

    def pb32(d_int):
        with jax.default_device(jax.devices("cpu")[0]):
            return np.asarray(jax.device_get(_pbf(np.asarray(d_int, dtype=np.int32))))

    ii = np.arange(128)
    base["pbB"] = c(pb32(np.arange(W)[None, :] - 32 - ii[:, None]))
    base["pbC"] = c(pb32(128 * np.arange(NIT)[None, :] + ii[:, None]))
    base["pbR"] = c(np.tile(pb32(-np.arange(T))[None, :], (16, 1)))

    xs = np.asarray(L0_attn_x, dtype=np.float32)
    in_maps = [dict(base, x=c(xs[b])) for b in range(8)]
    res = run_bass_kernel_spmd(nc, in_maps, core_ids=list(range(8)))
    out = np.stack([res.results[b]["y"] for b in range(8)], axis=0)
    return out.astype(np.float32)
